# version 14
# baseline (speedup 1.0000x reference)
import contextlib

import numpy as np
import ml_dtypes

import concourse.bass as bass
import concourse.mybir as mybir
from concourse.bass_utils import run_bass_kernel_spmd

F32 = mybir.dt.float32
BF16 = mybir.dt.bfloat16
AF = mybir.ActivationFunctionType
ALU = mybir.AluOpType
AX = mybir.AxisListType

S = 2048
D = 1024
NTT = 16
NCC = 8
NEG = -1.0e30
XFM_STRIDE = 2064

L0F_DEPTH = 6
L0F_HD = 64
L0F_WINDOW = 256
L0F_DI = 2730
L0F_QKSCALE = 8.0


def _host_constants():
    c = {}
    c["ident"] = np.eye(128, dtype=np.float32)
    c["ident64_2"] = np.concatenate([np.eye(64), np.eye(64)], 0).astype(
        np.float32)
    i = np.arange(128)[:, None]
    j = np.arange(512)[None, :]
    c["mask_even"] = np.where((j >= i) & (j <= i + 256), 0.0, NEG).astype(np.float32)
    c["mask_odd"] = np.where((j >= i + 128) & (j <= i + 384), 0.0, NEG).astype(
        np.float32)
    j1 = np.arange(128)[None, :]
    c["mask_tri"] = np.where(j1 <= i, 0.0, NEG).astype(np.float32)
    half = np.arange(0, L0F_HD, 2, dtype=np.float64)
    inv_freq = 1.0 / 10000.0 ** (half / L0F_HD)
    t = np.arange(S, dtype=np.float64)
    freqs = t[:, None] * inv_freq[None, :]
    freqs = np.concatenate([freqs, freqs], -1)
    base_scale = (half + 0.4 * L0F_HD) / (1.4 * L0F_HD)
    power = (t - S // 2) / (L0F_WINDOW // 2)
    scale = base_scale[None, :] ** power[:, None]
    scale = np.concatenate([scale, scale], -1)
    cos, sin = np.cos(freqs), np.sin(freqs)
    eps = np.where(np.arange(L0F_HD) < 32, -1.0, 1.0)[None, :]
    c["rot_cs_q"] = (L0F_QKSCALE * cos * scale).astype(np.float32)
    c["rot_ss_q"] = (L0F_QKSCALE * eps * sin * scale).astype(np.float32)
    c["rot_cs_k"] = (cos / scale).astype(np.float32)
    c["rot_ss_k"] = (eps * sin / scale).astype(np.float32)
    return c


def _host_prep(inputs):
    p = {}
    p["x"] = np.ascontiguousarray(inputs["L0_attn_x"])
    p["a0_swT"] = np.ascontiguousarray(
        np.transpose(inputs["L0_attn_spline_weight"], (2, 1, 0)))
    p["a0_scalerT"] = np.ascontiguousarray(inputs["L0_attn_spline_scaler"].T)
    p["a0_bwT"] = np.ascontiguousarray(inputs["L0_attn_base_weight"].T)
    p["a0_grid"] = np.ascontiguousarray(inputs["L0_attn_grid"])
    rt = inputs["L0_attn_rotation_matrix"].T
    p["a0_RT2"] = np.ascontiguousarray(np.concatenate([rt, rt], 0))
    p["a0_owT"] = np.ascontiguousarray(inputs["L0_attn_out_w"].T)
    p["a0_out_b"] = np.ascontiguousarray(inputs["L0_attn_out_b"][None, :])
    p["f0_Wqkv"] = np.ascontiguousarray(inputs["L0_ffn_Wqkv"])
    p["f0_Wo"] = np.ascontiguousarray(inputs["L0_ffn_Wo"])
    p["f0_W1"] = np.ascontiguousarray(inputs["L0_ffn_W1"])
    p["f0_W2"] = np.ascontiguousarray(inputs["L0_ffn_W2"])
    qs = inputs["L0_ffn_q_scale"]
    ks = inputs["L0_ffn_k_scale"]
    perm = (np.arange(L0F_HD) + 32) % L0F_HD
    p["f0_qs"] = np.ascontiguousarray(qs[:, None, :])
    p["f0_qs_perm"] = np.ascontiguousarray(qs[:, perm][:, None, :])
    p["f0_ks"] = np.ascontiguousarray(ks[:, None, :])
    p["f0_ks_perm"] = np.ascontiguousarray(ks[:, perm][:, None, :])
    p["f0_ln_g"] = np.ascontiguousarray(inputs["L0_ffn_ln_g"][:, :, None])
    p["f0_ln_b"] = np.ascontiguousarray(inputs["L0_ffn_ln_b"][:, :, None])
    p["f0_b1"] = np.ascontiguousarray(inputs["L0_ffn_b1"][:, None, :])
    p["f0_b2"] = np.ascontiguousarray(inputs["L0_ffn_b2"][:, None, :])
    p["a1_WKT"] = np.ascontiguousarray(inputs["L1_attn_W_K"].T)
    p["a1_W1T"] = np.ascontiguousarray(inputs["L1_attn_W1"].T)
    p["a1_W2T"] = np.ascontiguousarray(inputs["L1_attn_W2"].T)
    p["f1_Wg"] = np.ascontiguousarray(inputs["L1_ffn_Wg"])
    p["f1_A"] = np.ascontiguousarray(
        np.transpose(inputs["L1_ffn_A"], (1, 0, 2)).reshape(D, 128))
    p["f1_B"] = np.ascontiguousarray(inputs["L1_ffn_B"].reshape(128, D))
    p.update(_host_constants())
    return p


class Sem:
    def __init__(self, stack, nc, name):
        self.h = stack.enter_context(nc.semaphore(name))
        self.n = 0

    def inc(self, instr, amount=1):
        instr.then_inc(self.h, amount)
        self.n += amount
        return self.n


class Carver:

    def __init__(self, pool_ap, nbytes):
        self.pool = pool_ap
        self.nbytes = nbytes
        self.off = 0
        self.marks = []

    def take(self, nelem, dtype):
        esz = 4 if dtype == F32 else 2
        self.off = (self.off + 63) & ~63
        start = self.off
        self.off += nelem * esz
        assert self.off <= self.nbytes, f"pool overflow: {self.off}>{self.nbytes}"
        if dtype == BF16:
            return self.pool[:, start // 2: start // 2 + nelem]
        return self.pool.bitcast(F32)[:, start // 4: start // 4 + nelem]

    def push(self):
        self.marks.append(self.off)

    def pop(self):
        self.off = self.marks.pop()


class Prog:

    def __init__(self, nc, stack):
        self.nc = nc
        self.stack = stack
        es = stack.enter_context
        self.LD = Sem(stack, nc, "s_ld")
        self.GP = Sem(stack, nc, "s_gp")
        self.ST = Sem(stack, nc, "s_st")
        self.PE = Sem(stack, nc, "s_pe")
        self.DV = Sem(stack, nc, "s_dv")
        self.AC = Sem(stack, nc, "s_ac")
        self.PL = Sem(stack, nc, "s_pl")
        self.WT = [Sem(stack, nc, "s_wt0"), Sem(stack, nc, "s_wt1")]
        self.ST2 = [Sem(stack, nc, "s_sp0"), Sem(stack, nc, "s_sp1")]
        self.xres = es(nc.sbuf_tensor("xres", [128, NTT * D], F32))
        self.pool_bytes = 106 * 1024
        self.pool_t = es(nc.sbuf_tensor("pool", [128, self.pool_bytes // 2], BF16))
        self.wbuf = es(nc.sbuf_tensor("wbuf", [128, 4096], BF16))
        self.scr = es(nc.sbuf_tensor("scr", [128, 2560], F32))
        self.stat = es(nc.sbuf_tensor("stat", [128, 256], F32))
        self.identb = es(nc.sbuf_tensor("identb", [128, 128], BF16))
        self.identf = es(nc.sbuf_tensor("identf", [128, 128], F32))
        self.rt2b = es(nc.sbuf_tensor("rt2b", [128, 64], BF16))
        self.id642 = es(nc.sbuf_tensor("id642", [128, 64], BF16))
        self.brow = es(nc.sbuf_tensor("brow", [128, 2048], BF16))
        self.ones1 = es(nc.sbuf_tensor("ones1", [128, 512], BF16))
        self.masks = es(nc.sbuf_tensor("masks", [128, 512 * 2 + 128], BF16))
        self.rtab = None
        self.shift_scr = None
        self.psA = es(nc.psum_tensor("psA", [128, 3072], F32))
        self.psT = es(nc.psum_tensor("psT", [128, 2048], BF16))
        self.mu = self.stat[:, 0:NTT]
        self.rs = self.stat[:, NTT: 2 * NTT]
        self.tmp16 = self.stat[:, 32:48]
        self.eps = self.stat[:, 248:249]
        self.eps24 = self.stat[:, 249:250]

    def phase(self, name=None, **bodies):
        with self.nc.Block() as blk:
            for eng, fn in bodies.items():
                getattr(blk, eng)(fn)


def _dma(q, sem, out, in_, **kw):
    sem.inc(q.dma_start(out, in_, **kw), 16)




def emit_load_x(P, x_h):
    def sync(s):
        src = x_h.rearrange("(tt p) c -> p tt c", p=128)
        dst = P.xres[:].rearrange("p (tt c) -> p tt c", c=D)
        _dma(s, P.LD, dst, src)
        s.wait_ge(P.LD.h, P.LD.n)

    P.phase("load_x", sync=sync)


def emit_store_out(P, out_h):
    def sync(s):
        dst = out_h.rearrange("(tt p) c -> p tt c", p=128)
        src = P.xres[:].rearrange("p (tt c) -> p tt c", c=D)
        _dma(s, P.ST, dst, src)
        s.wait_ge(P.ST.h, P.ST.n)

    P.phase("store_out", sync=sync)


def emit_load_consts(P, h):
    def sync(s):
        _dma(s, P.LD, P.identf[:], h["ident"])
        s.wait_ge(P.LD.h, P.LD.n)

    def gpsimd(g):
        _dma(g, P.GP, P.identb[:], h["ident"])
        _dma(g, P.GP, P.id642[:], h["ident64_2"])
        _dma(g, P.GP, P.masks[:, 0:512], h["mask_even"])
        _dma(g, P.GP, P.masks[:, 512:1024], h["mask_odd"])
        _dma(g, P.GP, P.masks[:, 1024:1152], h["mask_tri"])
        g.wait_ge(P.GP.h, P.GP.n)
        g.memset(P.ones1[:], 1.0)
        g.memset(P.brow[96:97, 0:512], 0.0)
        g.drain()
        if P.shift_scr is not None:
            _dma(g, P.GP, P.shift_scr[0:1, :], P.brow[96:97, 0:512])
            g.wait_ge(P.GP.h, P.GP.n)

    def vector(v):
        v.memset(P.eps, 1e-5)
        v.memset(P.eps24, 1e-24)
        v.memset(P.stat[:, 96:100], 1.0)
        P.DV.inc(v.drain())

    def scalar(a):
        a.wait_ge(P.DV.h, P.DV.n)
        j = P.stat[:, 96:100]
        for fi, f in enumerate((AF.Exp, AF.Sqrt, AF.Gelu, AF.Silu, AF.Square,
                                AF.Sign)):
            a.activation(P.stat[:, 100 + fi:101 + fi], j[:, 0:1], f)
        P.AC.inc(a.drain())

    P.phase("load_consts", sync=sync, gpsimd=gpsimd, vector=vector, scalar=scalar)


def emit_load_rtab(P, h):

    def gpsimd(g):
        for i, k in enumerate(["rot_cs_q", "rot_ss_q", "rot_cs_k", "rot_ss_k"]):
            src = h[k].rearrange("(tt p) d -> p tt d", p=128)
            dst = P.rtab_b[:, i * NTT * 64:(i + 1) * NTT * 64].rearrange(
                "p (tt d) -> p tt d", d=64)
            _dma(g, P.GP, dst, src)
        g.wait_ge(P.GP.h, P.GP.n)

    P.phase("load_rtab", gpsimd=gpsimd)


def build_xfm(P, xfm, src_fn, nchunk=8, evac_scale=None):
    ncast = 2
    W = nchunk * 128
    cast = P.scr[:, 0: (2 * W) // 2].bitcast(BF16)

    dv0, pe0, ac0 = P.DV.n, P.PE.n, P.AC.n

    def vector(v):
        for cc in range(nchunk):
            v.memset(xfm[:, cc * XFM_STRIDE: cc * XFM_STRIDE + 1], 0.0)
            v.memset(xfm[:, cc * XFM_STRIDE + 1 + S:(cc + 1) * XFM_STRIDE], 0.0)
        for tt in range(NTT):
            slot = cast[:, (tt % ncast) * W:(tt % ncast + 1) * W]
            if tt >= ncast:
                v.wait_ge(P.PE.h, pe0 + (tt - ncast + 1))
            m = src_fn(v, tt, slot)
            P.DV.inc(m)

    def tensor(t):
        for tt in range(NTT):
            slot = cast[:, (tt % ncast) * W:(tt % ncast + 1) * W]
            t.wait_ge(P.DV.h, dv0 + tt + 1)
            if tt >= 1:
                t.wait_ge(P.AC.h, ac0 + tt)
            for cc in range(nchunk):
                m = t.transpose(P.psT[:, cc * 128:(cc + 1) * 128],
                                slot[:, cc * 128:(cc + 1) * 128], P.identb[:])
            P.PE.inc(m)

    def scalar(a):
        for tt in range(NTT):
            a.wait_ge(P.PE.h, pe0 + tt + 1)
            for cc in range(nchunk):
                dst = xfm[:, cc * XFM_STRIDE + 1 + tt * 128:
                          cc * XFM_STRIDE + 1 + (tt + 1) * 128]
                srcp = P.psT[:, cc * 128:(cc + 1) * 128]
                if evac_scale is None:
                    m = a.copy(dst, srcp)
                else:
                    m = a.activation(dst, srcp, AF.Copy,
                                     scale=evac_scale(cc))
            P.AC.inc(m)

    P.phase("xfm", vector=vector, tensor=tensor, scalar=scalar)


def emit_postln(P):
    mu, rs, t16, eps = P.mu, P.rs, P.tmp16, P.eps
    sq = P.scr[:, 0:2048]
    dv0, ac0 = P.DV.n, P.AC.n

    def scalar(a):
        for tt in range(NTT):
            if tt >= 2:
                a.wait_ge(P.DV.h, dv0 + tt - 1)
            m = a.activation(sq[:, (tt % 2) * 1024:(tt % 2 + 1) * 1024],
                             P.xres[:, tt * D:(tt + 1) * D], AF.Square)
            P.AC.inc(m)
        a.wait_ge(P.DV.h, dv0 + NTT + 1)
        a.activation(rs[:], rs[:], AF.Sqrt, bias=eps, scale=1.0)
        P.AC.inc(a.drain())

    def vector(v):
        for tt in range(NTT):
            xs = P.xres[:, tt * D:(tt + 1) * D]
            v.reduce_sum(mu[:, tt: tt + 1], xs, AX.X)
            v.wait_ge(P.AC.h, ac0 + tt + 1)
            m = v.reduce_sum(rs[:, tt: tt + 1],
                             sq[:, (tt % 2) * 1024:(tt % 2 + 1) * 1024], AX.X)
            P.DV.inc(m)
        v.drain()
        v.tensor_scalar_mul(mu[:], mu[:], 1.0 / D)
        v.tensor_scalar_mul(rs[:], rs[:], 1.0 / D)
        v.drain()
        v.tensor_mul(t16[:], mu[:], mu[:])
        v.drain()
        v.tensor_sub(rs[:], rs[:], t16[:])
        P.DV.inc(v.drain())
        v.wait_ge(P.AC.h, ac0 + NTT + 1)
        v.reciprocal(rs[:], rs[:])
        v.drain()
        for tt in range(NTT):
            xs = P.xres[:, tt * D:(tt + 1) * D]
            v.tensor_scalar(xs, xs, mu[:, tt: tt + 1], rs[:, tt: tt + 1],
                            ALU.subtract, ALU.mult)

    P.phase("postln", scalar=scalar, vector=vector)



A0_TG = 256
A0_NG = S // A0_TG


def emit_a0_prepass(P, h, carver, swsc):
    carver.push()
    stg = carver.take(2 * 3072, F32)
    sct = carver.take(3072, F32)
    ob = carver.take(2 * 3072, BF16)
    gp0 = P.GP.n
    wt0 = [P.WT[0].n, P.WT[1].n]
    st0 = [P.ST2[0].n, P.ST2[1].n]
    dv0, pl0 = P.DV.n, P.PL.n
    NK = 72

    def w_mult(eng, sem, k):
        if k >= 2:
            kk = k - 2
            eng.wait_ge(P.ST2[kk % 2].h, st0[kk % 2] + 16 * (kk // 2 + 1))
        eng.wait_ge(P.WT[k % 2].h, wt0[k % 2] + 16 * (k // 2 + 1))
        sl = slice((k % 2) * 3072, (k % 2) * 3072 + 3072)
        if k < 8:
            m = eng.tensor_copy(ob[:, sl], stg[:, sl])
        else:
            cc = (k - 8) // 8
            eng.wait_ge(P.GP.h, gp0 + 16 * (cc + 1))
            m = eng.tensor_tensor(ob[:, sl], stg[:, sl], sct[:], ALU.mult)
        sem.inc(m)

    def gp_all(g):
        for k in range(NK):
            if k >= 8 and (k - 8) % 8 == 0:
                cc = (k - 8) // 8
                if cc >= 1:
                    g.wait_ge(P.DV.h, dv0 + 4 * cc + 4)
                    g.wait_ge(P.PL.h, pl0 + 4 * cc + 4)
                _dma(g, P.GP, sct[:],
                     h["a0_scalerT"][cc * 128:(cc + 1) * 128, :])
            if k >= 2:
                kk = k - 2
                if kk % 2 == 0:
                    g.wait_ge(P.DV.h, dv0 + (kk // 2) + 1)
                else:
                    g.wait_ge(P.PL.h, pl0 + (kk // 2) + 1)
            if k < 8:
                src = h["a0_bwT"][k * 128:(k + 1) * 128, :]
            else:
                i = k - 8
                cc, s = i // 8, i % 8
                src = h["a0_swT"][s, cc * 128:(cc + 1) * 128, :]
            _dma(g, P.WT[k % 2],
                 stg[:, (k % 2) * 3072:(k % 2) * 3072 + 3072], src)
            if k % 2 == 1:
                w_mult(g, P.PL, k)

    def vector(v):
        for k in range(NK):
            if k % 2 == 0:
                w_mult(v, P.DV, k)

    def sync(sy):
        for k in range(NK):
            if k % 2 == 0:
                sy.wait_ge(P.DV.h, dv0 + (k // 2) + 1)
            else:
                sy.wait_ge(P.PL.h, pl0 + (k // 2) + 1)
            dst = swsc[k * 512:(k + 1) * 512, :].rearrange(
                "(f p) o -> p f o", p=128)
            src = ob[:, (k % 2) * 3072:(k % 2) * 3072 + 3072].rearrange(
                "p (f o) -> p f o", o=768)
            _dma(sy, P.ST2[k % 2], dst, src)
        sy.wait_ge(P.ST2[0].h, P.ST2[0].n)
        sy.wait_ge(P.ST2[1].h, P.ST2[1].n)

    P.phase("a0_prepass", gpsimd=gp_all, vector=vector, sync=sync)
    carver.pop()


KSTRIDE = 56


def emit_a0_knots(P, h, knots):
    gp0 = P.GP.n

    def gpsimd(g):
        for cc in range(8):
            _dma(g, P.GP, knots[:, cc * KSTRIDE: cc * KSTRIDE + 12],
                 h["a0_grid"][cc * 128:(cc + 1) * 128, :])
        _dma(g, P.GP, P.rt2b[:], h["a0_RT2"])
        _dma(g, P.GP, P.brow[0:1, 0:1024], h["a0_out_b"])
        g.wait_ge(P.GP.h, P.GP.n)

    def vector(v):
        v.wait_ge(P.GP.h, gp0 + 10 * 16)
        for cc in range(8):
            t = knots[:, cc * KSTRIDE: cc * KSTRIDE + 12]
            nt = knots[:, cc * KSTRIDE + 12: cc * KSTRIDE + 24]
            v.tensor_scalar_mul(nt, t, -1.0)
            v.drain()
            for p, off, n in ((1, 24, 11), (2, 35, 10), (3, 45, 9)):
                d = knots[:, cc * KSTRIDE + off: cc * KSTRIDE + off + n]
                v.tensor_sub(d, t[:, p:p + n], t[:, 0:n])
                v.drain()
                v.reciprocal(d, d)
                v.drain()
                if p == 1:
                    v.tensor_scalar_mul(d, d, 0.5)
                    v.drain()
        P.DV.inc(v.drain())

    P.phase("a0_knots", gpsimd=gpsimd, vector=vector)


def emit_a0_bs_group(P, h, g, knots, xg32, uw32, sgn32, bwork, bs, sx):
    pe0, dv0, ac0 = P.PE.n, P.DV.n, P.AC.n

    def tensor(t):
        for cc in range(8):
            if cc >= 2:
                t.wait_ge(P.DV.h, dv0 + (cc - 2) * 3 + 1)
            for kc in range(2):
                tt = 2 * g + kc
                m = t.transpose(
                    P.psA[:, 2048 + (cc % 2) * 512 + kc * 128:
                          2048 + (cc % 2) * 512 + kc * 128 + 128],
                    P.xres[:, tt * D + cc * 128: tt * D + (cc + 1) * 128],
                    P.identf[:])
            P.PE.inc(m)

    def vector(v):
        for cc in range(8):
            v.wait_ge(P.PE.h, pe0 + cc + 1)
            xs = xg32[:, (cc % 2) * 256:(cc % 2) * 256 + 256]
            if cc >= 2:
                v.wait_ge(P.AC.h, ac0 + 2 * (cc - 2) + 2)
            v.tensor_copy(xs[:, 0:128],
                          P.psA[:, 2048 + (cc % 2) * 512:
                                2048 + (cc % 2) * 512 + 128])
            m = v.tensor_copy(xs[:, 128:256],
                              P.psA[:, 2048 + (cc % 2) * 512 + 128:
                                    2048 + (cc % 2) * 512 + 256])
            P.DV.inc(m)
            v.wait_ge(P.AC.h, ac0 + 2 * cc + 1)
            kb = cc * KSTRIDE
            tcol = knots[:, kb: kb + 12]
            for j in range(11):
                m = v.tensor_sub(bwork[:, j * 256:(j + 1) * 256],
                                 sgn32[:, j * 256:(j + 1) * 256],
                                 sgn32[:, (j + 1) * 256:(j + 2) * 256])
            P.DV.inc(m)
            U = uw32[:, 0:256]
            W = uw32[:, 256:512]
            for p, off in ((1, 24), (2, 35), (3, 45)):
                for j in range(11 - p):
                    bj = bwork[:, j * 256:(j + 1) * 256]
                    bj1 = bwork[:, (j + 1) * 256:(j + 2) * 256]
                    r1 = knots[:, kb + off + j: kb + off + j + 1]
                    r2 = knots[:, kb + off + j + 1: kb + off + j + 2]
                    v.tensor_scalar(U, xs, tcol[:, j: j + 1], r1,
                                    ALU.subtract, ALU.mult)
                    v.tensor_scalar(W, xs, tcol[:, j + p + 1: j + p + 2], r2,
                                    ALU.subtract, ALU.mult)
                    v.tensor_mul(U, U, bj)
                    v.tensor_mul(W, W, bj1)
                    if p == 3:
                        m = v.tensor_sub(bs[:, (j * 8 + cc) * 256:
                                            (j * 8 + cc) * 256 + 256], U, W)
                    else:
                        m = v.tensor_sub(bj, U, W)
            P.DV.inc(m)

    def scalar(a):
        for cc in range(8):
            a.wait_ge(P.DV.h, dv0 + 3 * cc + 1)
            if cc >= 1:
                a.wait_ge(P.DV.h, dv0 + 3 * (cc - 1) + 2)
            xs = xg32[:, (cc % 2) * 256:(cc % 2) * 256 + 256]
            kb = cc * KSTRIDE
            for j in range(12):
                m = a.activation(sgn32[:, j * 256:(j + 1) * 256], xs, AF.Sign,
                                 bias=knots[:, kb + 12 + j: kb + 13 + j],
                                 scale=1.0)
            P.AC.inc(m)
            m = a.activation(sx[:, cc * 256:(cc + 1) * 256], xs, AF.Silu)
            P.AC.inc(m)
    P.phase(f"a0_bs_{g}", tensor=tensor, vector=vector, scalar=scalar)


def emit_a0_mm_group(P, h, g, bs, sx, wtl, stg6, swsc, qkv_scr):
    NL = 72
    NB = NL // 8
    wt0 = [P.WT[0].n, P.WT[1].n]
    st0 = [P.ST2[0].n, P.ST2[1].n]
    pe0, ac0 = P.PE.n, P.AC.n
    sw4 = swsc.rearrange("(i f p) o -> p i f o", f=4, p=128)

    def gpsimd(gq):
        for mg in range(4):
            for bb in range(NB):
                b = mg * NB + bb
                if b >= 2:
                    gq.wait_ge(P.PE.h, pe0 + (b - 2) + 1)
                dst = wtl[:, (b % 2) * 6144:(b % 2) * 6144 + 6144].rearrange(
                    "p (c o) -> p c o", o=768)
                _dma(gq, P.WT[b % 2], dst, sw4[:, 8 * bb:8 * bb + 8, mg])

    def tensor(t):
        for mg in range(4):
            for bb in range(NB):
                b = mg * NB + bb
                t.wait_ge(P.WT[b % 2].h, wt0[b % 2] + 16 * (b // 2 + 1))
                if bb == 0 and mg >= 1:
                    t.wait_ge(P.AC.h, ac0 + mg * 6)
                for sl in range(8):
                    l = bb * 8 + sl
                    if l < 8:
                        rhs = sx[:, l * 256:(l + 1) * 256]
                    else:
                        i = l - 8
                        slot = (i % 8) * 8 + i // 8
                        rhs = bs[:, slot * 256:(slot + 1) * 256]
                    for m in range(6):
                        mm = t.matmul(
                            P.psA[:, m * 512: m * 512 + 256],
                            wtl[:, (b % 2) * 6144 + sl * 768 + m * 128:
                                (b % 2) * 6144 + sl * 768 + (m + 1) * 128],
                            rhs, start=(l == 0), stop=(l == NL - 1))
                P.PE.inc(mm)

    def scalar(a):
        for mg in range(4):
            a.wait_ge(P.PE.h, pe0 + (mg + 1) * NB)
            if mg >= 1:
                a.wait_ge(P.ST2[0].h, st0[0] + 16 * 3 * mg)
                a.wait_ge(P.ST2[1].h, st0[1] + 16 * 3 * mg)
            for m in range(6):
                mm = a.copy(stg6[:, m * 256:(m + 1) * 256],
                            P.psA[:, m * 512: m * 512 + 256])
                P.AC.inc(mm)

    def sync(sy):
        for mg in range(4):
            sy.wait_ge(P.AC.h, ac0 + (mg + 1) * 6)
            for m in range(6):
                k = mg * 6 + m
                _dma(sy, P.ST2[k % 2],
                     qkv_scr[mg * 6 + m][:, g * 256:(g + 1) * 256],
                     stg6[:, m * 256:(m + 1) * 256])
        sy.wait_ge(P.ST2[0].h, P.ST2[0].n)
        sy.wait_ge(P.ST2[1].h, P.ST2[1].n)

    P.phase(f"a0_mm_{g}", gpsimd=gpsimd, tensor=tensor, scalar=scalar,
            sync=sync)


def emit_a0_reload_half(P, h, h0, qkv_scr, rbuf, qf, kf, vf):
    gp0, pe0, ac0 = P.GP.n, P.PE.n, P.AC.n
    chunks = list(range(h0 * 3 // 2, h0 * 3 // 2 + 12))
    nrot = []
    pe_plan = []
    for ci, m in enumerate(chunks):
        for half in range(2):
            b = 2 * m + half
            pe_plan.append((ci, m, b))

    def gpsimd(g):
        for ci, m in enumerate(chunks):
            if ci >= 2:
                t_inc = 2 * (ci - 1)
                g.wait_ge(P.PE.h, pe0 + t_inc)
            _dma(g, P.WT[ci % 2], rbuf[:, (ci % 2) * 2048:(ci % 2) * 2048 + 2048],
                 qkv_scr[m])

    wt0 = [P.WT[0].n, P.WT[1].n]

    def tensor(t):
        k_pe = 0
        for ci, m in enumerate(chunks):
            t.wait_ge(P.WT[ci % 2].h, wt0[ci % 2] + 16 * (ci // 2 + 1))
            for half in range(2):
                b = 2 * m + half
                hd, role = b // 3, b % 3
                rh = 64 * (b % 2)
                rho = 64 * (hd % 2)
                hl = hd - h0
                src = rbuf[rh:rh + 64, (ci % 2) * 2048:(ci % 2) * 2048 + 2048]
                k_pe += 1
                if role == 2:
                    if k_pe >= 2:
                        t.wait_ge(P.AC.h, ac0 + k_pe - 1)
                    for kc in range(16):
                        mm = t.transpose(
                            P.psT[:, kc * 64: kc * 64 + 64],
                            src[:, kc * 128:(kc + 1) * 128],
                            P.id642[rh:rh + 64, :], tile_position=(rh, 0))
                    P.PE.inc(mm)
                else:
                    if k_pe >= 2:
                        t.wait_ge(P.AC.h, ac0 + k_pe - 1)
                    for kb in range(4):
                        mm = t.matmul(
                            P.psA[rho:rho + 64, kb * 512:(kb + 1) * 512],
                            P.rt2b[rh:rh + 64, :],
                            src[:, kb * 512:(kb + 1) * 512],
                            start=True, stop=True, tile_position=(rh, rho))
                    P.PE.inc(mm)

    def scalar(a):
        k_pe = 0
        for ci, m in enumerate(chunks):
            for half in range(2):
                b = 2 * m + half
                hd, role = b // 3, b % 3
                rh = 64 * (b % 2)
                rho = 64 * (hd % 2)
                hl = hd - h0
                pr = hl // 2
                k_pe += 1
                a.wait_ge(P.PE.h, pe0 + k_pe)
                if role == 2:
                    for kc in range(16):
                        mm = a.copy(vf[:, kc * 512 + hl * 64:
                                       kc * 512 + hl * 64 + 64],
                                    P.psT[:, kc * 64: kc * 64 + 64])
                else:
                    arena = qf if role == 0 else kf
                    for kb in range(4):
                        dst = arena[rho:rho + 64, pr * 2048 + kb * 512:
                                    pr * 2048 + (kb + 1) * 512]
                        srcp = P.psA[rho:rho + 64, kb * 512:(kb + 1) * 512]
                        if role == 0:
                            mm = a.activation(dst, srcp, AF.Copy, scale=0.125)
                        else:
                            mm = a.copy(dst, srcp)
                P.AC.inc(mm)

    P.phase(f"a0_reload_{h0}", gpsimd=gpsimd, tensor=tensor, scalar=scalar)


def emit_a0_attn_half(P, h, h0, qf, kf, vf, carver):
    carver.push()
    probs = carver.take(2 * 2048, BF16)
    attnT = carver.take(2048, BF16)
    ctxq = carver.take(2 * 512, BF16)
    nm1 = [P.stat[:, 48:49], P.stat[:, 49:50]]
    zsum = [P.stat[:, 50:51], P.stat[:, 51:52]]
    rcW = P.stat[:, 64:128]
    NSH = 8 * NTT
    CB = 2048
    WB = 2560
    gp0 = P.GP.n
    pe0, dv0, ac0 = P.PE.n, P.DV.n, P.AC.n

    def gp(g):
        src = h["a0_owT"][h0 * 64: h0 * 64 + 512, :].rearrange(
            "(cc p) o -> p cc o", p=128)
        dst = P.wbuf[:, 0:4096].rearrange("p (cc o) -> p cc o", o=1024)
        _dma(g, P.GP, dst, src)

    order = {"pe": [], "dv": [], "ac": []}
    for i in range(NSH + 7):
        if 0 <= i - 3 < NSH:
            order["pe"].append(("B", i - 3))
        if 0 <= i - 4 < NSH:
            order["pe"].append(("C", i - 4))
        if 0 <= i - 6 < NSH and (i - 6) % 8 == 7:
            order["pe"].append(("D1", (i - 6) // 8))
        if i < NSH:
            order["pe"].append(("A", i))
        if 0 <= i - 5 < NSH and (i - 5) % 8 == 7:
            order["pe"].append(("D0", (i - 5) // 8))
        if 0 <= i - 1 < NSH:
            order["dv"].append(("minn", i - 1))
        if 0 <= i - 2 < NSH:
            order["dv"].append(("recip", i - 2))
            order["dv"].append(("mult", i - 2))
        if 0 <= i - 5 < NSH and (i - 5) % 8 == 7:
            order["dv"].append(("eo0", (i - 5) // 8))
        if 0 <= i - 6 < NSH and (i - 6) % 8 == 7:
            order["dv"].append(("eo1", (i - 6) // 8))
        if 0 <= i - 1 < NSH:
            order["ac"].append(("exp", i - 1))
        if 0 <= i - 3 < NSH:
            order["ac"].append(("aev", i - 3))
        if 0 <= i - 4 < NSH:
            order["ac"].append(("cev", i - 4))
    pe_i = {k: n + 1 for n, k in enumerate(order["pe"])}
    dv_i = {k: n + 1 for n, k in enumerate(order["dv"])}
    ac_i = {k: n + 1 for n, k in enumerate(order["ac"])}

    def tensor(t):
        for tag, s in order["pe"]:
            if tag == "A":
                qt, hl = s // 8, s % 8
                rh, pr = 64 * (hl % 2), hl // 2
                if s >= 1:
                    t.wait_ge(P.AC.h, ac0 + ac_i[("exp", s - 1)])
                for kb in range(4):
                    mm = t.matmul(
                        P.psA[:, kb * 512:(kb + 1) * 512],
                        qf[rh:rh + 64, pr * 2048 + qt * 128:
                           pr * 2048 + (qt + 1) * 128],
                        kf[rh:rh + 64, pr * 2048 + kb * 512:
                           pr * 2048 + (kb + 1) * 512],
                        start=True, stop=True, tile_position=(rh, 0))
                P.PE.inc(mm)
            elif tag == "B":
                t.wait_ge(P.DV.h, dv0 + dv_i[("mult", s)])
                if s >= 1:
                    t.wait_ge(P.AC.h, ac0 + ac_i[("aev", s - 1)])
                for kc in range(16):
                    mm = t.transpose(
                        P.psT[:, kc * 128:(kc + 1) * 128],
                        probs[:, (s % 2) * 2048 + kc * 128:
                              (s % 2) * 2048 + (kc + 1) * 128],
                        P.identb[:])
                P.PE.inc(mm)
            elif tag == "C":
                hl = s % 8
                rh, pr = 64 * (hl % 2), hl // 2
                t.wait_ge(P.AC.h, ac0 + ac_i[("aev", s)])
                if s >= 1:
                    t.wait_ge(P.AC.h, ac0 + ac_i[("cev", s - 1)])
                for kc in range(16):
                    mm = t.matmul(
                        P.psA[rh:rh + 64, CB + (pr % 2) * 128:
                              CB + (pr % 2) * 128 + 128],
                        vf[:, kc * 512 + hl * 64: kc * 512 + hl * 64 + 64],
                        attnT[:, kc * 128:(kc + 1) * 128],
                        start=(kc == 0), stop=(kc == 15),
                        tile_position=(0, rh))
                P.PE.inc(mm)
            elif tag == "D0":
                qt = s
                t.wait_ge(P.AC.h, ac0 + ac_i[("cev", 8 * qt + 7)])
                if qt == 0:
                    t.wait_ge(P.GP.h, gp0 + 16)
                else:
                    t.wait_ge(P.DV.h, dv0 + dv_i[("eo1", qt - 1)])
                ob = P.psA[:, WB:WB + 512]
                for cc in range(4):
                    mm = t.matmul(ob, ctxq[:, (qt % 2) * 512 + cc * 128:
                                           (qt % 2) * 512 + (cc + 1) * 128],
                                  P.wbuf[:, cc * 1024: cc * 1024 + 512],
                                  start=(cc == 0), stop=(cc == 3 and
                                                         not (h0 == 0)))
                if h0 == 0:
                    mm = t.matmul(ob, P.ones1[0:1, 0:128],
                                  P.brow[0:1, 0:512],
                                  start=False, stop=True)
                P.PE.inc(mm)
            else:
                qt = s
                t.wait_ge(P.DV.h, dv0 + dv_i[("eo0", qt)])
                ob = P.psA[:, WB:WB + 512]
                for cc in range(4):
                    mm = t.matmul(ob, ctxq[:, (qt % 2) * 512 + cc * 128:
                                           (qt % 2) * 512 + (cc + 1) * 128],
                                  P.wbuf[:, cc * 1024 + 512:
                                         cc * 1024 + 1024],
                                  start=(cc == 0), stop=(cc == 3 and
                                                         not (h0 == 0)))
                if h0 == 0:
                    mm = t.matmul(ob, P.ones1[0:1, 0:128],
                                  P.brow[0:1, 512:1024],
                                  start=False, stop=True)
                P.PE.inc(mm)

    def vector(v):
        for tag, s in order["dv"]:
            if tag == "minn":
                v.wait_ge(P.PE.h, pe0 + pe_i[("A", s)])
                if s >= 2:
                    v.wait_ge(P.AC.h, ac0 + ac_i[("exp", s - 2)])
                m = v.reduce_max(nm1[s % 2], P.psA[:, 0:2048].rearrange(
                    "p (g n) -> p g n", g=1), AX.X, negate=True)
                P.DV.inc(m)
            elif tag == "recip":
                v.wait_ge(P.AC.h, ac0 + ac_i[("exp", s)])
                m = v.reciprocal(rcW, zsum[s % 2].broadcast_to([128, 64]))
                P.DV.inc(m)
            elif tag == "mult":
                m = v.tensor_scalar_mul(
                    probs[:, (s % 2) * 2048:(s % 2) * 2048 + 2048],
                    probs[:, (s % 2) * 2048:(s % 2) * 2048 + 2048],
                    rcW[:, 0:1])
                P.DV.inc(m)
            else:
                qt, oc = s, 0 if tag == "eo0" else 1
                v.wait_ge(P.PE.h, pe0 + pe_i[("D" + str(oc), qt)])
                xs = P.xres[:, qt * D + oc * 512: qt * D + (oc + 1) * 512]
                m = v.tensor_add(xs, xs, P.psA[:, WB:WB + 512])
                P.DV.inc(m)

    def scalar(a):
        for tag, s in order["ac"]:
            if tag == "exp":
                a.wait_ge(P.DV.h, dv0 + dv_i[("minn", s)])
                if s >= 2:
                    a.wait_ge(P.PE.h, pe0 + pe_i[("B", s - 2)])
                m = a.activation(
                    probs[:, (s % 2) * 2048:(s % 2) * 2048 + 2048],
                    P.psA[:, 0:2048], AF.Exp,
                    bias=nm1[s % 2], scale=1.0, accum_out=zsum[s % 2])
                P.AC.inc(m)
            elif tag == "aev":
                a.wait_ge(P.PE.h, pe0 + pe_i[("B", s)])
                if s >= 1:
                    a.wait_ge(P.PE.h, pe0 + pe_i[("C", s - 1)])
                m = a.copy(attnT[:], P.psT[:])
                P.AC.inc(m)
            else:
                qt, hl = s // 8, s % 8
                rh, pr = 64 * (hl % 2), hl // 2
                a.wait_ge(P.PE.h, pe0 + pe_i[("C", s)])
                if hl == 0 and qt >= 2:
                    a.wait_ge(P.PE.h, pe0 + pe_i[("D1", qt - 2)])
                m = a.copy(ctxq[rh:rh + 64, (qt % 2) * 512 + pr * 128:
                                (qt % 2) * 512 + pr * 128 + 128],
                           P.psA[rh:rh + 64, CB + (pr % 2) * 128:
                                 CB + (pr % 2) * 128 + 128])
                P.AC.inc(m)

    P.phase(f"a0_attn_{h0}", gpsimd=gp, tensor=tensor, vector=vector,
            scalar=scalar)
    carver.pop()


def emit_l0_attn(P, carver, h):
    swsc = P.nc.dram_tensor("swsc", [72 * 4 * 128, 768], BF16).ap()
    qkv_scr = [P.nc.dram_tensor(f"qkvscr{m}", [128, 2048], BF16).ap()
               for m in range(24)]
    emit_a0_prepass(P, h, carver, swsc)
    carver.push()
    knots = carver.take(8 * KSTRIDE, F32)
    emit_a0_knots(P, h, knots)
    carver.push()
    bs = carver.take(64 * 256, BF16)
    sx = carver.take(8 * 256, BF16)
    xg32 = carver.take(2 * 256, F32)
    uw32 = carver.take(2 * 256, F32)
    sgn32 = carver.take(12 * 256, F32)
    bwork = carver.take(11 * 256, F32)
    wtl = carver.take(2 * 8 * 768, BF16)
    stg6 = carver.take(6 * 256, BF16)
    for g in range(A0_NG):
        emit_a0_bs_group(P, h, g, knots, xg32, uw32, sgn32, bwork, bs, sx)
        emit_a0_mm_group(P, h, g, bs, sx, wtl, stg6, swsc, qkv_scr)
    carver.pop()
    for h0 in (0, 8):
        carver.push()
        qf = carver.take(4 * 2048, BF16)
        kf = carver.take(4 * 2048, BF16)
        vf = carver.take(NTT * 512, BF16)
        rbuf = carver.take(2 * 2048, BF16)
        emit_a0_reload_half(P, h, h0, qkv_scr, rbuf, qf, kf, vf)
        emit_a0_attn_half(P, h, h0, qf, kf, vf, carver)
        carver.pop()
    carver.pop()
    emit_postln(P)




def emit_f0_shift(P, xsb, src_cols):
    gp0 = P.GP.n

    def gpsimd(g):
        srcs = P.xres[:, :].rearrange("p (tt c) -> p tt c", c=D)[
            :, :, src_cols:src_cols + 512]
        dsts = P.shift_scr[1:S + 1, :].rearrange("(tt p) c -> p tt c", p=128)
        _dma(g, P.GP, dsts, srcs)
        g.wait_ge(P.GP.h, gp0 + 16)
        srcl = P.shift_scr[0:S, :].rearrange("(tt p) c -> p tt c", p=128)
        dstl = xsb[:, :].rearrange("p (tt c) -> p tt c", c=512)
        _dma(g, P.GP, dstl, srcl)
        g.wait_ge(P.GP.h, gp0 + 32)

    P.phase("f0_shift", gpsimd=gpsimd)


def emit_f0_qkv_half(P, h, li, h0, xfm, qk_sc, qf, kf, vf, wsl, qb16):
    wq = h["f0_Wqkv"]
    qk32 = P.scr[:, 0:1024]
    sq32 = P.scr[:, 1024:1536]
    qn32 = P.scr[:, 1536:2048]
    rot32 = P.scr[:, 2048:2560]
    ss8 = P.stat[:, 48:56]
    rr8 = P.stat[:, 56:64]
    rrW = P.stat[:, 64:128]

    for c0, kind in ((h0 * 64, "q"), (1024 + h0 * 64, "k"),
                     (2048 + h0 * 64, "v")):
        gp0, pe0, dv0, ac0 = P.GP.n, P.PE.n, P.DV.n, P.AC.n

        def gp(g, c0=c0):
            src = wq[li, :, c0:c0 + 512].rearrange("(cc p) o -> p cc o", p=128)
            dst = wsl[:, 0:4096].rearrange("p (cc o) -> p cc o", o=512)
            _dma(g, P.GP, dst, src)

        inc_mm = lambda tt: tt + 1 if tt <= 2 else 2 * tt - 1
        inc_T = lambda j: 4 + 2 * j if j <= 12 else 17 + j
        inc_sqrt = lambda tt: tt + 1 if tt < 4 else 2 * tt - 2
        inc_evac = lambda j: 5 + 2 * j if j < 12 else 17 + j

        def emit_mm(t, tt, dv0, ac0, kind):
            bank = P.psA[:, (tt % 2) * 512:(tt % 2) * 512 + 512]
            if tt >= 2:
                if kind == "v":
                    t.wait_ge(P.AC.h, ac0 + (tt - 2) + 1)
                else:
                    t.wait_ge(P.DV.h, dv0 + (tt - 2) * 8 + 1)
            for cc in range(8):
                mm = t.matmul(bank,
                              xfm[:, cc * XFM_STRIDE + 1 + tt * 128:
                                  cc * XFM_STRIDE + 1 + (tt + 1) * 128],
                              wsl[:, cc * 512:(cc + 1) * 512],
                              start=(cc == 0), stop=(cc == 7))
            P.PE.inc(mm)

        def emit_T(t, j, dv0, ac0):
            t.wait_ge(P.DV.h, dv0 + j * 8 + 8)
            if j >= 1:
                t.wait_ge(P.AC.h, ac0 + inc_evac(j - 1))
            sb = qb16[:, (j % 2) * 512:(j % 2) * 512 + 512]
            for jj in range(8):
                mm = t.transpose(
                    P.psT[64 * (jj % 2):64 * (jj % 2) + 64,
                          (jj // 2) * 128:(jj // 2) * 128 + 128],
                    sb[:, jj * 64:(jj + 1) * 64], P.identb[:],
                    tile_position=(0, 64 * (jj % 2)))
            P.PE.inc(mm)

        def tensor(t, kind=kind, gp0=gp0, pe0=pe0, dv0=dv0, ac0=ac0):
            t.wait_ge(P.GP.h, gp0 + 16)
            if kind == "v":
                for tt in range(NTT):
                    emit_mm(t, tt, dv0, ac0, kind)
            else:
                for tt in range(3):
                    emit_mm(t, tt, dv0, ac0, kind)
                for j in range(13):
                    emit_T(t, j, dv0, ac0)
                    if j + 3 < NTT:
                        emit_mm(t, j + 3, dv0, ac0, kind)
                for j in range(13, NTT):
                    emit_T(t, j, dv0, ac0)

        def vector(v, kind=kind, pe0=pe0, ac0=ac0):
            if kind == "v":
                return
            soff = 0 if kind == "q" else 128
            toff = 0 if kind == "q" else 2 * NTT * 64
            for tt in range(NTT):
                v.wait_ge(P.PE.h, pe0 + (tt + 1 if tt <= 2 else 2 * tt - 1))
                bank = P.psA[:, (tt % 2) * 512:(tt % 2) * 512 + 512]
                x32 = qk32[:, (tt % 2) * 512:(tt % 2) * 512 + 512]
                m = v.tensor_copy(x32, bank)
                P.DV.inc(m)
                m = v.scalar_tensor_tensor(sq32[:], x32, 1.0, x32, ALU.mult,
                                           ALU.mult)
                P.DV.inc(m)
                m = v.reduce_sum(ss8, sq32.rearrange("p (g d) -> p g d", d=64),
                                 AX.X)
                P.DV.inc(m)
                v.wait_ge(P.AC.h, ac0 + (tt + 1 if tt < 4 else 2 * tt - 2))
                m = v.reciprocal(rrW.rearrange("p (g r) -> p g r", r=8),
                                 rr8.unsqueeze(2).broadcast_to([128, 8, 8]))
                P.DV.inc(m)
                rrb = rrW.rearrange("p (g r) -> p g r", r=8)[
                    :, :, 0:1].broadcast_to([128, 8, 64])
                m = v.tensor_tensor(qn32.rearrange("p (g d) -> p g d", d=64),
                                    x32.rearrange("p (g d) -> p g d", d=64),
                                    rrb, ALU.mult)
                P.DV.inc(m)
                cs_t = P.rtab_b[:, toff + tt * 64: toff + tt * 64 + 64]
                ss_t = P.rtab_b[:, NTT * 64 + toff + tt * 64:
                                NTT * 64 + toff + tt * 64 + 64]
                csb = cs_t.unsqueeze(1).broadcast_to([128, 8, 64])
                m = v.tensor_tensor(rot32.rearrange("p (g d) -> p g d", d=64),
                                    qn32.rearrange("p (g d) -> p g d", d=64),
                                    csb, ALU.mult)
                P.DV.inc(m)
                ssa = ss_t[:, 0:32].unsqueeze(1).broadcast_to([128, 8, 32])
                ssb = ss_t[:, 32:64].unsqueeze(1).broadcast_to([128, 8, 32])
                va = qn32.rearrange("p (g d) -> p g d", d=64)
                sw = sq32.rearrange("p (g d) -> p g d", d=64)
                v.tensor_tensor(sw[:, :, 0:32], va[:, :, 32:64], ssa, ALU.mult)
                m = v.tensor_tensor(sw[:, :, 32:64], va[:, :, 0:32], ssb,
                                    ALU.mult)
                P.DV.inc(m)
                if tt >= 2:
                    v.wait_ge(P.PE.h, pe0 + (4 + 2 * (tt - 2)
                                             if tt - 2 <= 12 else 15 + tt))
                dst = qb16[:, (tt % 2) * 512:(tt % 2) * 512 + 512]
                m = v.tensor_add(dst, rot32, sq32)
                P.DV.inc(m)

        def scalar(a, kind=kind, pe0=pe0, dv0=dv0):
            if kind == "v":
                for tt in range(NTT):
                    a.wait_ge(P.PE.h, pe0 + tt + 1)
                    bank = P.psA[:, (tt % 2) * 512:(tt % 2) * 512 + 512]
                    m = a.copy(vf[:, tt * 512:(tt + 1) * 512], bank)
                    P.AC.inc(m)
            else:
                arena = qf if kind == "q" else kf

                def emit_sqrt(a, tt):
                    a.wait_ge(P.DV.h, dv0 + tt * 8 + 3)
                    m = a.activation(rr8, ss8, AF.Sqrt, bias=P.eps24,
                                     scale=1.0)
                    P.AC.inc(m)

                def emit_ev(a, j):
                    a.wait_ge(P.PE.h, pe0 + (4 + 2 * j if j <= 12 else 17 + j))
                    for pr in range(4):
                        m = a.copy(arena[:, pr * 2048 + j * 128:
                                         pr * 2048 + (j + 1) * 128],
                                   P.psT[:, pr * 128:(pr + 1) * 128])
                    P.AC.inc(m)

                for tt in range(4):
                    emit_sqrt(a, tt)
                for j in range(12):
                    emit_ev(a, j)
                    emit_sqrt(a, j + 4)
                for j in range(12, NTT):
                    emit_ev(a, j)
        P.phase(f"f0_qkv_{kind}", gpsimd=gp, tensor=tensor, vector=vector,
                scalar=scalar)


def emit_f0_attn_half(P, h, li, h0, qf, kf, vf, carver):
    carver.push()
    probs = carver.take(2 * 2048, BF16)
    attnT = carver.take(2048, BF16)
    ctxq = carver.take(2 * 512, BF16)
    asum = P.stat[:, 48:52]
    nm4 = [P.stat[:, 52:56], P.stat[:, 56:60]]
    rc4 = [P.stat[:, 64:128], P.stat[:, 128:192]]
    NS = 2 * NTT
    CB = 2048
    WB = 2560

    def nkof(s):
        return 512 if s // 2 >= 2 else 256

    def ksof(s):
        return 256 * (s // 2 // 2 - 1) if s // 2 >= 2 else 0

    order = {"pe": [], "dv": [], "ac": []}
    for s in range(NS + 6):
        if s < NS:
            order["pe"].append(("A", s))
        if 0 <= s - 5 < NS and (s - 5) % 2 == 1:
            order["pe"].append(("D1", (s - 5) // 2))
        if 0 <= s - 2 < NS:
            order["pe"].append(("B", s - 2))
        if 0 <= s - 3 < NS:
            order["pe"].append(("C", s - 3))
        if 0 <= s - 4 < NS and (s - 4) % 2 == 1:
            order["pe"].append(("D0", (s - 4) // 2))
        if 0 <= s - 1 < NS:
            order["dv"].append(("recip", s - 1))
        if 0 <= s - 2 < NS:
            order["dv"].append(("mult", s - 2))
        if s < NS:
            order["dv"].append(("minnA", s))
            order["dv"].append(("minn", s))
        if 0 <= s - 4 < NS and (s - 4) % 2 == 1:
            order["dv"].append(("eo0", (s - 4) // 2))
        if 0 <= s - 5 < NS and (s - 5) % 2 == 1:
            order["dv"].append(("eo1", (s - 5) // 2))
        if 0 <= s - 1 < NS:
            order["ac"].append(("exp", s - 1))
        if 0 <= s - 2 < NS:
            order["ac"].append(("aev", s - 2))
        if 0 <= s - 3 < NS:
            order["ac"].append(("cev", s - 3))
    pe_i = {k: i + 1 for i, k in enumerate(order["pe"])}
    dv_i = {k: i + 1 for i, k in enumerate(order["dv"])}
    ac_i = {k: i + 1 for i, k in enumerate(order["ac"])}

    gp0 = P.GP.n
    pe0, dv0, ac0 = P.PE.n, P.DV.n, P.AC.n

    def gp(g):
        src = h["f0_Wo"][li, h0 * 64: h0 * 64 + 512, :].rearrange(
            "(cc p) o -> p cc o", p=128)
        dst = P.wbuf[:, 0:4096].rearrange("p (cc o) -> p cc o", o=1024)
        _dma(g, P.GP, dst, src)

    def tensor(t):
        for tag, s in order["pe"]:
            if tag == "A":
                nk, ks, qt = nkof(s), ksof(s), s // 2
                hg = s % 2
                if s >= 1:
                    t.wait_ge(P.DV.h, dv0 + dv_i[("minn", s - 1)])
                for jj in range(4):
                    j = hg * 4 + jj
                    rh, pr = 64 * (j % 2), j // 2
                    mm = t.matmul(
                        P.psA[:, jj * 512: jj * 512 + nk],
                        qf[rh:rh + 64, pr * 2048 + qt * 128:
                           pr * 2048 + (qt + 1) * 128],
                        kf[rh:rh + 64, pr * 2048 + ks: pr * 2048 + ks + nk],
                        start=True, stop=True, tile_position=(rh, 0))
                P.PE.inc(mm)
            elif tag == "B":
                nk = nkof(s)
                t.wait_ge(P.DV.h, dv0 + dv_i[("mult", s)])
                if s >= 1:
                    t.wait_ge(P.AC.h, ac0 + ac_i[("aev", s - 1)])
                for jj in range(4):
                    for kc in range(nk // 128):
                        mm = t.transpose(
                            P.psT[:, (jj * 4 + kc) * 128:
                                  (jj * 4 + kc) * 128 + 128],
                            probs[:, (s % 2) * 2048 + jj * 512 + kc * 128:
                                  (s % 2) * 2048 + jj * 512 + (kc + 1) * 128],
                            P.identb[:])
                P.PE.inc(mm)
            elif tag == "C":
                nk, ks, hg = nkof(s), ksof(s), s % 2
                t.wait_ge(P.AC.h, ac0 + ac_i[("aev", s)])
                if s >= 1:
                    t.wait_ge(P.AC.h, ac0 + ac_i[("cev", s - 1)])
                for jj in range(4):
                    j = hg * 4 + jj
                    rh, pr = 64 * (j % 2), j // 2
                    prl = pr - 2 * hg
                    for kc in range(nk // 128):
                        ktt = (ks // 128) + kc
                        mm = t.matmul(
                            P.psA[rh:rh + 64, CB + prl * 128:
                                  CB + prl * 128 + 128],
                            vf[:, ktt * 512 + j * 64: ktt * 512 + j * 64 + 64],
                            attnT[:, (jj * 4 + kc) * 128:
                                  (jj * 4 + kc) * 128 + 128],
                            start=(kc == 0), stop=(kc == nk // 128 - 1),
                            tile_position=(0, rh))
                P.PE.inc(mm)
            elif tag == "D0":
                qt = s
                t.wait_ge(P.AC.h, ac0 + ac_i[("cev", 2 * qt + 1)])
                if qt == 0:
                    t.wait_ge(P.GP.h, gp0 + 16)
                else:
                    t.wait_ge(P.DV.h, dv0 + dv_i[("eo1", qt - 1)])
                for cc in range(4):
                    mm = t.matmul(P.psA[:, WB:WB + 512],
                                  ctxq[:, (qt % 2) * 512 + cc * 128:
                                       (qt % 2) * 512 + (cc + 1) * 128],
                                  P.wbuf[:, cc * 1024: cc * 1024 + 512],
                                  start=(cc == 0), stop=(cc == 3))
                P.PE.inc(mm)
            else:
                qt = s
                t.wait_ge(P.DV.h, dv0 + dv_i[("eo0", qt)])
                for cc in range(4):
                    mm = t.matmul(P.psA[:, WB:WB + 512],
                                  ctxq[:, (qt % 2) * 512 + cc * 128:
                                       (qt % 2) * 512 + (cc + 1) * 128],
                                  P.wbuf[:, cc * 1024 + 512:
                                         cc * 1024 + 1024],
                                  start=(cc == 0), stop=(cc == 3))
                P.PE.inc(mm)

    def vector(v):
        for tag, s in order["dv"]:
            if tag == "recip":
                v.wait_ge(P.AC.h, ac0 + ac_i[("exp", s)])
                m = v.reciprocal(rc4[s % 2].rearrange("p (g r) -> p g r", r=16),
                                 asum.unsqueeze(2).broadcast_to([128, 4, 16]))
                P.DV.inc(m)
            elif tag == "mult":
                nk = nkof(s)
                for jj in range(4):
                    m = v.tensor_scalar_mul(
                        probs[:, (s % 2) * 2048 + jj * 512:
                              (s % 2) * 2048 + jj * 512 + nk],
                        probs[:, (s % 2) * 2048 + jj * 512:
                              (s % 2) * 2048 + jj * 512 + nk],
                        rc4[s % 2][:, jj * 16: jj * 16 + 1])
                P.DV.inc(m)
            elif tag == "minnA":
                nk, qt = nkof(s), s // 2
                v.wait_ge(P.PE.h, pe0 + pe_i[("A", s)])
                if s >= 2:
                    v.wait_ge(P.AC.h, ac0 + ac_i[("exp", s - 2)])
                mk = P.masks[:, 512 * (qt % 2): 512 * (qt % 2) + 512]
                mkb = mk[:, 512 - nk:512].unsqueeze(1).broadcast_to(
                    [128, 4, nk])
                sc = P.scr[:, 0:4 * nk]
                m = v.tensor_tensor(sc.rearrange("p (g n) -> p g n", g=4),
                                    P.psA[:, 0:2048].rearrange(
                                        "p (g n) -> p g n", g=4)[:, :, 0:nk],
                                    mkb, ALU.add)
                P.DV.inc(m)
            elif tag == "minn":
                nk = nkof(s)
                sc = P.scr[:, 0:4 * nk]
                m = v.reduce_max(nm4[s % 2],
                                 sc.rearrange("p (g n) -> p g n", g=4),
                                 AX.X, negate=True)
                P.DV.inc(m)
            else:
                qt, oc = s, 0 if tag == "eo0" else 1
                v.wait_ge(P.PE.h, pe0 + pe_i[("D" + str(oc), qt)])
                xs = P.xres[:, qt * D + oc * 512: qt * D + (oc + 1) * 512]
                m = v.tensor_add(xs, xs, P.psA[:, WB:WB + 512])
                P.DV.inc(m)

    def scalar(a):
        for tag, s in order["ac"]:
            if tag == "exp":
                nk = nkof(s)
                a.wait_ge(P.DV.h, dv0 + dv_i[("minn", s)])
                if s >= 2:
                    a.wait_ge(P.PE.h, pe0 + pe_i[("B", s - 2)])
                sc = P.scr[:, 0:4 * nk]
                for jj in range(4):
                    m = a.activation(
                        probs[:, (s % 2) * 2048 + jj * 512:
                              (s % 2) * 2048 + jj * 512 + nk],
                        sc[:, jj * nk:(jj + 1) * nk], AF.Exp,
                        bias=nm4[s % 2][:, jj: jj + 1], scale=1.0,
                        accum_out=asum[:, jj: jj + 1])
                P.AC.inc(m)
            elif tag == "aev":
                nk = nkof(s)
                a.wait_ge(P.PE.h, pe0 + pe_i[("B", s)])
                if s >= 1:
                    a.wait_ge(P.PE.h, pe0 + pe_i[("C", s - 1)])
                m = a.copy(
                    attnT[:, 0:2048].rearrange("p (j n) -> p j n", j=4)[
                        :, :, 0:nk],
                    P.psT[:, 0:2048].rearrange("p (j n) -> p j n", j=4)[
                        :, :, 0:nk])
                P.AC.inc(m)
            else:
                qt, hg = s // 2, s % 2
                a.wait_ge(P.PE.h, pe0 + pe_i[("C", s)])
                if qt >= 2 and hg == 0:
                    a.wait_ge(P.PE.h, pe0 + pe_i[("D1", qt - 2)])
                m = a.copy(ctxq[:, (qt % 2) * 512 + 2 * hg * 128:
                                (qt % 2) * 512 + (2 * hg + 2) * 128],
                           P.psA[:, CB: CB + 256])
                P.AC.inc(m)

    P.phase("f0_attn", gpsimd=gp, tensor=tensor, vector=vector, scalar=scalar)
    carver.pop()


def emit_f0_ffn(P, h, li, xfm, carver):
    nc = P.nc
    lgc = carver.take(8, F32)
    lnbb = carver.take(8, BF16)
    wsl = carver.take(2 * 4096, BF16)
    carver.push()
    xsb = carver.take(NTT * 512, BF16)
    emit_f0_shift(P, xsb, 512)

    mu, rs, t16 = P.mu, P.rs, P.tmp16
    sB = P.stat[:, 48:64]
    ssB = P.stat[:, 64:80]
    junk = P.scr[:, 0:512]
    dv0, ac0 = P.DV.n, P.AC.n

    def v_stats(v):
        jk = lambda i: P.scr[:, i * 512:(i + 1) * 512]
        for tt in range(NTT):
            xa = P.xres[:, tt * D: tt * D + 512]
            xb = xsb[:, tt * 512:(tt + 1) * 512]
            v.tensor_scalar(jk(0), xa, 1.0, 0.0, ALU.mult, ALU.add,
                            accum_out=mu[:, tt: tt + 1])
            v.scalar_tensor_tensor(jk(1), xa, 1.0, xa, ALU.mult, ALU.mult,
                                   accum_out=rs[:, tt: tt + 1])
            v.tensor_scalar(jk(2), xb, 1.0, 0.0, ALU.mult, ALU.add,
                            accum_out=sB[:, tt: tt + 1])
            v.scalar_tensor_tensor(jk(3), xb, 1.0, xb, ALU.mult, ALU.mult,
                                   accum_out=ssB[:, tt: tt + 1])
            P.DV.inc(v.drain())
            v.wait_ge(P.DV.h, P.DV.n)
        v.tensor_add(mu[:], mu[:], sB[:])
        v.tensor_add(rs[:], rs[:], ssB[:])
        v.drain()
        v.tensor_scalar_mul(mu[:], mu[:], 1.0 / D)
        v.tensor_scalar_mul(rs[:], rs[:], 1.0 / D)
        v.drain()
        v.tensor_mul(t16[:], mu[:], mu[:])
        v.drain()
        v.tensor_sub(rs[:], rs[:], t16[:])
        P.DV.inc(v.drain())
        v.wait_ge(P.AC.h, ac0 + 1)
        v.reciprocal(rs[:], rs[:])
        v.drain()

    def a_stats(a):
        a.wait_ge(P.DV.h, dv0 + NTT + 1)
        m = a.activation(rs[:], rs[:], AF.Sqrt, bias=P.eps, scale=1.0)
        P.AC.inc(m)

    P.phase("f0_stats", vector=v_stats, scalar=a_stats)

    def src_xn(v, tt, slot):
        v.tensor_scalar(slot[:, 0:512], P.xres[:, tt * D: tt * D + 512],
                        mu[:, tt: tt + 1], rs[:, tt: tt + 1],
                        ALU.subtract, ALU.mult)
        return v.tensor_scalar(slot[:, 512:1024], xsb[:, tt * 512:(tt + 1) * 512],
                               mu[:, tt: tt + 1], rs[:, tt: tt + 1],
                               ALU.subtract, ALU.mult)

    gpb = P.GP.n

    def gp_bias(g):
        _dma(g, P.GP, lgc[:, 0:8],
             h["f0_ln_g"][li].rearrange("(cc p) u -> p (cc u)", p=128),
             allow_slow_non_contiguous=True)
        _dma(g, P.GP, lnbb[:, 0:8],
             h["f0_ln_b"][li].rearrange("(cc p) u -> p (cc u)", p=128),
             allow_slow_non_contiguous=True)
        for sj in range(12):
            j, part = sj // 2, sj % 2
            w = 512 if j < 5 else L0F_DI - 512 * 5
            c0 = part * L0F_DI + 512 * j
            r, cs = 32 * (sj // 3), 512 * (sj % 3)
            _dma(g, P.GP, P.brow[r:r + 1, cs:cs + w],
                 h["f0_b1"][li][:, c0:c0 + w])
        _dma(g, P.GP, P.brow[32:33, 1536:2048], h["f0_b2"][li][:, 0:512])
        _dma(g, P.GP, P.brow[64:65, 1536:2048], h["f0_b2"][li][:, 512:1024])
        g.wait_ge(P.GP.h, P.GP.n)

    WJ = [512] * 5 + [L0F_DI - 2560]
    w1_ready = {}

    def w1_load(g, ttg, j):
        sem = P.ST2[j % 2]
        w = WJ[j]
        for part in range(2):
            c0 = part * L0F_DI + 512 * j
            src = h["f0_W1"][li][:, c0:c0 + w].rearrange(
                "(cc p) o -> p cc o", p=128)
            dst = wsl[:, part * 4096: part * 4096 + 8 * w].rearrange(
                "p (cc o) -> p cc o", o=w)
            _dma(g, sem, dst, src)
        w1_ready[(ttg, j)] = (sem, sem.n)

    def gp_bias2(g):
        gp_bias(g)
        w1_load(g, 0, 0)

    P.phase("f0_bias", gpsimd=gp_bias2)
    build_xfm(P, xfm, src_xn, evac_scale=lambda cc: lgc[:, cc: cc + 1])
    carver.pop()

    carver.push()
    agf = carver.take(22 * 1024, BF16)
    gel = carver.take(512, BF16)
    ag16 = carver.take(512, BF16)
    for ttg in range(2):
        for j in range(6):
            w = WJ[j]
            nblk = (w + 127) // 128
            gp0, pe0, dv0, ac0, pl0 = P.GP.n, P.PE.n, P.DV.n, P.AC.n, P.PL.n
            sj_a, sj_g = j * 2, j * 2 + 1

            def gp(g, j=j, ttg=ttg, pe0=pe0):
                if j < 5:
                    npe_beff = 2 if ttg == 0 else 0
                    g.wait_ge(P.PE.h, pe0 + npe_beff + 13)
                    w1_load(g, ttg, j + 1)

            def tensor(t, j=j, w=w, ttg=ttg, gp0=gp0, pe0=pe0, dv0=dv0,
                       ac0=ac0, pl0=pl0, sj_a=sj_a, sj_g=sj_g):
                sem_r, val_r = w1_ready[(ttg, j)]
                t.wait_ge(sem_r.h, val_r)
                npe = 0
                if ttg == 0:
                    for part, sj in ((0, sj_a), (1, sj_g)):
                        r = 32 * (sj // 3)
                        for cc in range(8):
                            mm = t.matmul(
                                P.psA[r:r + 1, 2048 + part * 512:
                                      2048 + part * 512 + w],
                                lnbb[:, cc: cc + 1],
                                wsl[:, part * 4096 + cc * w:
                                    part * 4096 + (cc + 1) * w],
                                start=(cc == 0), stop=(cc == 7),
                                tile_position=(0, r))
                        P.PE.inc(mm)
                        npe += 1
                ndv_beff = 2 if ttg == 0 else 0
                if ttg == 0:
                    t.wait_ge(P.DV.h, dv0 + 2)

                def emit_mm(tl):
                    tt = ttg * 8 + tl
                    ba = P.psA[:, (tl % 2) * 1024:(tl % 2) * 1024 + w]
                    bg = P.psA[:, (tl % 2) * 1024 + 512:
                               (tl % 2) * 1024 + 512 + w]
                    if tl >= 2:
                        t.wait_ge(P.DV.h, dv0 + ndv_beff + (tl - 2) + 1)
                        t.wait_ge(P.AC.h, ac0 + 2 * (tl - 2) + 1)
                    for part, bank, sj in ((0, ba, sj_a), (1, bg, sj_g)):
                        r, cs = 32 * (sj // 3), 512 * (sj % 3)
                        for cc in range(8):
                            t.matmul(bank,
                                     xfm[:, cc * XFM_STRIDE + 1 + tt * 128:
                                         cc * XFM_STRIDE + 1 + (tt + 1) * 128],
                                     wsl[:, part * 4096 + cc * w:
                                         part * 4096 + (cc + 1) * w],
                                     start=(cc == 0), stop=False)
                        mm = t.matmul(bank, P.ones1[r:r + 1, 0:128],
                                      P.brow[r:r + 1, cs:cs + w],
                                      start=False, stop=True,
                                      tile_position=(r, 0))
                    P.PE.inc(mm)

                def emit_T(j):
                    t.wait_ge(P.DV.h, dv0 + ndv_beff + j + 1)
                    if j >= 1:
                        t.wait_ge(P.AC.h, ac0 + 2 * j)
                    for b in range(nblk):
                        bw = min(128, w - b * 128)
                        mm = t.transpose(P.psT[0:bw, b * 128: b * 128 + 128],
                                         ag16[:, b * 128: b * 128 + bw],
                                         P.identb[:])
                    P.PE.inc(mm)

                emit_mm(0)
                emit_mm(1)
                for j in range(6):
                    emit_mm(j + 2)
                    emit_T(j)
                emit_T(6)
                emit_T(7)

            def vector(v, j=j, w=w, ttg=ttg, pe0=pe0, ac0=ac0, sj_a=sj_a,
                       sj_g=sj_g):
                npe_beff = 2 if ttg == 0 else 0
                if ttg == 0:
                    v.wait_ge(P.PE.h, pe0 + 2)
                    for part, sj in ((0, sj_a), (1, sj_g)):
                        r, cs = 32 * (sj // 3), 512 * (sj % 3)
                        m = v.tensor_add(P.brow[r:r + 1, cs:cs + w],
                                         P.brow[r:r + 1, cs:cs + w],
                                         P.psA[r:r + 1, 2048 + part * 512:
                                               2048 + part * 512 + w])
                        P.DV.inc(m)
                imm = lambda tl: tl + 1 if tl < 2 else 2 * tl - 1
                iT = lambda j: 2 * j + 4 if j <= 5 else j + 9
                for tl in range(8):
                    v.wait_ge(P.PE.h, pe0 + npe_beff + imm(tl))
                    v.wait_ge(P.AC.h, ac0 + 2 * tl + 1)
                    if tl >= 1:
                        v.wait_ge(P.PE.h, pe0 + npe_beff + iT(tl - 1))
                    ba = P.psA[:, (tl % 2) * 1024:(tl % 2) * 1024 + w]
                    m = v.tensor_mul(ag16[:, 0:w], ba, gel[:, 0:w])
                    P.DV.inc(m)

            def scalar(a, j=j, w=w, ttg=ttg, pe0=pe0, dv0=dv0, nblk=nblk):
                npe_beff = 2 if ttg == 0 else 0
                ndv_beff = 2 if ttg == 0 else 0
                imm = lambda tl: tl + 1 if tl < 2 else 2 * tl - 1
                iT = lambda jj: 2 * jj + 4 if jj <= 5 else jj + 9
                for tl in range(8):
                    a.wait_ge(P.PE.h, pe0 + npe_beff + imm(tl))
                    if tl >= 1:
                        a.wait_ge(P.DV.h, dv0 + ndv_beff + tl)
                    bg = P.psA[:, (tl % 2) * 1024 + 512:
                               (tl % 2) * 1024 + 512 + w]
                    m = a.activation(gel[:, 0:w], bg, AF.Gelu)
                    P.AC.inc(m)
                    a.wait_ge(P.PE.h, pe0 + npe_beff + iT(tl))
                    for b in range(nblk):
                        bw = min(128, w - b * 128)
                        m = a.copy(agf[0:bw, (4 * j + b) * 1024 + tl * 128:
                                       (4 * j + b) * 1024 + (tl + 1) * 128],
                                   P.psT[0:bw, b * 128: b * 128 + 128])
                    P.AC.inc(m)

            P.phase(f"f0_w1_{ttg}_{j}", gpsimd=gp,
                    tensor=tensor, vector=vector, scalar=scalar)

        CCW = [128] * 21 + [L0F_DI - 21 * 128]
        bcc = [[0, 1, 2, 3], [4, 5, 6, 7], [8, 9, 10, 11], [12, 13, 14, 15],
               [16, 17, 18, 19], [20], [21]]
        W2B = [(oc, ccs) for oc in range(2) for ccs in bcc]
        batchof = lambda cc: cc // 4 if cc < 20 else (5 if cc == 20 else 6)
        w2slot = lambda cc: ((batchof(cc) % 2) * 4 +
                             (cc - 4 * batchof(cc) if cc < 20 else 0))
        for sub in range(2):
            gp0, pe0, dv0 = P.GP.n, P.PE.n, P.DV.n

            wt0 = [P.WT[0].n, P.WT[1].n]

            def gp2(g, ttg=ttg, sub=sub, gp0=gp0, pe0=pe0):
                lastg = {}
                for bi, (oc, ccs) in enumerate(W2B):
                    grp = w2slot(ccs[0]) // 4
                    if grp in lastg:
                        poc, pccs = W2B[lastg[grp]]
                        g.wait_ge(P.PE.h, pe0 + poc * 23 + pccs[-1] + 1)
                    lastg[grp] = bi
                    c0 = ccs[0]
                    nb = len(ccs)
                    sl0 = w2slot(c0)
                    if CCW[ccs[-1]] == 128:
                        src = h["f0_W2"][li][c0 * 128:(c0 + nb) * 128,
                                             oc * 512:(oc + 1) * 512
                                             ].rearrange("(c p) o -> p c o",
                                                         p=128)
                        dst = wsl[:, sl0 * 512:(sl0 + nb) * 512].rearrange(
                            "p (c o) -> p c o", o=512)
                    else:
                        wcc = CCW[ccs[-1]]
                        src = h["f0_W2"][li][c0 * 128: c0 * 128 + wcc,
                                             oc * 512:(oc + 1) * 512]
                        dst = wsl[0:wcc, sl0 * 512: sl0 * 512 + 512]
                    _dma(g, P.WT[bi % 2], dst, src)
                if ttg == 0 and sub == 1:
                    g.wait_ge(P.PE.h, pe0 + 46)
                    w1_load(g, 1, 0)

            def tensor2(t, ttg=ttg, sub=sub, gp0=gp0, pe0=pe0, dv0=dv0):
                wtc = [0, 0]
                for bi, (oc, ccs) in enumerate(W2B):
                    wtc[bi % 2] += 16
                    for cc in ccs:
                        wcc = CCW[cc]
                        t.wait_ge(P.WT[bi % 2].h, wt0[bi % 2] + wtc[bi % 2])
                        for tq in range(4):
                            tl = sub * 4 + tq
                            bank = P.psA[:, tq * 512:(tq + 1) * 512]
                            if cc == 0 and oc == 1:
                                t.wait_ge(P.DV.h, dv0 + tq + 1)
                            mm = t.matmul(
                                bank,
                                agf[0:wcc, cc * 1024 + tl * 128:
                                    cc * 1024 + (tl + 1) * 128],
                                wsl[0:wcc, w2slot(cc) * 512:
                                    w2slot(cc) * 512 + 512],
                                start=(cc == 0), stop=False)
                        P.PE.inc(mm)
                    if ccs[-1] != 21:
                        continue
                    r = 32 * (oc + 1)
                    for tq in range(4):
                        bank = P.psA[:, tq * 512:(tq + 1) * 512]
                        mm = t.matmul(bank, P.ones1[r:r + 1, 0:128],
                                      P.brow[r:r + 1, 1536:2048],
                                      start=False, stop=True,
                                      tile_position=(r, 0))
                    P.PE.inc(mm)

            def vector2(v, ttg=ttg, sub=sub, pe0=pe0):
                for oc in range(2):
                    v.wait_ge(P.PE.h, pe0 + (oc + 1) * 23)
                    for tq in range(4):
                        tt = ttg * 8 + sub * 4 + tq
                        bank = P.psA[:, tq * 512:(tq + 1) * 512]
                        xs = P.xres[:, tt * D + oc * 512:
                                    tt * D + (oc + 1) * 512]
                        m = v.tensor_add(xs, xs, bank)
                        P.DV.inc(m)

            P.phase(f"f0_w2_{ttg}_{sub}", gpsimd=gp2, tensor=tensor2,
                    vector=vector2)
    carver.pop()


def emit_l0_ffn_layer(P, carver, h, li):
    carver.push()
    xfm = carver.take(8 * XFM_STRIDE, BF16)
    qk_sc = carver.take(4 * 64, F32)
    qk_scb = carver.take(4 * 64, BF16)

    carver.push()
    xsb = carver.take(NTT * 512, BF16)
    emit_f0_shift(P, xsb, 512)

    def src_xs(v, tt, slot):
        v.tensor_copy(slot[:, 0:512], P.xres[:, tt * D: tt * D + 512])
        return v.tensor_copy(slot[:, 512:1024], xsb[:, tt * 512:(tt + 1) * 512])

    build_xfm(P, xfm, src_xs)
    carver.pop()

    gp0, pe0 = P.GP.n, P.PE.n

    def gp_qs(g):
        _dma(g, P.GP, P.brow[0:1, 1536:1600], h["f0_qs"][li])
        _dma(g, P.GP, P.brow[0:1, 1600:1664], h["f0_qs_perm"][li])
        _dma(g, P.GP, P.brow[0:1, 1664:1728], h["f0_ks"][li])
        _dma(g, P.GP, P.brow[0:1, 1728:1792], h["f0_ks_perm"][li])
        g.wait_ge(P.GP.h, gp0 + 64)
        for i, k in enumerate(["rot_cs_q", "rot_ss_q", "rot_cs_k",
                               "rot_ss_k"]):
            src = h[k].rearrange("(tt p) d -> p tt d", p=128)
            dst = P.rtab_b[:, i * NTT * 64:(i + 1) * NTT * 64].rearrange(
                "p (tt d) -> p tt d", d=64)
            _dma(g, P.GP, dst, src)

    def t_qs(t):
        t.wait_ge(P.GP.h, gp0 + 64)
        for i in range(4):
            mm = t.matmul(P.psA[:, i * 512: i * 512 + 64],
                          P.ones1[0:1, 0:128],
                          P.brow[0:1, 1536 + i * 64: 1600 + i * 64],
                          start=True, stop=True)
            P.PE.inc(mm)

    def v_qs(v):
        v.wait_ge(P.PE.h, pe0 + 4)
        for i in range(4):
            v.tensor_copy(qk_sc[:, i * 64:(i + 1) * 64],
                          P.psA[:, i * 512: i * 512 + 64])
        P.DV.inc(v.drain())
        v.wait_ge(P.GP.h, gp0 + 8 * 16)
        v.tensor_copy(qk_scb[:], qk_sc[:])
        v.drain()
        for i in range(4):
            tab = P.rtab_b[:, i * NTT * 64:(i + 1) * NTT * 64].rearrange(
                "p (t d) -> p t d", d=64)
            scb = qk_scb[:, i * 64:(i + 1) * 64].unsqueeze(1).broadcast_to(
                [128, NTT, 64])
            m = v.tensor_tensor(tab, tab, scb, ALU.mult)
        P.DV.inc(m)

    P.phase("f0_qscales", gpsimd=gp_qs, tensor=t_qs, vector=v_qs)

    for h0 in (0, 8):
        carver.push()
        qf = carver.take(4 * 2048, BF16)
        kf = carver.take(4 * 2048, BF16)
        vf = carver.take(NTT * 512, BF16)
        carver.push()
        wsl = carver.take(4096, BF16)
        qb16 = carver.take(2 * 512, BF16)
        emit_f0_qkv_half(P, h, li, h0, xfm, qk_sc, qf, kf, vf, wsl, qb16)
        carver.pop()
        emit_f0_attn_half(P, h, li, h0, qf, kf, vf, carver)
        carver.pop()
    emit_f0_ffn(P, h, li, xfm, carver)
    carver.pop()


def emit_l0_ffn(P, carver, h):
    xsave = P.nc.dram_tensor("f0_xsave", [S, D], F32).ap()

    def save(s):
        dst = xsave.rearrange("(tt p) c -> p tt c", p=128)
        srcv = P.xres[:].rearrange("p (tt c) -> p tt c", c=D)
        _dma(s, P.ST, dst, srcv)
        s.wait_ge(P.ST.h, P.ST.n)

    P.phase("f0_xsave", sync=save)
    for li in range(L0F_DEPTH):
        emit_l0_ffn_layer(P, carver, h, li)

    def addback2(g):
        srcv = xsave.rearrange("(tt p) c -> p tt c", p=128)
        dst = P.xres[:].rearrange("p (tt c) -> p tt c", c=D)
        P.GP.inc(g.dma_start(dst, srcv, accum_op=ALU.add), 16)
        g.wait_ge(P.GP.h, P.GP.n)

    P.phase("f0_addback", gpsimd=addback2)
    emit_postln(P)





def emit_l1_attn(P, carver, h):
    carver.push()
    xfm = carver.take(8 * XFM_STRIDE, BF16)
    kfm = carver.take(8 * 2048, BF16)
    gh = carver.take(2048, BF16)

    def src_plain(v, tt, slot):
        return v.tensor_copy(slot, P.xres[:, tt * D:(tt + 1) * D])

    build_xfm(P, xfm, src_plain)

    gp0, pe0, ac0 = P.GP.n, P.PE.n, P.AC.n

    wt0 = [P.WT[0].n, P.WT[1].n]

    def gp_loadw(g):
        for m in range(8):
            src = h["a1_WKT"][:, m * 128:(m + 1) * 128].rearrange(
                "(cc p) o -> p cc o", p=128)
            dst = P.wbuf[:, (m % 2) * 3072:(m % 2) * 3072 + 1024].rearrange(
                "p (cc o) -> p cc o", o=128)
            if m >= 2:
                g.wait_ge(P.PE.h, pe0 + 4 * (m - 1))
            _dma(g, P.WT[m % 2], dst, src)

    def tensor(t):
        for m in range(8):
            t.wait_ge(P.WT[m % 2].h, wt0[m % 2] + 16 * (m // 2 + 1))
            wsl = P.wbuf[:, (m % 2) * 3072:(m % 2) * 3072 + 1024]
            for tg in range(4):
                k = m * 4 + tg
                bank = P.psA[:, (k % 2) * 512:(k % 2) * 512 + 512]
                if k >= 2:
                    t.wait_ge(P.AC.h, ac0 + k - 1)
                for cc in range(8):
                    mm = t.matmul(bank, wsl[:, cc * 128:(cc + 1) * 128],
                                  xfm[:, cc * XFM_STRIDE + 1 + tg * 512:
                                      cc * XFM_STRIDE + 1 + (tg + 1) * 512],
                                  start=(cc == 0), stop=(cc == 7))
                P.PE.inc(mm)

    def scalar(a):
        for m in range(8):
            for tg in range(4):
                k = m * 4 + tg
                a.wait_ge(P.PE.h, pe0 + k + 1)
                bank = P.psA[:, (k % 2) * 512:(k % 2) * 512 + 512]
                P.AC.inc(a.copy(
                    kfm[:, m * 2048 + tg * 512: m * 2048 + (tg + 1) * 512], bank))

    P.phase("l1_kfm", gpsimd=gp_loadw, tensor=tensor, scalar=scalar)

    gp1, pe1, ac1 = P.GP.n, P.PE.n, P.AC.n

    def gp2(g):
        src = h["a1_W1T"].rearrange("(cc p) d -> p cc d", p=128)
        dst = P.wbuf[:, 0:512].rearrange("p (cc d) -> p cc d", d=64)
        _dma(g, P.GP, dst, src)

    def tensor2(t):
        t.wait_ge(P.GP.h, gp1 + 16)
        for tg in range(4):
            bank = P.psA[:, 1024 + (tg % 2) * 512: 1024 + (tg % 2) * 512 + 512]
            if tg >= 2:
                t.wait_ge(P.AC.h, ac1 + tg - 1)
            for cc in range(8):
                mm = t.matmul(bank[0:64, :], P.wbuf[:, cc * 64:(cc + 1) * 64],
                              kfm[:, cc * 2048 + tg * 512:
                                  cc * 2048 + (tg + 1) * 512],
                              start=(cc == 0), stop=(cc == 7))
            P.PE.inc(mm)

    def scalar2(a):
        for tg in range(4):
            a.wait_ge(P.PE.h, pe1 + tg + 1)
            bank = P.psA[:, 1024 + (tg % 2) * 512: 1024 + (tg % 2) * 512 + 512]
            P.AC.inc(a.activation(gh[0:64, tg * 512:(tg + 1) * 512],
                                  bank[0:64, :], AF.Gelu))

    P.phase("l1_hidden", gpsimd=gp2, tensor=tensor2, scalar=scalar2)

    gp2_, pe2, dv2 = P.GP.n, P.PE.n, P.DV.n

    def gp3(g):
        _dma(g, P.GP, P.wbuf[0:64, 0:1024], h["a1_W2T"])

    def tensor3(t):
        t.wait_ge(P.GP.h, gp2_ + 16)
        for tt in range(NTT):
            for oc in range(2):
                k = tt * 2 + oc
                bank = P.psA[:, (k % 4) * 512:(k % 4) * 512 + 512]
                if k >= 4:
                    t.wait_ge(P.DV.h, dv2 + k - 3)
                mm = t.matmul(bank, gh[0:64, tt * 128:(tt + 1) * 128],
                              P.wbuf[0:64, oc * 512:(oc + 1) * 512],
                              start=True, stop=True)
                P.PE.inc(mm)

    def vector3(v):
        for tt in range(NTT):
            for oc in range(2):
                k = tt * 2 + oc
                v.wait_ge(P.PE.h, pe2 + k + 1)
                bank = P.psA[:, (k % 4) * 512:(k % 4) * 512 + 512]
                xs = P.xres[:, tt * D + oc * 512: tt * D + (oc + 1) * 512]
                P.DV.inc(v.tensor_add(xs, xs, bank))

    P.phase("l1_reads", gpsimd=gp3, tensor=tensor3, vector=vector3)

    carver.pop()
    emit_postln(P)




def emit_l1_ffn(P, carver, h):
    carver.push()
    xfm = carver.take(8 * XFM_STRIDE, BF16)
    hw_tm = carver.take(NTT * 128, BF16)
    hwfm = carver.take(2048, BF16)
    gsc = carver.take(32, F32)

    def src_plain(v, tt, slot):
        return v.tensor_copy(slot, P.xres[:, tt * D:(tt + 1) * D])

    build_xfm(P, xfm, src_plain)

    gp0, pe0, dv0, ac0 = P.GP.n, P.PE.n, P.DV.n, P.AC.n

    def gp(g):
        _dma(g, P.GP, P.wbuf[:, 0:64].rearrange("p (cc e) -> p cc e", e=8),
             h["f1_Wg"].rearrange("(cc p) e -> p cc e", p=128))
        _dma(g, P.GP, P.wbuf[:, 1024:2048].rearrange("p (cc o) -> p cc o", o=128),
             h["f1_A"].rearrange("(cc p) o -> p cc o", p=128))

    def tensor(t):
        t.wait_ge(P.GP.h, gp0 + 32)
        for tt in range(NTT):
            lbank = P.psA[:, (tt % 2) * 512:(tt % 2) * 512 + 8]
            hbank = P.psA[:, 1024 + (tt % 2) * 512: 1024 + (tt % 2) * 512 + 128]
            if tt >= 2:
                t.wait_ge(P.DV.h, dv0 + 2 * (tt - 2) + 2)
            for cc in range(8):
                xsl = xfm[:, cc * XFM_STRIDE + 1 + tt * 128:
                          cc * XFM_STRIDE + 1 + (tt + 1) * 128]
                t.matmul(lbank, xsl, P.wbuf[:, cc * 8:(cc + 1) * 8],
                         start=(cc == 0), stop=(cc == 7))
            for cc in range(8):
                xsl = xfm[:, cc * XFM_STRIDE + 1 + tt * 128:
                          cc * XFM_STRIDE + 1 + (tt + 1) * 128]
                mm = t.matmul(hbank, xsl,
                              P.wbuf[:, 1024 + cc * 128: 1024 + (cc + 1) * 128],
                              start=(cc == 0), stop=(cc == 7))
            P.PE.inc(mm)
        for tt in range(NTT):
            t.wait_ge(P.DV.h, dv0 + 2 * tt + 2)
            t.wait_ge(P.AC.h, ac0 + NTT + tt)
            m = t.transpose(P.psT[:, tt * 128:(tt + 1) * 128],
                            hw_tm[:, tt * 128:(tt + 1) * 128], P.identb[:])
            P.PE.inc(m)

    def vector(v):
        L = gsc[:, 0:8]
        m1 = gsc[:, 8:9]
        m2 = gsc[:, 9:10]
        wk = gsc[:, 10:18]
        E = gsc[:, 18:26]
        z = gsc[:, 26:27]
        nm1 = gsc[:, 27:28]
        cwt = gsc[:, 28:29]
        for tt in range(NTT):
            v.wait_ge(P.PE.h, pe0 + tt + 1)
            lbank = P.psA[:, (tt % 2) * 512:(tt % 2) * 512 + 8]
            hbank = P.psA[:, 1024 + (tt % 2) * 512: 1024 + (tt % 2) * 512 + 128]
            v.tensor_copy(L, lbank)
            v.drain()
            v.reduce_max(m1, L, AX.X)
            v.drain()
            v.tensor_single_scalar(wk, L, m1, ALU.is_ge)
            v.drain()
            v.scalar_tensor_tensor(wk, wk, NEG, L, ALU.mult, ALU.add)
            v.drain()
            v.reduce_max(m2, wk, AX.X)
            v.drain()
            v.tensor_scalar_mul(nm1, m1, -1.0)
            P.DV.inc(v.drain())
            v.wait_ge(P.AC.h, ac0 + tt + 1)
            v.tensor_single_scalar(wk, L, m2, ALU.is_ge)
            v.drain()
            v.tensor_mul(E, E, wk)
            v.drain()
            v.reduce_sum(z, E, AX.X)
            v.drain()
            v.reciprocal(z, z)
            v.drain()
            v.tensor_scalar_mul(E, E, z)
            v.drain()
            cwb = E.unsqueeze(2).broadcast_to([128, 8, 16])
            m = v.tensor_mul(
                hw_tm[:, tt * 128:(tt + 1) * 128].rearrange("p (e r) -> p e r", e=8),
                hbank.rearrange("p (e r) -> p e r", e=8), cwb)
            P.DV.inc(m)

    def scalar(a):
        L = gsc[:, 0:8]
        E = gsc[:, 18:26]
        nm1 = gsc[:, 27:28]
        for tt in range(NTT):
            a.wait_ge(P.DV.h, dv0 + 2 * tt + 1)
            a.activation(E, L, AF.Exp, bias=nm1, scale=1.0)
            P.AC.inc(a.drain())
        for tt in range(NTT):
            a.wait_ge(P.PE.h, pe0 + NTT + tt + 1)
            P.AC.inc(a.copy(hwfm[:, tt * 128:(tt + 1) * 128],
                            P.psT[:, tt * 128:(tt + 1) * 128]))

    P.phase("l1_gate", gpsimd=gp, tensor=tensor, vector=vector, scalar=scalar)

    gp1, pe1, dv1 = P.GP.n, P.PE.n, P.DV.n

    def gp2(g):
        _dma(g, P.GP, P.wbuf[:, 0:1024], h["f1_B"])

    def tensor2(t):
        t.wait_ge(P.GP.h, gp1 + 16)
        for tt in range(NTT):
            for oc in range(2):
                k = tt * 2 + oc
                bank = P.psA[:, (k % 4) * 512:(k % 4) * 512 + 512]
                if k >= 4:
                    t.wait_ge(P.DV.h, dv1 + k - 3)
                mm = t.matmul(bank, hwfm[:, tt * 128:(tt + 1) * 128],
                              P.wbuf[:, oc * 512:(oc + 1) * 512],
                              start=True, stop=True)
                P.PE.inc(mm)

    def vector2(v):
        for tt in range(NTT):
            for oc in range(2):
                k = tt * 2 + oc
                v.wait_ge(P.PE.h, pe1 + k + 1)
                bank = P.psA[:, (k % 4) * 512:(k % 4) * 512 + 512]
                xs = P.xres[:, tt * D + oc * 512: tt * D + (oc + 1) * 512]
                P.DV.inc(v.tensor_add(xs, xs, bank))

    P.phase("l1_out", gpsimd=gp2, tensor=tensor2, vector=vector2)

    carver.pop()
    emit_postln(P)


BLOCK_EMITTERS = {}


def _declare_hbm(nc, h, keys):
    hbm = {}
    for k in keys:
        a = h[k]
        hbm[k] = nc.dram_tensor(k, list(a.shape), F32, kind="ExternalInput").ap()
    return hbm


L1_KEYS = ["a1_WKT", "a1_W1T", "a1_W2T", "f1_Wg", "f1_A", "f1_B"]
L0F_KEYS = ["f0_Wqkv", "f0_Wo", "f0_W1", "f0_W2", "f0_qs", "f0_qs_perm",
            "f0_ks", "f0_ks_perm", "f0_ln_g", "f0_ln_b", "f0_b1", "f0_b2"]
A0_KEYS = ["a0_swT", "a0_scalerT", "a0_bwT", "a0_grid", "a0_RT2", "a0_owT",
           "a0_out_b"]
CONST_KEYS = ["ident", "ident64_2", "mask_even", "mask_odd", "mask_tri",
              "rot_cs_q", "rot_ss_q", "rot_cs_k", "rot_ss_k"]


def build_program(h, blocks):
    nc = bass.Bass("TRN2", target_bir_lowering=False, debug=False)
    keys = set(CONST_KEYS)
    if "l1_attn" in blocks:
        keys.update(["a1_WKT", "a1_W1T", "a1_W2T"])
    if "l1_ffn" in blocks:
        keys.update(["f1_Wg", "f1_A", "f1_B"])
    if any(b.startswith("l0_ffn") for b in blocks):
        keys.update(L0F_KEYS)
    if "l0_attn" in blocks:
        keys.update(A0_KEYS)
    hbm = _declare_hbm(nc, h, sorted(keys))
    hbm["x"] = nc.dram_tensor("x", [S, D], F32, kind="ExternalInput").ap()
    hbm["out"] = nc.dram_tensor("out", [S, D], F32, kind="ExternalOutput").ap()
    with contextlib.ExitStack() as stack:
        P = Prog(nc, stack)
        if any(b.startswith("l0_ffn") for b in blocks):
            P.shift_scr = nc.dram_tensor("shift_scr", [S + 1, 512], BF16).ap()
        carver = Carver(P.pool_t[:], P.pool_bytes)
        emit_load_consts(P, hbm)
        emit_load_x(P, hbm["x"])
        need_rtab = any(b.startswith("l0_ffn") for b in blocks)
        if need_rtab:
            P.rtab_b = carver.take(4 * NTT * 64, BF16)
            emit_load_rtab(P, hbm)
        for b in blocks:
            if b == "l0_attn":
                emit_l0_attn(P, carver, hbm)
            elif b == "l1_attn":
                emit_l1_attn(P, carver, hbm)
            elif b == "l1_ffn":
                emit_l1_ffn(P, carver, hbm)
            elif b == "l0_ffn":
                emit_l0_ffn(P, carver, hbm)
            elif b.startswith("l0_ffn_layer"):
                emit_l0_ffn_layer(P, carver, hbm, int(b.rsplit("_", 1)[1]))
            else:
                raise ValueError(b)
        emit_store_out(P, hbm["out"])
    return nc, sorted(keys)


def run_blocks(h, x_full, blocks):
    nc, keys = build_program(h, blocks)
    in_maps = []
    for c in range(8):
        m = {k: h[k] for k in keys}
        m["x"] = np.ascontiguousarray(x_full[c])
        in_maps.append(m)
    res = run_bass_kernel_spmd(nc, in_maps, core_ids=list(range(8)))
    return np.stack([r["out"] for r in res.results], axis=0)


def kernel(**inputs):
    h = _host_prep(inputs)
    out = run_blocks(h, h["x"], ["l0_attn", "l0_ffn", "l1_attn", "l1_ffn"])
    return np.ascontiguousarray(out.astype(np.float32))


# version 15
# speedup vs baseline: 1.1698x; 1.1698x over previous
import contextlib

import numpy as np
import ml_dtypes

import concourse.bass as bass
import concourse.mybir as mybir
from concourse.bass_utils import run_bass_kernel_spmd

F32 = mybir.dt.float32
BF16 = mybir.dt.bfloat16
AF = mybir.ActivationFunctionType
ALU = mybir.AluOpType
AX = mybir.AxisListType

S = 2048
D = 1024
NTT = 16
NCC = 8
NEG = -1.0e30
XFM_STRIDE = 2064

L0F_DEPTH = 6
L0F_HD = 64
L0F_WINDOW = 256
L0F_DI = 2730
L0F_QKSCALE = 8.0


def _host_constants():
    c = {}
    c["ident"] = np.eye(128, dtype=np.float32)
    c["ident64_2"] = np.concatenate([np.eye(64), np.eye(64)], 0).astype(
        np.float32)
    i = np.arange(128)[:, None]
    j = np.arange(512)[None, :]
    c["mask_even"] = np.where((j >= i) & (j <= i + 256), 0.0, NEG).astype(np.float32)
    c["mask_odd"] = np.where((j >= i + 128) & (j <= i + 384), 0.0, NEG).astype(
        np.float32)
    j1 = np.arange(128)[None, :]
    c["mask_tri"] = np.where(j1 <= i, 0.0, NEG).astype(np.float32)
    half = np.arange(0, L0F_HD, 2, dtype=np.float64)
    inv_freq = 1.0 / 10000.0 ** (half / L0F_HD)
    t = np.arange(S, dtype=np.float64)
    freqs = t[:, None] * inv_freq[None, :]
    freqs = np.concatenate([freqs, freqs], -1)
    base_scale = (half + 0.4 * L0F_HD) / (1.4 * L0F_HD)
    power = (t - S // 2) / (L0F_WINDOW // 2)
    scale = base_scale[None, :] ** power[:, None]
    scale = np.concatenate([scale, scale], -1)
    cos, sin = np.cos(freqs), np.sin(freqs)
    eps = np.where(np.arange(L0F_HD) < 32, -1.0, 1.0)[None, :]
    c["rot_cs_q"] = (L0F_QKSCALE * cos * scale).astype(np.float32)
    c["rot_ss_q"] = (L0F_QKSCALE * eps * sin * scale).astype(np.float32)
    c["rot_cs_k"] = (cos / scale).astype(np.float32)
    c["rot_ss_k"] = (eps * sin / scale).astype(np.float32)
    return c


def _host_prep(inputs):
    p = {}
    p["x"] = np.ascontiguousarray(inputs["L0_attn_x"])
    p["a0_swT"] = np.ascontiguousarray(
        np.transpose(inputs["L0_attn_spline_weight"], (2, 1, 0)))
    p["a0_scalerT"] = np.ascontiguousarray(inputs["L0_attn_spline_scaler"].T)
    p["a0_bwT"] = np.ascontiguousarray(inputs["L0_attn_base_weight"].T)
    p["a0_grid"] = np.ascontiguousarray(inputs["L0_attn_grid"])
    rt = inputs["L0_attn_rotation_matrix"].T
    p["a0_RT2"] = np.ascontiguousarray(np.concatenate([rt, rt], 0))
    p["a0_owT"] = np.ascontiguousarray(inputs["L0_attn_out_w"].T)
    p["a0_out_b"] = np.ascontiguousarray(inputs["L0_attn_out_b"][None, :])
    p["f0_Wqkv"] = np.ascontiguousarray(inputs["L0_ffn_Wqkv"])
    p["f0_Wo"] = np.ascontiguousarray(inputs["L0_ffn_Wo"])
    p["f0_W1"] = np.ascontiguousarray(inputs["L0_ffn_W1"])
    p["f0_W2"] = np.ascontiguousarray(inputs["L0_ffn_W2"])
    qs = inputs["L0_ffn_q_scale"]
    ks = inputs["L0_ffn_k_scale"]
    perm = (np.arange(L0F_HD) + 32) % L0F_HD
    p["f0_qs"] = np.ascontiguousarray(qs[:, None, :])
    p["f0_qs_perm"] = np.ascontiguousarray(qs[:, perm][:, None, :])
    p["f0_ks"] = np.ascontiguousarray(ks[:, None, :])
    p["f0_ks_perm"] = np.ascontiguousarray(ks[:, perm][:, None, :])
    p["f0_ln_g"] = np.ascontiguousarray(inputs["L0_ffn_ln_g"][:, :, None])
    p["f0_ln_b"] = np.ascontiguousarray(inputs["L0_ffn_ln_b"][:, :, None])
    p["f0_b1"] = np.ascontiguousarray(inputs["L0_ffn_b1"][:, None, :])
    p["f0_b2"] = np.ascontiguousarray(inputs["L0_ffn_b2"][:, None, :])
    p["a1_WKT"] = np.ascontiguousarray(inputs["L1_attn_W_K"].T)
    p["a1_W1T"] = np.ascontiguousarray(inputs["L1_attn_W1"].T)
    p["a1_W2T"] = np.ascontiguousarray(inputs["L1_attn_W2"].T)
    p["f1_Wg"] = np.ascontiguousarray(inputs["L1_ffn_Wg"])
    p["f1_A"] = np.ascontiguousarray(
        np.transpose(inputs["L1_ffn_A"], (1, 0, 2)).reshape(D, 128))
    p["f1_B"] = np.ascontiguousarray(inputs["L1_ffn_B"].reshape(128, D))
    p.update(_host_constants())
    return p


class Sem:
    def __init__(self, stack, nc, name):
        self.h = stack.enter_context(nc.semaphore(name))
        self.n = 0

    def inc(self, instr, amount=1):
        instr.then_inc(self.h, amount)
        self.n += amount
        return self.n


class Carver:

    def __init__(self, pool_ap, nbytes):
        self.pool = pool_ap
        self.nbytes = nbytes
        self.off = 0
        self.marks = []

    def take(self, nelem, dtype):
        esz = 4 if dtype == F32 else 2
        self.off = (self.off + 63) & ~63
        start = self.off
        self.off += nelem * esz
        assert self.off <= self.nbytes, f"pool overflow: {self.off}>{self.nbytes}"
        if dtype == BF16:
            return self.pool[:, start // 2: start // 2 + nelem]
        return self.pool.bitcast(F32)[:, start // 4: start // 4 + nelem]

    def push(self):
        self.marks.append(self.off)

    def pop(self):
        self.off = self.marks.pop()


class Prog:

    def __init__(self, nc, stack):
        self.nc = nc
        self.stack = stack
        es = stack.enter_context
        self.LD = Sem(stack, nc, "s_ld")
        self.GP = Sem(stack, nc, "s_gp")
        self.ST = Sem(stack, nc, "s_st")
        self.PE = Sem(stack, nc, "s_pe")
        self.DV = Sem(stack, nc, "s_dv")
        self.AC = Sem(stack, nc, "s_ac")
        self.PL = Sem(stack, nc, "s_pl")
        self.WT = [Sem(stack, nc, "s_wt0"), Sem(stack, nc, "s_wt1")]
        self.ST2 = [Sem(stack, nc, "s_sp0"), Sem(stack, nc, "s_sp1")]
        self.xres = es(nc.sbuf_tensor("xres", [128, NTT * D], F32))
        self.pool_bytes = 106 * 1024
        self.pool_t = es(nc.sbuf_tensor("pool", [128, self.pool_bytes // 2], BF16))
        self.wbuf = es(nc.sbuf_tensor("wbuf", [128, 4096], BF16))
        self.scr = es(nc.sbuf_tensor("scr", [128, 2560], F32))
        self.stat = es(nc.sbuf_tensor("stat", [128, 256], F32))
        self.identb = es(nc.sbuf_tensor("identb", [128, 128], BF16))
        self.identf = es(nc.sbuf_tensor("identf", [128, 128], F32))
        self.rt2b = es(nc.sbuf_tensor("rt2b", [128, 64], BF16))
        self.id642 = es(nc.sbuf_tensor("id642", [128, 64], BF16))
        self.brow = es(nc.sbuf_tensor("brow", [128, 2048], BF16))
        self.ones1 = es(nc.sbuf_tensor("ones1", [128, 512], BF16))
        self.masks = es(nc.sbuf_tensor("masks", [128, 512 * 2 + 128], BF16))
        self.rtab = None
        self.shift_scr = None
        self.psA = es(nc.psum_tensor("psA", [128, 3072], F32))
        self.psT = es(nc.psum_tensor("psT", [128, 2048], BF16))
        self.mu = self.stat[:, 0:NTT]
        self.rs = self.stat[:, NTT: 2 * NTT]
        self.tmp16 = self.stat[:, 32:48]
        self.eps = self.stat[:, 248:249]
        self.eps24 = self.stat[:, 249:250]

    def phase(self, name=None, **bodies):
        with self.nc.Block() as blk:
            for eng, fn in bodies.items():
                getattr(blk, eng)(fn)


def _dma(q, sem, out, in_, **kw):
    sem.inc(q.dma_start(out, in_, **kw), 16)




def emit_load_x(P, x_h):
    def sync(s):
        src = x_h.rearrange("(tt p) c -> p tt c", p=128)
        dst = P.xres[:].rearrange("p (tt c) -> p tt c", c=D)
        _dma(s, P.LD, dst, src)
        s.wait_ge(P.LD.h, P.LD.n)

    P.phase("load_x", sync=sync)


def emit_store_out(P, out_h):
    def sync(s):
        dst = out_h.rearrange("(tt p) c -> p tt c", p=128)
        src = P.xres[:].rearrange("p (tt c) -> p tt c", c=D)
        _dma(s, P.ST, dst, src)
        s.wait_ge(P.ST.h, P.ST.n)

    P.phase("store_out", sync=sync)


def emit_load_consts(P, h):
    def sync(s):
        _dma(s, P.LD, P.identf[:], h["ident"])
        s.wait_ge(P.LD.h, P.LD.n)

    def gpsimd(g):
        _dma(g, P.GP, P.identb[:], h["ident"])
        _dma(g, P.GP, P.id642[:], h["ident64_2"])
        _dma(g, P.GP, P.masks[:, 0:512], h["mask_even"])
        _dma(g, P.GP, P.masks[:, 512:1024], h["mask_odd"])
        _dma(g, P.GP, P.masks[:, 1024:1152], h["mask_tri"])
        g.wait_ge(P.GP.h, P.GP.n)
        g.memset(P.ones1[:], 1.0)
        g.memset(P.brow[96:97, 0:512], 0.0)
        g.drain()
        if P.shift_scr is not None:
            _dma(g, P.GP, P.shift_scr[0:1, :], P.brow[96:97, 0:512])
            g.wait_ge(P.GP.h, P.GP.n)

    def vector(v):
        v.memset(P.eps, 1e-5)
        v.memset(P.eps24, 1e-24)
        v.memset(P.stat[:, 96:100], 1.0)
        P.DV.inc(v.drain())

    def scalar(a):
        a.wait_ge(P.DV.h, P.DV.n)
        j = P.stat[:, 96:100]
        for fi, f in enumerate((AF.Exp, AF.Sqrt, AF.Gelu, AF.Silu, AF.Square,
                                AF.Sign)):
            a.activation(P.stat[:, 100 + fi:101 + fi], j[:, 0:1], f)
        P.AC.inc(a.drain())

    P.phase("load_consts", sync=sync, gpsimd=gpsimd, vector=vector, scalar=scalar)


def emit_load_rtab(P, h):

    def gpsimd(g):
        for i, k in enumerate(["rot_cs_q", "rot_ss_q", "rot_cs_k", "rot_ss_k"]):
            src = h[k].rearrange("(tt p) d -> p tt d", p=128)
            dst = P.rtab_b[:, i * NTT * 64:(i + 1) * NTT * 64].rearrange(
                "p (tt d) -> p tt d", d=64)
            _dma(g, P.GP, dst, src)
        g.wait_ge(P.GP.h, P.GP.n)

    P.phase("load_rtab", gpsimd=gpsimd)


def build_xfm(P, xfm, src_fn, nchunk=8, evac_scale=None):
    ncast = 2
    W = nchunk * 128
    cast = P.scr[:, 0: (2 * W) // 2].bitcast(BF16)

    dv0, pe0, ac0 = P.DV.n, P.PE.n, P.AC.n

    def vector(v):
        for cc in range(nchunk):
            v.memset(xfm[:, cc * XFM_STRIDE: cc * XFM_STRIDE + 1], 0.0)
            v.memset(xfm[:, cc * XFM_STRIDE + 1 + S:(cc + 1) * XFM_STRIDE], 0.0)
        for tt in range(NTT):
            slot = cast[:, (tt % ncast) * W:(tt % ncast + 1) * W]
            if tt >= ncast:
                v.wait_ge(P.PE.h, pe0 + (tt - ncast + 1))
            m = src_fn(v, tt, slot)
            P.DV.inc(m)

    def tensor(t):
        for tt in range(NTT):
            slot = cast[:, (tt % ncast) * W:(tt % ncast + 1) * W]
            t.wait_ge(P.DV.h, dv0 + tt + 1)
            if tt >= 1:
                t.wait_ge(P.AC.h, ac0 + tt)
            for cc in range(nchunk):
                m = t.transpose(P.psT[:, cc * 128:(cc + 1) * 128],
                                slot[:, cc * 128:(cc + 1) * 128], P.identb[:])
            P.PE.inc(m)

    def scalar(a):
        for tt in range(NTT):
            a.wait_ge(P.PE.h, pe0 + tt + 1)
            for cc in range(nchunk):
                dst = xfm[:, cc * XFM_STRIDE + 1 + tt * 128:
                          cc * XFM_STRIDE + 1 + (tt + 1) * 128]
                srcp = P.psT[:, cc * 128:(cc + 1) * 128]
                if evac_scale is None:
                    m = a.copy(dst, srcp)
                else:
                    m = a.activation(dst, srcp, AF.Copy,
                                     scale=evac_scale(cc))
            P.AC.inc(m)

    P.phase("xfm", vector=vector, tensor=tensor, scalar=scalar)


def emit_postln(P):
    mu, rs, t16, eps = P.mu, P.rs, P.tmp16, P.eps
    sq = P.scr[:, 0:2048]
    dv0, ac0 = P.DV.n, P.AC.n

    def scalar(a):
        for tt in range(NTT):
            if tt >= 2:
                a.wait_ge(P.DV.h, dv0 + tt - 1)
            m = a.activation(sq[:, (tt % 2) * 1024:(tt % 2 + 1) * 1024],
                             P.xres[:, tt * D:(tt + 1) * D], AF.Square)
            P.AC.inc(m)
        a.wait_ge(P.DV.h, dv0 + NTT + 1)
        a.activation(rs[:], rs[:], AF.Sqrt, bias=eps, scale=1.0)
        P.AC.inc(a.drain())

    def vector(v):
        for tt in range(NTT):
            xs = P.xres[:, tt * D:(tt + 1) * D]
            v.reduce_sum(mu[:, tt: tt + 1], xs, AX.X)
            v.wait_ge(P.AC.h, ac0 + tt + 1)
            m = v.reduce_sum(rs[:, tt: tt + 1],
                             sq[:, (tt % 2) * 1024:(tt % 2 + 1) * 1024], AX.X)
            P.DV.inc(m)
        v.drain()
        v.tensor_scalar_mul(mu[:], mu[:], 1.0 / D)
        v.tensor_scalar_mul(rs[:], rs[:], 1.0 / D)
        v.drain()
        v.tensor_mul(t16[:], mu[:], mu[:])
        v.drain()
        v.tensor_sub(rs[:], rs[:], t16[:])
        P.DV.inc(v.drain())
        v.wait_ge(P.AC.h, ac0 + NTT + 1)
        v.reciprocal(rs[:], rs[:])
        v.drain()
        for tt in range(NTT):
            xs = P.xres[:, tt * D:(tt + 1) * D]
            v.tensor_scalar(xs, xs, mu[:, tt: tt + 1], rs[:, tt: tt + 1],
                            ALU.subtract, ALU.mult)

    P.phase("postln", scalar=scalar, vector=vector)



A0_TG = 256
A0_NG = S // A0_TG


def emit_a0_prepass(P, h, carver, swsc):
    carver.push()
    stg = carver.take(2 * 3072, F32)
    sct = carver.take(3072, F32)
    ob = carver.take(2 * 3072, BF16)
    gp0 = P.GP.n
    wt0 = [P.WT[0].n, P.WT[1].n]
    st0 = [P.ST2[0].n, P.ST2[1].n]
    dv0, pl0 = P.DV.n, P.PL.n
    NK = 72

    def w_mult(eng, sem, k):
        if k >= 2:
            kk = k - 2
            eng.wait_ge(P.ST2[kk % 2].h, st0[kk % 2] + 16 * (kk // 2 + 1))
        eng.wait_ge(P.WT[k % 2].h, wt0[k % 2] + 16 * (k // 2 + 1))
        sl = slice((k % 2) * 3072, (k % 2) * 3072 + 3072)
        if k < 8:
            m = eng.tensor_copy(ob[:, sl], stg[:, sl])
        else:
            cc = (k - 8) // 8
            eng.wait_ge(P.GP.h, gp0 + 16 * (cc + 1))
            m = eng.tensor_tensor(ob[:, sl], stg[:, sl], sct[:], ALU.mult)
        sem.inc(m)

    def gp_all(g):
        for k in range(NK):
            if k >= 8 and (k - 8) % 8 == 0:
                cc = (k - 8) // 8
                if cc >= 1:
                    g.wait_ge(P.DV.h, dv0 + 4 * cc + 4)
                    g.wait_ge(P.PL.h, pl0 + 4 * cc + 4)
                _dma(g, P.GP, sct[:],
                     h["a0_scalerT"][cc * 128:(cc + 1) * 128, :])
            if k >= 2:
                kk = k - 2
                if kk % 2 == 0:
                    g.wait_ge(P.DV.h, dv0 + (kk // 2) + 1)
                else:
                    g.wait_ge(P.PL.h, pl0 + (kk // 2) + 1)
            if k < 8:
                src = h["a0_bwT"][k * 128:(k + 1) * 128, :]
            else:
                i = k - 8
                cc, s = i // 8, i % 8
                src = h["a0_swT"][s, cc * 128:(cc + 1) * 128, :]
            _dma(g, P.WT[k % 2],
                 stg[:, (k % 2) * 3072:(k % 2) * 3072 + 3072], src)
            if k % 2 == 1:
                w_mult(g, P.PL, k)

    def vector(v):
        for k in range(NK):
            if k % 2 == 0:
                w_mult(v, P.DV, k)

    def sync(sy):
        for k in range(NK):
            if k % 2 == 0:
                sy.wait_ge(P.DV.h, dv0 + (k // 2) + 1)
            else:
                sy.wait_ge(P.PL.h, pl0 + (k // 2) + 1)
            dst = swsc[k * 512:(k + 1) * 512, :].rearrange(
                "(f p) o -> p f o", p=128)
            src = ob[:, (k % 2) * 3072:(k % 2) * 3072 + 3072].rearrange(
                "p (f o) -> p f o", o=768)
            _dma(sy, P.ST2[k % 2], dst, src)
        sy.wait_ge(P.ST2[0].h, P.ST2[0].n)
        sy.wait_ge(P.ST2[1].h, P.ST2[1].n)

    P.phase("a0_prepass", gpsimd=gp_all, vector=vector, sync=sync)
    carver.pop()


KSTRIDE = 56


def emit_a0_knots(P, h, knots):
    gp0 = P.GP.n

    def gpsimd(g):
        for cc in range(8):
            _dma(g, P.GP, knots[:, cc * KSTRIDE: cc * KSTRIDE + 12],
                 h["a0_grid"][cc * 128:(cc + 1) * 128, :])
        _dma(g, P.GP, P.rt2b[:], h["a0_RT2"])
        _dma(g, P.GP, P.brow[0:1, 0:1024], h["a0_out_b"])
        g.wait_ge(P.GP.h, P.GP.n)

    def vector(v):
        v.wait_ge(P.GP.h, gp0 + 10 * 16)
        for cc in range(8):
            t = knots[:, cc * KSTRIDE: cc * KSTRIDE + 12]
            nt = knots[:, cc * KSTRIDE + 12: cc * KSTRIDE + 24]
            v.tensor_scalar_mul(nt, t, -1.0)
            v.drain()
            for p, off, n in ((1, 24, 11), (2, 35, 10), (3, 45, 9)):
                d = knots[:, cc * KSTRIDE + off: cc * KSTRIDE + off + n]
                v.tensor_sub(d, t[:, p:p + n], t[:, 0:n])
                v.drain()
                v.reciprocal(d, d)
                v.drain()
                if p == 1:
                    v.tensor_scalar_mul(d, d, 0.5)
                    v.drain()
        P.DV.inc(v.drain())

    P.phase("a0_knots", gpsimd=gpsimd, vector=vector)


def emit_a0_bs_group(P, h, g, knots, xg32, uw32, sgn32, bwork, bs, sx):
    pe0, dv0, ac0 = P.PE.n, P.DV.n, P.AC.n

    def tensor(t):
        for cc in range(8):
            if cc >= 2:
                t.wait_ge(P.DV.h, dv0 + (cc - 2) * 3 + 1)
            for kc in range(2):
                tt = 2 * g + kc
                m = t.transpose(
                    P.psA[:, 2048 + (cc % 2) * 512 + kc * 128:
                          2048 + (cc % 2) * 512 + kc * 128 + 128],
                    P.xres[:, tt * D + cc * 128: tt * D + (cc + 1) * 128],
                    P.identf[:])
            P.PE.inc(m)

    def vector(v):
        for cc in range(8):
            v.wait_ge(P.PE.h, pe0 + cc + 1)
            xs = xg32[:, (cc % 2) * 256:(cc % 2) * 256 + 256]
            if cc >= 2:
                v.wait_ge(P.AC.h, ac0 + 2 * (cc - 2) + 2)
            v.tensor_copy(xs[:, 0:128],
                          P.psA[:, 2048 + (cc % 2) * 512:
                                2048 + (cc % 2) * 512 + 128])
            m = v.tensor_copy(xs[:, 128:256],
                              P.psA[:, 2048 + (cc % 2) * 512 + 128:
                                    2048 + (cc % 2) * 512 + 256])
            P.DV.inc(m)
            v.wait_ge(P.AC.h, ac0 + 2 * cc + 1)
            kb = cc * KSTRIDE
            tcol = knots[:, kb: kb + 12]
            for j in range(11):
                m = v.tensor_sub(bwork[:, j * 256:(j + 1) * 256],
                                 sgn32[:, j * 256:(j + 1) * 256],
                                 sgn32[:, (j + 1) * 256:(j + 2) * 256])
            P.DV.inc(m)
            U = uw32[:, 0:256]
            W = uw32[:, 256:512]
            for p, off in ((1, 24), (2, 35), (3, 45)):
                for j in range(11 - p):
                    bj = bwork[:, j * 256:(j + 1) * 256]
                    bj1 = bwork[:, (j + 1) * 256:(j + 2) * 256]
                    r1 = knots[:, kb + off + j: kb + off + j + 1]
                    r2 = knots[:, kb + off + j + 1: kb + off + j + 2]
                    v.tensor_scalar(U, xs, tcol[:, j: j + 1], r1,
                                    ALU.subtract, ALU.mult)
                    v.tensor_scalar(W, xs, tcol[:, j + p + 1: j + p + 2], r2,
                                    ALU.subtract, ALU.mult)
                    v.tensor_mul(U, U, bj)
                    v.tensor_mul(W, W, bj1)
                    if p == 3:
                        m = v.tensor_sub(bs[:, (j * 8 + cc) * 256:
                                            (j * 8 + cc) * 256 + 256], U, W)
                    else:
                        m = v.tensor_sub(bj, U, W)
            P.DV.inc(m)

    def scalar(a):
        for cc in range(8):
            a.wait_ge(P.DV.h, dv0 + 3 * cc + 1)
            if cc >= 1:
                a.wait_ge(P.DV.h, dv0 + 3 * (cc - 1) + 2)
            xs = xg32[:, (cc % 2) * 256:(cc % 2) * 256 + 256]
            kb = cc * KSTRIDE
            for j in range(12):
                m = a.activation(sgn32[:, j * 256:(j + 1) * 256], xs, AF.Sign,
                                 bias=knots[:, kb + 12 + j: kb + 13 + j],
                                 scale=1.0)
            P.AC.inc(m)
            m = a.activation(sx[:, cc * 256:(cc + 1) * 256], xs, AF.Silu)
            P.AC.inc(m)
    P.phase(f"a0_bs_{g}", tensor=tensor, vector=vector, scalar=scalar)


def emit_a0_mm_group(P, h, g, bs, sx, wtl, stg6, swsc, qkv_scr):
    NL = 72
    NB = NL // 8
    wt0 = [P.WT[0].n, P.WT[1].n]
    st0 = [P.ST2[0].n, P.ST2[1].n]
    pe0, ac0 = P.PE.n, P.AC.n
    sw4 = swsc.rearrange("(i f p) o -> p i f o", f=4, p=128)

    def gpsimd(gq):
        for mg in range(4):
            for bb in range(NB):
                b = mg * NB + bb
                if b >= 2:
                    gq.wait_ge(P.PE.h, pe0 + (b - 2) + 1)
                dst = wtl[:, (b % 2) * 6144:(b % 2) * 6144 + 6144].rearrange(
                    "p (c o) -> p c o", o=768)
                _dma(gq, P.WT[b % 2], dst, sw4[:, 8 * bb:8 * bb + 8, mg])

    def tensor(t):
        for mg in range(4):
            for bb in range(NB):
                b = mg * NB + bb
                t.wait_ge(P.WT[b % 2].h, wt0[b % 2] + 16 * (b // 2 + 1))
                if bb == 0 and mg >= 1:
                    t.wait_ge(P.AC.h, ac0 + mg * 6)
                for sl in range(8):
                    l = bb * 8 + sl
                    if l < 8:
                        rhs = sx[:, l * 256:(l + 1) * 256]
                    else:
                        i = l - 8
                        slot = (i % 8) * 8 + i // 8
                        rhs = bs[:, slot * 256:(slot + 1) * 256]
                    for m in range(6):
                        mm = t.matmul(
                            P.psA[:, m * 512: m * 512 + 256],
                            wtl[:, (b % 2) * 6144 + sl * 768 + m * 128:
                                (b % 2) * 6144 + sl * 768 + (m + 1) * 128],
                            rhs, start=(l == 0), stop=(l == NL - 1))
                P.PE.inc(mm)

    def scalar(a):
        for mg in range(4):
            a.wait_ge(P.PE.h, pe0 + (mg + 1) * NB)
            if mg >= 1:
                a.wait_ge(P.ST2[0].h, st0[0] + 16 * 3 * mg)
                a.wait_ge(P.ST2[1].h, st0[1] + 16 * 3 * mg)
            for m in range(6):
                mm = a.copy(stg6[:, m * 256:(m + 1) * 256],
                            P.psA[:, m * 512: m * 512 + 256])
                P.AC.inc(mm)

    def sync(sy):
        for mg in range(4):
            sy.wait_ge(P.AC.h, ac0 + (mg + 1) * 6)
            for m in range(6):
                k = mg * 6 + m
                _dma(sy, P.ST2[k % 2],
                     qkv_scr[mg * 6 + m][:, g * 256:(g + 1) * 256],
                     stg6[:, m * 256:(m + 1) * 256])
        sy.wait_ge(P.ST2[0].h, P.ST2[0].n)
        sy.wait_ge(P.ST2[1].h, P.ST2[1].n)

    P.phase(f"a0_mm_{g}", gpsimd=gpsimd, tensor=tensor, scalar=scalar,
            sync=sync)


def emit_a0_reload_half(P, h, h0, qkv_scr, rbuf, qf, kf, vf):
    gp0, pe0, ac0 = P.GP.n, P.PE.n, P.AC.n
    chunks = list(range(h0 * 3 // 2, h0 * 3 // 2 + 12))
    nrot = []
    pe_plan = []
    for ci, m in enumerate(chunks):
        for half in range(2):
            b = 2 * m + half
            pe_plan.append((ci, m, b))

    def gpsimd(g):
        for ci, m in enumerate(chunks):
            if ci >= 2:
                t_inc = 2 * (ci - 1)
                g.wait_ge(P.PE.h, pe0 + t_inc)
            _dma(g, P.WT[ci % 2], rbuf[:, (ci % 2) * 2048:(ci % 2) * 2048 + 2048],
                 qkv_scr[m])

    wt0 = [P.WT[0].n, P.WT[1].n]

    def tensor(t):
        k_pe = 0
        for ci, m in enumerate(chunks):
            t.wait_ge(P.WT[ci % 2].h, wt0[ci % 2] + 16 * (ci // 2 + 1))
            for half in range(2):
                b = 2 * m + half
                hd, role = b // 3, b % 3
                rh = 64 * (b % 2)
                rho = 64 * (hd % 2)
                hl = hd - h0
                src = rbuf[rh:rh + 64, (ci % 2) * 2048:(ci % 2) * 2048 + 2048]
                k_pe += 1
                if role == 2:
                    if k_pe >= 2:
                        t.wait_ge(P.AC.h, ac0 + k_pe - 1)
                    for kc in range(16):
                        mm = t.transpose(
                            P.psT[:, kc * 64: kc * 64 + 64],
                            src[:, kc * 128:(kc + 1) * 128],
                            P.id642[rh:rh + 64, :], tile_position=(rh, 0))
                    P.PE.inc(mm)
                else:
                    if k_pe >= 2:
                        t.wait_ge(P.AC.h, ac0 + k_pe - 1)
                    for kb in range(4):
                        mm = t.matmul(
                            P.psA[rho:rho + 64, kb * 512:(kb + 1) * 512],
                            P.rt2b[rh:rh + 64, :],
                            src[:, kb * 512:(kb + 1) * 512],
                            start=True, stop=True, tile_position=(rh, rho))
                    P.PE.inc(mm)

    def scalar(a):
        k_pe = 0
        for ci, m in enumerate(chunks):
            for half in range(2):
                b = 2 * m + half
                hd, role = b // 3, b % 3
                rh = 64 * (b % 2)
                rho = 64 * (hd % 2)
                hl = hd - h0
                pr = hl // 2
                k_pe += 1
                a.wait_ge(P.PE.h, pe0 + k_pe)
                if role == 2:
                    for kc in range(16):
                        mm = a.copy(vf[:, kc * 512 + hl * 64:
                                       kc * 512 + hl * 64 + 64],
                                    P.psT[:, kc * 64: kc * 64 + 64])
                else:
                    arena = qf if role == 0 else kf
                    for kb in range(4):
                        dst = arena[rho:rho + 64, pr * 2048 + kb * 512:
                                    pr * 2048 + (kb + 1) * 512]
                        srcp = P.psA[rho:rho + 64, kb * 512:(kb + 1) * 512]
                        if role == 0:
                            mm = a.activation(dst, srcp, AF.Copy, scale=0.125)
                        else:
                            mm = a.copy(dst, srcp)
                P.AC.inc(mm)

    P.phase(f"a0_reload_{h0}", gpsimd=gpsimd, tensor=tensor, scalar=scalar)


def emit_a0_attn_half(P, h, h0, qf, kf, vf, carver):
    carver.push()
    probs = carver.take(2 * 2048, BF16)
    attnT = carver.take(2048, BF16)
    ctxq = carver.take(2 * 512, BF16)
    nm1 = [P.stat[:, 48:49], P.stat[:, 49:50]]
    zsum = [P.stat[:, 50:51], P.stat[:, 51:52]]
    rcW = P.stat[:, 64:128]
    NSH = 8 * NTT
    CB = 2048
    WB = 2560
    gp0 = P.GP.n
    pe0, dv0, ac0 = P.PE.n, P.DV.n, P.AC.n

    def gp(g):
        src = h["a0_owT"][h0 * 64: h0 * 64 + 512, :].rearrange(
            "(cc p) o -> p cc o", p=128)
        dst = P.wbuf[:, 0:4096].rearrange("p (cc o) -> p cc o", o=1024)
        _dma(g, P.GP, dst, src)

    order = {"pe": [], "dv": [], "ac": []}
    for i in range(NSH + 7):
        if 0 <= i - 3 < NSH:
            order["pe"].append(("B", i - 3))
        if 0 <= i - 4 < NSH:
            order["pe"].append(("C", i - 4))
        if 0 <= i - 6 < NSH and (i - 6) % 8 == 7:
            order["pe"].append(("D1", (i - 6) // 8))
        if i < NSH:
            order["pe"].append(("A", i))
        if 0 <= i - 5 < NSH and (i - 5) % 8 == 7:
            order["pe"].append(("D0", (i - 5) // 8))
        if 0 <= i - 1 < NSH:
            order["dv"].append(("minn", i - 1))
        if 0 <= i - 2 < NSH:
            order["dv"].append(("recip", i - 2))
            order["dv"].append(("mult", i - 2))
        if 0 <= i - 5 < NSH and (i - 5) % 8 == 7:
            order["dv"].append(("eo0", (i - 5) // 8))
        if 0 <= i - 6 < NSH and (i - 6) % 8 == 7:
            order["dv"].append(("eo1", (i - 6) // 8))
        if 0 <= i - 1 < NSH:
            order["ac"].append(("exp", i - 1))
        if 0 <= i - 3 < NSH:
            order["ac"].append(("aev", i - 3))
        if 0 <= i - 4 < NSH:
            order["ac"].append(("cev", i - 4))
    pe_i = {k: n + 1 for n, k in enumerate(order["pe"])}
    dv_i = {k: n + 1 for n, k in enumerate(order["dv"])}
    ac_i = {k: n + 1 for n, k in enumerate(order["ac"])}

    def tensor(t):
        for tag, s in order["pe"]:
            if tag == "A":
                qt, hl = s // 8, s % 8
                rh, pr = 64 * (hl % 2), hl // 2
                if s >= 1:
                    t.wait_ge(P.AC.h, ac0 + ac_i[("exp", s - 1)])
                for kb in range(4):
                    mm = t.matmul(
                        P.psA[:, kb * 512:(kb + 1) * 512],
                        qf[rh:rh + 64, pr * 2048 + qt * 128:
                           pr * 2048 + (qt + 1) * 128],
                        kf[rh:rh + 64, pr * 2048 + kb * 512:
                           pr * 2048 + (kb + 1) * 512],
                        start=True, stop=True, tile_position=(rh, 0))
                P.PE.inc(mm)
            elif tag == "B":
                t.wait_ge(P.DV.h, dv0 + dv_i[("mult", s)])
                if s >= 1:
                    t.wait_ge(P.AC.h, ac0 + ac_i[("aev", s - 1)])
                for kc in range(16):
                    mm = t.transpose(
                        P.psT[:, kc * 128:(kc + 1) * 128],
                        probs[:, (s % 2) * 2048 + kc * 128:
                              (s % 2) * 2048 + (kc + 1) * 128],
                        P.identb[:])
                P.PE.inc(mm)
            elif tag == "C":
                hl = s % 8
                rh, pr = 64 * (hl % 2), hl // 2
                t.wait_ge(P.AC.h, ac0 + ac_i[("aev", s)])
                if s >= 1:
                    t.wait_ge(P.AC.h, ac0 + ac_i[("cev", s - 1)])
                for kc in range(16):
                    mm = t.matmul(
                        P.psA[rh:rh + 64, CB + (pr % 2) * 128:
                              CB + (pr % 2) * 128 + 128],
                        vf[:, kc * 512 + hl * 64: kc * 512 + hl * 64 + 64],
                        attnT[:, kc * 128:(kc + 1) * 128],
                        start=(kc == 0), stop=(kc == 15),
                        tile_position=(0, rh))
                P.PE.inc(mm)
            elif tag == "D0":
                qt = s
                t.wait_ge(P.AC.h, ac0 + ac_i[("cev", 8 * qt + 7)])
                if qt == 0:
                    t.wait_ge(P.GP.h, gp0 + 16)
                else:
                    t.wait_ge(P.DV.h, dv0 + dv_i[("eo1", qt - 1)])
                ob = P.psA[:, WB:WB + 512]
                for cc in range(4):
                    mm = t.matmul(ob, ctxq[:, (qt % 2) * 512 + cc * 128:
                                           (qt % 2) * 512 + (cc + 1) * 128],
                                  P.wbuf[:, cc * 1024: cc * 1024 + 512],
                                  start=(cc == 0), stop=(cc == 3 and
                                                         not (h0 == 0)))
                if h0 == 0:
                    mm = t.matmul(ob, P.ones1[0:1, 0:128],
                                  P.brow[0:1, 0:512],
                                  start=False, stop=True)
                P.PE.inc(mm)
            else:
                qt = s
                t.wait_ge(P.DV.h, dv0 + dv_i[("eo0", qt)])
                ob = P.psA[:, WB:WB + 512]
                for cc in range(4):
                    mm = t.matmul(ob, ctxq[:, (qt % 2) * 512 + cc * 128:
                                           (qt % 2) * 512 + (cc + 1) * 128],
                                  P.wbuf[:, cc * 1024 + 512:
                                         cc * 1024 + 1024],
                                  start=(cc == 0), stop=(cc == 3 and
                                                         not (h0 == 0)))
                if h0 == 0:
                    mm = t.matmul(ob, P.ones1[0:1, 0:128],
                                  P.brow[0:1, 512:1024],
                                  start=False, stop=True)
                P.PE.inc(mm)

    def vector(v):
        for tag, s in order["dv"]:
            if tag == "minn":
                v.wait_ge(P.PE.h, pe0 + pe_i[("A", s)])
                if s >= 2:
                    v.wait_ge(P.AC.h, ac0 + ac_i[("exp", s - 2)])
                m = v.reduce_max(nm1[s % 2], P.psA[:, 0:2048].rearrange(
                    "p (g n) -> p g n", g=1), AX.X, negate=True)
                P.DV.inc(m)
            elif tag == "recip":
                v.wait_ge(P.AC.h, ac0 + ac_i[("exp", s)])
                m = v.reciprocal(rcW, zsum[s % 2].broadcast_to([128, 64]))
                P.DV.inc(m)
            elif tag == "mult":
                m = v.tensor_scalar_mul(
                    probs[:, (s % 2) * 2048:(s % 2) * 2048 + 2048],
                    probs[:, (s % 2) * 2048:(s % 2) * 2048 + 2048],
                    rcW[:, 0:1])
                P.DV.inc(m)
            else:
                qt, oc = s, 0 if tag == "eo0" else 1
                v.wait_ge(P.PE.h, pe0 + pe_i[("D" + str(oc), qt)])
                xs = P.xres[:, qt * D + oc * 512: qt * D + (oc + 1) * 512]
                m = v.tensor_add(xs, xs, P.psA[:, WB:WB + 512])
                P.DV.inc(m)

    def scalar(a):
        for tag, s in order["ac"]:
            if tag == "exp":
                a.wait_ge(P.DV.h, dv0 + dv_i[("minn", s)])
                if s >= 2:
                    a.wait_ge(P.PE.h, pe0 + pe_i[("B", s - 2)])
                m = a.activation(
                    probs[:, (s % 2) * 2048:(s % 2) * 2048 + 2048],
                    P.psA[:, 0:2048], AF.Exp,
                    bias=nm1[s % 2], scale=1.0, accum_out=zsum[s % 2])
                P.AC.inc(m)
            elif tag == "aev":
                a.wait_ge(P.PE.h, pe0 + pe_i[("B", s)])
                if s >= 1:
                    a.wait_ge(P.PE.h, pe0 + pe_i[("C", s - 1)])
                m = a.copy(attnT[:], P.psT[:])
                P.AC.inc(m)
            else:
                qt, hl = s // 8, s % 8
                rh, pr = 64 * (hl % 2), hl // 2
                a.wait_ge(P.PE.h, pe0 + pe_i[("C", s)])
                if hl == 0 and qt >= 2:
                    a.wait_ge(P.PE.h, pe0 + pe_i[("D1", qt - 2)])
                m = a.copy(ctxq[rh:rh + 64, (qt % 2) * 512 + pr * 128:
                                (qt % 2) * 512 + pr * 128 + 128],
                           P.psA[rh:rh + 64, CB + (pr % 2) * 128:
                                 CB + (pr % 2) * 128 + 128])
                P.AC.inc(m)

    P.phase(f"a0_attn_{h0}", gpsimd=gp, tensor=tensor, vector=vector,
            scalar=scalar)
    carver.pop()


def emit_l0_attn(P, carver, h):
    swsc = P.nc.dram_tensor("swsc", [72 * 4 * 128, 768], BF16).ap()
    qkv_scr = [P.nc.dram_tensor(f"qkvscr{m}", [128, 2048], BF16).ap()
               for m in range(24)]
    emit_a0_prepass(P, h, carver, swsc)
    carver.push()
    knots = carver.take(8 * KSTRIDE, F32)
    emit_a0_knots(P, h, knots)
    carver.push()
    bs = carver.take(64 * 256, BF16)
    sx = carver.take(8 * 256, BF16)
    xg32 = carver.take(2 * 256, F32)
    uw32 = carver.take(2 * 256, F32)
    sgn32 = carver.take(12 * 256, F32)
    bwork = carver.take(11 * 256, F32)
    wtl = carver.take(2 * 8 * 768, BF16)
    stg6 = carver.take(6 * 256, BF16)
    for g in range(A0_NG):
        emit_a0_bs_group(P, h, g, knots, xg32, uw32, sgn32, bwork, bs, sx)
        emit_a0_mm_group(P, h, g, bs, sx, wtl, stg6, swsc, qkv_scr)
    carver.pop()
    for h0 in (0, 8):
        carver.push()
        qf = carver.take(4 * 2048, BF16)
        kf = carver.take(4 * 2048, BF16)
        vf = carver.take(NTT * 512, BF16)
        rbuf = carver.take(2 * 2048, BF16)
        emit_a0_reload_half(P, h, h0, qkv_scr, rbuf, qf, kf, vf)
        emit_a0_attn_half(P, h, h0, qf, kf, vf, carver)
        carver.pop()
    carver.pop()
    emit_postln(P)




def emit_f0_shift(P, xsb, src_cols):
    gp0 = P.GP.n

    def gpsimd(g):
        srcs = P.xres[:, :].rearrange("p (tt c) -> p tt c", c=D)[
            :, :, src_cols:src_cols + 512]
        dsts = P.shift_scr[1:S + 1, :].rearrange("(tt p) c -> p tt c", p=128)
        _dma(g, P.GP, dsts, srcs)
        g.wait_ge(P.GP.h, gp0 + 16)
        srcl = P.shift_scr[0:S, :].rearrange("(tt p) c -> p tt c", p=128)
        dstl = xsb[:, :].rearrange("p (tt c) -> p tt c", c=512)
        _dma(g, P.GP, dstl, srcl)
        g.wait_ge(P.GP.h, gp0 + 32)

    P.phase("f0_shift", gpsimd=gpsimd)


def emit_f0_qkv_half(P, h, li, h0, xfm, qk_sc, qf, kf, vf, wsl, qb16):
    wq = h["f0_Wqkv"]
    qk32 = P.scr[:, 0:1024]
    sq32 = P.scr[:, 1024:1536]
    qn32 = P.scr[:, 1536:2048]
    rot32 = P.scr[:, 2048:2560]
    ss8 = P.stat[:, 48:56]
    rr8 = P.stat[:, 56:64]
    rrW = P.stat[:, 64:128]

    for c0, kind in ((h0 * 64, "q"), (1024 + h0 * 64, "k"),
                     (2048 + h0 * 64, "v")):
        gp0, pe0, dv0, ac0 = P.GP.n, P.PE.n, P.DV.n, P.AC.n

        def gp(g, c0=c0):
            src = wq[li, :, c0:c0 + 512].rearrange("(cc p) o -> p cc o", p=128)
            dst = wsl[:, 0:4096].rearrange("p (cc o) -> p cc o", o=512)
            _dma(g, P.GP, dst, src)

        inc_mm = lambda tt: tt + 1 if tt <= 2 else 2 * tt - 1
        inc_T = lambda j: 4 + 2 * j if j <= 12 else 17 + j
        inc_sqrt = lambda tt: tt + 1 if tt < 4 else 2 * tt - 2
        inc_evac = lambda j: 5 + 2 * j if j < 12 else 17 + j

        def emit_mm(t, tt, dv0, ac0, kind):
            bank = P.psA[:, (tt % 2) * 512:(tt % 2) * 512 + 512]
            if tt >= 2:
                if kind == "v":
                    t.wait_ge(P.AC.h, ac0 + (tt - 2) + 1)
                else:
                    t.wait_ge(P.DV.h, dv0 + (tt - 2) * 8 + 1)
            for cc in range(8):
                mm = t.matmul(bank,
                              xfm[:, cc * XFM_STRIDE + 1 + tt * 128:
                                  cc * XFM_STRIDE + 1 + (tt + 1) * 128],
                              wsl[:, cc * 512:(cc + 1) * 512],
                              start=(cc == 0), stop=(cc == 7))
            P.PE.inc(mm)

        def emit_T(t, j, dv0, ac0):
            t.wait_ge(P.DV.h, dv0 + j * 8 + 8)
            if j >= 1:
                t.wait_ge(P.AC.h, ac0 + inc_evac(j - 1))
            sb = qb16[:, (j % 2) * 512:(j % 2) * 512 + 512]
            for jj in range(8):
                mm = t.transpose(
                    P.psT[64 * (jj % 2):64 * (jj % 2) + 64,
                          (jj // 2) * 128:(jj // 2) * 128 + 128],
                    sb[:, jj * 64:(jj + 1) * 64], P.identb[:],
                    tile_position=(0, 64 * (jj % 2)))
            P.PE.inc(mm)

        def tensor(t, kind=kind, gp0=gp0, pe0=pe0, dv0=dv0, ac0=ac0):
            t.wait_ge(P.GP.h, gp0 + 16)
            if kind == "v":
                for tt in range(NTT):
                    emit_mm(t, tt, dv0, ac0, kind)
            else:
                for tt in range(3):
                    emit_mm(t, tt, dv0, ac0, kind)
                for j in range(13):
                    emit_T(t, j, dv0, ac0)
                    if j + 3 < NTT:
                        emit_mm(t, j + 3, dv0, ac0, kind)
                for j in range(13, NTT):
                    emit_T(t, j, dv0, ac0)

        def vector(v, kind=kind, pe0=pe0, ac0=ac0):
            if kind == "v":
                return
            soff = 0 if kind == "q" else 128
            toff = 0 if kind == "q" else 2 * NTT * 64
            for tt in range(NTT):
                v.wait_ge(P.PE.h, pe0 + (tt + 1 if tt <= 2 else 2 * tt - 1))
                bank = P.psA[:, (tt % 2) * 512:(tt % 2) * 512 + 512]
                x32 = qk32[:, (tt % 2) * 512:(tt % 2) * 512 + 512]
                m = v.tensor_copy(x32, bank)
                P.DV.inc(m)
                m = v.scalar_tensor_tensor(sq32[:], x32, 1.0, x32, ALU.mult,
                                           ALU.mult)
                P.DV.inc(m)
                m = v.reduce_sum(ss8, sq32.rearrange("p (g d) -> p g d", d=64),
                                 AX.X)
                P.DV.inc(m)
                v.wait_ge(P.AC.h, ac0 + (tt + 1 if tt < 4 else 2 * tt - 2))
                m = v.reciprocal(rrW.rearrange("p (g r) -> p g r", r=8),
                                 rr8.unsqueeze(2).broadcast_to([128, 8, 8]))
                P.DV.inc(m)
                rrb = rrW.rearrange("p (g r) -> p g r", r=8)[
                    :, :, 0:1].broadcast_to([128, 8, 64])
                m = v.tensor_tensor(qn32.rearrange("p (g d) -> p g d", d=64),
                                    x32.rearrange("p (g d) -> p g d", d=64),
                                    rrb, ALU.mult)
                P.DV.inc(m)
                cs_t = P.rtab_b[:, toff + tt * 64: toff + tt * 64 + 64]
                ss_t = P.rtab_b[:, NTT * 64 + toff + tt * 64:
                                NTT * 64 + toff + tt * 64 + 64]
                csb = cs_t.unsqueeze(1).broadcast_to([128, 8, 64])
                m = v.tensor_tensor(rot32.rearrange("p (g d) -> p g d", d=64),
                                    qn32.rearrange("p (g d) -> p g d", d=64),
                                    csb, ALU.mult)
                P.DV.inc(m)
                ssa = ss_t[:, 0:32].unsqueeze(1).broadcast_to([128, 8, 32])
                ssb = ss_t[:, 32:64].unsqueeze(1).broadcast_to([128, 8, 32])
                va = qn32.rearrange("p (g d) -> p g d", d=64)
                sw = sq32.rearrange("p (g d) -> p g d", d=64)
                v.tensor_tensor(sw[:, :, 0:32], va[:, :, 32:64], ssa, ALU.mult)
                m = v.tensor_tensor(sw[:, :, 32:64], va[:, :, 0:32], ssb,
                                    ALU.mult)
                P.DV.inc(m)
                if tt >= 2:
                    v.wait_ge(P.PE.h, pe0 + (4 + 2 * (tt - 2)
                                             if tt - 2 <= 12 else 15 + tt))
                dst = qb16[:, (tt % 2) * 512:(tt % 2) * 512 + 512]
                m = v.tensor_add(dst, rot32, sq32)
                P.DV.inc(m)

        def scalar(a, kind=kind, pe0=pe0, dv0=dv0):
            if kind == "v":
                for tt in range(NTT):
                    a.wait_ge(P.PE.h, pe0 + tt + 1)
                    bank = P.psA[:, (tt % 2) * 512:(tt % 2) * 512 + 512]
                    m = a.copy(vf[:, tt * 512:(tt + 1) * 512], bank)
                    P.AC.inc(m)
            else:
                arena = qf if kind == "q" else kf

                def emit_sqrt(a, tt):
                    a.wait_ge(P.DV.h, dv0 + tt * 8 + 3)
                    m = a.activation(rr8, ss8, AF.Sqrt, bias=P.eps24,
                                     scale=1.0)
                    P.AC.inc(m)

                def emit_ev(a, j):
                    a.wait_ge(P.PE.h, pe0 + (4 + 2 * j if j <= 12 else 17 + j))
                    for pr in range(4):
                        m = a.copy(arena[:, pr * 2048 + j * 128:
                                         pr * 2048 + (j + 1) * 128],
                                   P.psT[:, pr * 128:(pr + 1) * 128])
                    P.AC.inc(m)

                for tt in range(4):
                    emit_sqrt(a, tt)
                for j in range(12):
                    emit_ev(a, j)
                    emit_sqrt(a, j + 4)
                for j in range(12, NTT):
                    emit_ev(a, j)
        P.phase(f"f0_qkv_{kind}", gpsimd=gp, tensor=tensor, vector=vector,
                scalar=scalar)


def emit_f0_attn_half(P, h, li, h0, qf, kf, vf, carver):
    carver.push()
    probs = carver.take(2 * 2048, BF16)
    attnT = carver.take(2048, BF16)
    ctxq = carver.take(2 * 512, BF16)
    asum = P.stat[:, 48:52]
    nm4 = [P.stat[:, 52:56], P.stat[:, 56:60]]
    rc4 = [P.stat[:, 64:128], P.stat[:, 128:192]]
    NS = 2 * NTT
    CB = 2048
    WB = 2560

    def nkof(s):
        return 512 if s // 2 >= 2 else 256

    def ksof(s):
        return 256 * (s // 2 // 2 - 1) if s // 2 >= 2 else 0

    order = {"pe": [], "dv": [], "ac": []}
    for s in range(NS + 6):
        if s < NS:
            order["pe"].append(("A", s))
        if 0 <= s - 5 < NS and (s - 5) % 2 == 1:
            order["pe"].append(("D1", (s - 5) // 2))
        if 0 <= s - 2 < NS:
            order["pe"].append(("B", s - 2))
        if 0 <= s - 3 < NS:
            order["pe"].append(("C", s - 3))
        if 0 <= s - 4 < NS and (s - 4) % 2 == 1:
            order["pe"].append(("D0", (s - 4) // 2))
        if 0 <= s - 1 < NS:
            order["dv"].append(("recip", s - 1))
        if 0 <= s - 2 < NS:
            order["dv"].append(("mult", s - 2))
        if s < NS:
            order["dv"].append(("minnA", s))
            order["dv"].append(("minn", s))
        if 0 <= s - 4 < NS and (s - 4) % 2 == 1:
            order["dv"].append(("eo0", (s - 4) // 2))
        if 0 <= s - 5 < NS and (s - 5) % 2 == 1:
            order["dv"].append(("eo1", (s - 5) // 2))
        if 0 <= s - 1 < NS:
            order["ac"].append(("exp", s - 1))
        if 0 <= s - 2 < NS:
            order["ac"].append(("aev", s - 2))
        if 0 <= s - 3 < NS:
            order["ac"].append(("cev", s - 3))
    pe_i = {k: i + 1 for i, k in enumerate(order["pe"])}
    dv_i = {k: i + 1 for i, k in enumerate(order["dv"])}
    ac_i = {k: i + 1 for i, k in enumerate(order["ac"])}

    gp0 = P.GP.n
    pe0, dv0, ac0 = P.PE.n, P.DV.n, P.AC.n

    def gp(g):
        src = h["f0_Wo"][li, h0 * 64: h0 * 64 + 512, :].rearrange(
            "(cc p) o -> p cc o", p=128)
        dst = P.wbuf[:, 0:4096].rearrange("p (cc o) -> p cc o", o=1024)
        _dma(g, P.GP, dst, src)

    def tensor(t):
        for tag, s in order["pe"]:
            if tag == "A":
                nk, ks, qt = nkof(s), ksof(s), s // 2
                hg = s % 2
                if s >= 1:
                    t.wait_ge(P.DV.h, dv0 + dv_i[("minn", s - 1)])
                for jj in range(4):
                    j = hg * 4 + jj
                    rh, pr = 64 * (j % 2), j // 2
                    mm = t.matmul(
                        P.psA[:, jj * 512: jj * 512 + nk],
                        qf[rh:rh + 64, pr * 2048 + qt * 128:
                           pr * 2048 + (qt + 1) * 128],
                        kf[rh:rh + 64, pr * 2048 + ks: pr * 2048 + ks + nk],
                        start=True, stop=True, tile_position=(rh, 0))
                P.PE.inc(mm)
            elif tag == "B":
                nk = nkof(s)
                t.wait_ge(P.DV.h, dv0 + dv_i[("mult", s)])
                if s >= 1:
                    t.wait_ge(P.AC.h, ac0 + ac_i[("aev", s - 1)])
                for jj in range(4):
                    for kc in range(nk // 128):
                        mm = t.transpose(
                            P.psT[:, (jj * 4 + kc) * 128:
                                  (jj * 4 + kc) * 128 + 128],
                            probs[:, (s % 2) * 2048 + jj * 512 + kc * 128:
                                  (s % 2) * 2048 + jj * 512 + (kc + 1) * 128],
                            P.identb[:])
                P.PE.inc(mm)
            elif tag == "C":
                nk, ks, hg = nkof(s), ksof(s), s % 2
                t.wait_ge(P.AC.h, ac0 + ac_i[("aev", s)])
                if s >= 1:
                    t.wait_ge(P.AC.h, ac0 + ac_i[("cev", s - 1)])
                for jj in range(4):
                    j = hg * 4 + jj
                    rh, pr = 64 * (j % 2), j // 2
                    prl = pr - 2 * hg
                    for kc in range(nk // 128):
                        ktt = (ks // 128) + kc
                        mm = t.matmul(
                            P.psA[rh:rh + 64, CB + prl * 128:
                                  CB + prl * 128 + 128],
                            vf[:, ktt * 512 + j * 64: ktt * 512 + j * 64 + 64],
                            attnT[:, (jj * 4 + kc) * 128:
                                  (jj * 4 + kc) * 128 + 128],
                            start=(kc == 0), stop=(kc == nk // 128 - 1),
                            tile_position=(0, rh))
                P.PE.inc(mm)
            elif tag == "D0":
                qt = s
                t.wait_ge(P.AC.h, ac0 + ac_i[("cev", 2 * qt + 1)])
                if qt == 0:
                    t.wait_ge(P.GP.h, gp0 + 16)
                else:
                    t.wait_ge(P.DV.h, dv0 + dv_i[("eo1", qt - 1)])
                for cc in range(4):
                    mm = t.matmul(P.psA[:, WB:WB + 512],
                                  ctxq[:, (qt % 2) * 512 + cc * 128:
                                       (qt % 2) * 512 + (cc + 1) * 128],
                                  P.wbuf[:, cc * 1024: cc * 1024 + 512],
                                  start=(cc == 0), stop=(cc == 3))
                P.PE.inc(mm)
            else:
                qt = s
                t.wait_ge(P.DV.h, dv0 + dv_i[("eo0", qt)])
                for cc in range(4):
                    mm = t.matmul(P.psA[:, WB:WB + 512],
                                  ctxq[:, (qt % 2) * 512 + cc * 128:
                                       (qt % 2) * 512 + (cc + 1) * 128],
                                  P.wbuf[:, cc * 1024 + 512:
                                         cc * 1024 + 1024],
                                  start=(cc == 0), stop=(cc == 3))
                P.PE.inc(mm)

    def vector(v):
        for tag, s in order["dv"]:
            if tag == "recip":
                v.wait_ge(P.AC.h, ac0 + ac_i[("exp", s)])
                m = v.reciprocal(rc4[s % 2].rearrange("p (g r) -> p g r", r=16),
                                 asum.unsqueeze(2).broadcast_to([128, 4, 16]))
                P.DV.inc(m)
            elif tag == "mult":
                nk = nkof(s)
                for jj in range(4):
                    m = v.tensor_scalar_mul(
                        probs[:, (s % 2) * 2048 + jj * 512:
                              (s % 2) * 2048 + jj * 512 + nk],
                        probs[:, (s % 2) * 2048 + jj * 512:
                              (s % 2) * 2048 + jj * 512 + nk],
                        rc4[s % 2][:, jj * 16: jj * 16 + 1])
                P.DV.inc(m)
            elif tag == "minnA":
                nk, qt = nkof(s), s // 2
                v.wait_ge(P.PE.h, pe0 + pe_i[("A", s)])
                if s >= 2:
                    v.wait_ge(P.AC.h, ac0 + ac_i[("exp", s - 2)])
                mk = P.masks[:, 512 * (qt % 2): 512 * (qt % 2) + 512]
                mkb = mk[:, 512 - nk:512].unsqueeze(1).broadcast_to(
                    [128, 4, nk])
                sc = P.scr[:, 0:4 * nk]
                m = v.tensor_tensor(sc.rearrange("p (g n) -> p g n", g=4),
                                    P.psA[:, 0:2048].rearrange(
                                        "p (g n) -> p g n", g=4)[:, :, 0:nk],
                                    mkb, ALU.add)
                P.DV.inc(m)
            elif tag == "minn":
                nk = nkof(s)
                sc = P.scr[:, 0:4 * nk]
                m = v.reduce_max(nm4[s % 2],
                                 sc.rearrange("p (g n) -> p g n", g=4),
                                 AX.X, negate=True)
                P.DV.inc(m)
            else:
                qt, oc = s, 0 if tag == "eo0" else 1
                v.wait_ge(P.PE.h, pe0 + pe_i[("D" + str(oc), qt)])
                xs = P.xres[:, qt * D + oc * 512: qt * D + (oc + 1) * 512]
                m = v.tensor_add(xs, xs, P.psA[:, WB:WB + 512])
                P.DV.inc(m)

    def scalar(a):
        for tag, s in order["ac"]:
            if tag == "exp":
                nk = nkof(s)
                a.wait_ge(P.DV.h, dv0 + dv_i[("minn", s)])
                if s >= 2:
                    a.wait_ge(P.PE.h, pe0 + pe_i[("B", s - 2)])
                sc = P.scr[:, 0:4 * nk]
                for jj in range(4):
                    m = a.activation(
                        probs[:, (s % 2) * 2048 + jj * 512:
                              (s % 2) * 2048 + jj * 512 + nk],
                        sc[:, jj * nk:(jj + 1) * nk], AF.Exp,
                        bias=nm4[s % 2][:, jj: jj + 1], scale=1.0,
                        accum_out=asum[:, jj: jj + 1])
                P.AC.inc(m)
            elif tag == "aev":
                nk = nkof(s)
                a.wait_ge(P.PE.h, pe0 + pe_i[("B", s)])
                if s >= 1:
                    a.wait_ge(P.PE.h, pe0 + pe_i[("C", s - 1)])
                m = a.copy(
                    attnT[:, 0:2048].rearrange("p (j n) -> p j n", j=4)[
                        :, :, 0:nk],
                    P.psT[:, 0:2048].rearrange("p (j n) -> p j n", j=4)[
                        :, :, 0:nk])
                P.AC.inc(m)
            else:
                qt, hg = s // 2, s % 2
                a.wait_ge(P.PE.h, pe0 + pe_i[("C", s)])
                if qt >= 2 and hg == 0:
                    a.wait_ge(P.PE.h, pe0 + pe_i[("D1", qt - 2)])
                m = a.copy(ctxq[:, (qt % 2) * 512 + 2 * hg * 128:
                                (qt % 2) * 512 + (2 * hg + 2) * 128],
                           P.psA[:, CB: CB + 256])
                P.AC.inc(m)

    P.phase("f0_attn", gpsimd=gp, tensor=tensor, vector=vector, scalar=scalar)
    carver.pop()


def emit_f0_ffn(P, h, li, xfm, carver):
    nc = P.nc
    lgc = carver.take(8, F32)
    lnbb = carver.take(8, BF16)
    wsl = carver.take(2 * 4096, BF16)
    carver.push()
    xsb = carver.take(NTT * 512, BF16)
    emit_f0_shift(P, xsb, 512)

    mu, rs, t16 = P.mu, P.rs, P.tmp16
    sB = P.stat[:, 48:64]
    ssB = P.stat[:, 64:80]
    junk = P.scr[:, 0:512]
    dv0, ac0 = P.DV.n, P.AC.n

    def v_stats(v):
        jk = lambda i: P.scr[:, i * 512:(i + 1) * 512]
        for tt in range(NTT):
            xa = P.xres[:, tt * D: tt * D + 512]
            xb = xsb[:, tt * 512:(tt + 1) * 512]
            v.tensor_scalar(jk(0), xa, 1.0, 0.0, ALU.mult, ALU.add,
                            accum_out=mu[:, tt: tt + 1])
            v.scalar_tensor_tensor(jk(1), xa, 1.0, xa, ALU.mult, ALU.mult,
                                   accum_out=rs[:, tt: tt + 1])
            v.tensor_scalar(jk(2), xb, 1.0, 0.0, ALU.mult, ALU.add,
                            accum_out=sB[:, tt: tt + 1])
            v.scalar_tensor_tensor(jk(3), xb, 1.0, xb, ALU.mult, ALU.mult,
                                   accum_out=ssB[:, tt: tt + 1])
            P.DV.inc(v.drain())
            v.wait_ge(P.DV.h, P.DV.n)
        v.tensor_add(mu[:], mu[:], sB[:])
        v.tensor_add(rs[:], rs[:], ssB[:])
        v.drain()
        v.tensor_scalar_mul(mu[:], mu[:], 1.0 / D)
        v.tensor_scalar_mul(rs[:], rs[:], 1.0 / D)
        v.drain()
        v.tensor_mul(t16[:], mu[:], mu[:])
        v.drain()
        v.tensor_sub(rs[:], rs[:], t16[:])
        P.DV.inc(v.drain())
        v.wait_ge(P.AC.h, ac0 + 1)
        v.reciprocal(rs[:], rs[:])
        v.drain()

    def a_stats(a):
        a.wait_ge(P.DV.h, dv0 + NTT + 1)
        m = a.activation(rs[:], rs[:], AF.Sqrt, bias=P.eps, scale=1.0)
        P.AC.inc(m)

    P.phase("f0_stats", vector=v_stats, scalar=a_stats)

    def src_xn(v, tt, slot):
        v.tensor_scalar(slot[:, 0:512], P.xres[:, tt * D: tt * D + 512],
                        mu[:, tt: tt + 1], rs[:, tt: tt + 1],
                        ALU.subtract, ALU.mult)
        return v.tensor_scalar(slot[:, 512:1024], xsb[:, tt * 512:(tt + 1) * 512],
                               mu[:, tt: tt + 1], rs[:, tt: tt + 1],
                               ALU.subtract, ALU.mult)

    gpb = P.GP.n

    def gp_bias(g):
        _dma(g, P.GP, lgc[:, 0:8],
             h["f0_ln_g"][li].rearrange("(cc p) u -> p (cc u)", p=128),
             allow_slow_non_contiguous=True)
        _dma(g, P.GP, lnbb[:, 0:8],
             h["f0_ln_b"][li].rearrange("(cc p) u -> p (cc u)", p=128),
             allow_slow_non_contiguous=True)
        for sj in range(12):
            j, part = sj // 2, sj % 2
            w = 512 if j < 5 else L0F_DI - 512 * 5
            c0 = part * L0F_DI + 512 * j
            r, cs = 32 * (sj // 3), 512 * (sj % 3)
            _dma(g, P.GP, P.brow[r:r + 1, cs:cs + w],
                 h["f0_b1"][li][:, c0:c0 + w])
        _dma(g, P.GP, P.brow[32:33, 1536:2048], h["f0_b2"][li][:, 0:512])
        _dma(g, P.GP, P.brow[64:65, 1536:2048], h["f0_b2"][li][:, 512:1024])
        g.wait_ge(P.GP.h, P.GP.n)

    WJ = [512] * 5 + [L0F_DI - 2560]
    w1_ready = {}

    def w1_load(g, ttg, j):
        sem = P.ST2[j % 2]
        w = WJ[j]
        for part in range(2):
            c0 = part * L0F_DI + 512 * j
            src = h["f0_W1"][li][:, c0:c0 + w].rearrange(
                "(cc p) o -> p cc o", p=128)
            dst = wsl[:, part * 4096: part * 4096 + 8 * w].rearrange(
                "p (cc o) -> p cc o", o=w)
            _dma(g, sem, dst, src)
        w1_ready[(ttg, j)] = (sem, sem.n)

    def gp_bias2(g):
        gp_bias(g)
        w1_load(g, 0, 0)

    P.phase("f0_bias", gpsimd=gp_bias2)
    build_xfm(P, xfm, src_xn, evac_scale=lambda cc: lgc[:, cc: cc + 1])
    carver.pop()

    carver.push()
    agf = carver.take(22 * 1024, BF16)
    gel = carver.take(2 * 512, BF16)
    ag16 = carver.take(2 * 512, BF16)
    for ttg in range(2):
        for j in range(6):
            w = WJ[j]
            nblk = (w + 127) // 128
            gp0, pe0, dv0, ac0, pl0 = P.GP.n, P.PE.n, P.DV.n, P.AC.n, P.PL.n
            sj_a, sj_g = j * 2, j * 2 + 1

            def gp(g, j=j, ttg=ttg, pe0=pe0):
                if j < 5:
                    npe_beff = 2 if ttg == 0 else 0
                    g.wait_ge(P.PE.h, pe0 + npe_beff + 13)
                    w1_load(g, ttg, j + 1)

            def tensor(t, j=j, w=w, ttg=ttg, gp0=gp0, pe0=pe0, dv0=dv0,
                       ac0=ac0, pl0=pl0, sj_a=sj_a, sj_g=sj_g):
                sem_r, val_r = w1_ready[(ttg, j)]
                t.wait_ge(sem_r.h, val_r)
                npe = 0
                if ttg == 0:
                    for part, sj in ((0, sj_a), (1, sj_g)):
                        r = 32 * (sj // 3)
                        for cc in range(8):
                            mm = t.matmul(
                                P.psA[r:r + 1, 2048 + part * 512:
                                      2048 + part * 512 + w],
                                lnbb[:, cc: cc + 1],
                                wsl[:, part * 4096 + cc * w:
                                    part * 4096 + (cc + 1) * w],
                                start=(cc == 0), stop=(cc == 7),
                                tile_position=(0, r))
                        P.PE.inc(mm)
                        npe += 1
                ndv_beff = 2 if ttg == 0 else 0
                if ttg == 0:
                    t.wait_ge(P.DV.h, dv0 + 2)

                def emit_mm(tl):
                    tt = ttg * 8 + tl
                    ba = P.psA[:, (tl % 2) * 1024:(tl % 2) * 1024 + w]
                    bg = P.psA[:, (tl % 2) * 1024 + 512:
                               (tl % 2) * 1024 + 512 + w]
                    if tl >= 2:
                        t.wait_ge(P.DV.h, dv0 + ndv_beff + (tl - 2) + 1)
                        t.wait_ge(P.AC.h, ac0 + 2 * (tl - 2) + 1)
                    for part, bank, sj in ((0, ba, sj_a), (1, bg, sj_g)):
                        r, cs = 32 * (sj // 3), 512 * (sj % 3)
                        for cc in range(8):
                            t.matmul(bank,
                                     xfm[:, cc * XFM_STRIDE + 1 + tt * 128:
                                         cc * XFM_STRIDE + 1 + (tt + 1) * 128],
                                     wsl[:, part * 4096 + cc * w:
                                         part * 4096 + (cc + 1) * w],
                                     start=(cc == 0), stop=False)
                        mm = t.matmul(bank, P.ones1[r:r + 1, 0:128],
                                      P.brow[r:r + 1, cs:cs + w],
                                      start=False, stop=True,
                                      tile_position=(r, 0))
                    P.PE.inc(mm)

                def emit_T(j):
                    t.wait_ge(P.DV.h, dv0 + ndv_beff + j + 1)
                    if j >= 1:
                        t.wait_ge(P.AC.h, ac0 + 2 * j)
                    for b in range(nblk):
                        bw = min(128, w - b * 128)
                        mm = t.transpose(P.psT[0:bw, b * 128: b * 128 + 128],
                                         ag16[:, (j % 2) * 512 + b * 128:
                                              (j % 2) * 512 + b * 128 + bw],
                                         P.identb[:])
                    P.PE.inc(mm)

                emit_mm(0)
                emit_mm(1)
                for j in range(6):
                    emit_mm(j + 2)
                    emit_T(j)
                emit_T(6)
                emit_T(7)

            def vector(v, j=j, w=w, ttg=ttg, pe0=pe0, ac0=ac0, sj_a=sj_a,
                       sj_g=sj_g):
                npe_beff = 2 if ttg == 0 else 0
                if ttg == 0:
                    v.wait_ge(P.PE.h, pe0 + 2)
                    for part, sj in ((0, sj_a), (1, sj_g)):
                        r, cs = 32 * (sj // 3), 512 * (sj % 3)
                        m = v.tensor_add(P.brow[r:r + 1, cs:cs + w],
                                         P.brow[r:r + 1, cs:cs + w],
                                         P.psA[r:r + 1, 2048 + part * 512:
                                               2048 + part * 512 + w])
                        P.DV.inc(m)
                imm = lambda tl: tl + 1 if tl < 2 else 2 * tl - 1
                iT = lambda j: 2 * j + 4 if j <= 5 else j + 9
                for tl in range(8):
                    v.wait_ge(P.PE.h, pe0 + npe_beff + imm(tl))
                    v.wait_ge(P.AC.h, ac0 + 2 * tl + 1)
                    if tl >= 2:
                        v.wait_ge(P.PE.h, pe0 + npe_beff + iT(tl - 2))
                    ba = P.psA[:, (tl % 2) * 1024:(tl % 2) * 1024 + w]
                    m = v.tensor_mul(ag16[:, (tl % 2) * 512:(tl % 2) * 512 + w],
                                     ba, gel[:, (tl % 2) * 512:(tl % 2) * 512 + w])
                    P.DV.inc(m)

            def scalar(a, j=j, w=w, ttg=ttg, pe0=pe0, dv0=dv0, nblk=nblk):
                npe_beff = 2 if ttg == 0 else 0
                ndv_beff = 2 if ttg == 0 else 0
                imm = lambda tl: tl + 1 if tl < 2 else 2 * tl - 1
                iT = lambda jj: 2 * jj + 4 if jj <= 5 else jj + 9
                for tl in range(8):
                    a.wait_ge(P.PE.h, pe0 + npe_beff + imm(tl))
                    if tl >= 2:
                        a.wait_ge(P.DV.h, dv0 + ndv_beff + tl - 1)
                    bg = P.psA[:, (tl % 2) * 1024 + 512:
                               (tl % 2) * 1024 + 512 + w]
                    m = a.activation(gel[:, (tl % 2) * 512:(tl % 2) * 512 + w],
                                     bg, AF.Gelu)
                    P.AC.inc(m)
                    a.wait_ge(P.PE.h, pe0 + npe_beff + iT(tl))
                    for b in range(nblk):
                        bw = min(128, w - b * 128)
                        m = a.copy(agf[0:bw, (4 * j + b) * 1024 + tl * 128:
                                       (4 * j + b) * 1024 + (tl + 1) * 128],
                                   P.psT[0:bw, b * 128: b * 128 + 128])
                    P.AC.inc(m)

            P.phase(f"f0_w1_{ttg}_{j}", gpsimd=gp,
                    tensor=tensor, vector=vector, scalar=scalar)

        CCW = [128] * 21 + [L0F_DI - 21 * 128]
        bcc = [[0, 1, 2, 3], [4, 5, 6, 7], [8, 9, 10, 11], [12, 13, 14, 15],
               [16, 17, 18, 19], [20], [21]]
        W2B = [(oc, ccs) for oc in range(2) for ccs in bcc]
        batchof = lambda cc: cc // 4 if cc < 20 else (5 if cc == 20 else 6)
        w2slot = lambda cc: ((batchof(cc) % 2) * 4 +
                             (cc - 4 * batchof(cc) if cc < 20 else 0))
        for sub in range(2):
            gp0, pe0, dv0 = P.GP.n, P.PE.n, P.DV.n

            wt0 = [P.WT[0].n, P.WT[1].n]

            def gp2(g, ttg=ttg, sub=sub, gp0=gp0, pe0=pe0):
                lastg = {}
                for bi, (oc, ccs) in enumerate(W2B):
                    grp = w2slot(ccs[0]) // 4
                    if grp in lastg:
                        poc, pccs = W2B[lastg[grp]]
                        g.wait_ge(P.PE.h, pe0 + poc * 23 + pccs[-1] + 1)
                    lastg[grp] = bi
                    c0 = ccs[0]
                    nb = len(ccs)
                    sl0 = w2slot(c0)
                    if CCW[ccs[-1]] == 128:
                        src = h["f0_W2"][li][c0 * 128:(c0 + nb) * 128,
                                             oc * 512:(oc + 1) * 512
                                             ].rearrange("(c p) o -> p c o",
                                                         p=128)
                        dst = wsl[:, sl0 * 512:(sl0 + nb) * 512].rearrange(
                            "p (c o) -> p c o", o=512)
                    else:
                        wcc = CCW[ccs[-1]]
                        src = h["f0_W2"][li][c0 * 128: c0 * 128 + wcc,
                                             oc * 512:(oc + 1) * 512]
                        dst = wsl[0:wcc, sl0 * 512: sl0 * 512 + 512]
                    _dma(g, P.WT[bi % 2], dst, src)
                if ttg == 0 and sub == 1:
                    g.wait_ge(P.PE.h, pe0 + 46)
                    w1_load(g, 1, 0)

            def tensor2(t, ttg=ttg, sub=sub, gp0=gp0, pe0=pe0, dv0=dv0):
                wtc = [0, 0]
                for bi, (oc, ccs) in enumerate(W2B):
                    wtc[bi % 2] += 16
                    for cc in ccs:
                        wcc = CCW[cc]
                        t.wait_ge(P.WT[bi % 2].h, wt0[bi % 2] + wtc[bi % 2])
                        for tq in range(4):
                            tl = sub * 4 + tq
                            bank = P.psA[:, tq * 512:(tq + 1) * 512]
                            if cc == 0 and oc == 1:
                                t.wait_ge(P.DV.h, dv0 + tq + 1)
                            mm = t.matmul(
                                bank,
                                agf[0:wcc, cc * 1024 + tl * 128:
                                    cc * 1024 + (tl + 1) * 128],
                                wsl[0:wcc, w2slot(cc) * 512:
                                    w2slot(cc) * 512 + 512],
                                start=(cc == 0), stop=False)
                        P.PE.inc(mm)
                    if ccs[-1] != 21:
                        continue
                    r = 32 * (oc + 1)
                    for tq in range(4):
                        bank = P.psA[:, tq * 512:(tq + 1) * 512]
                        mm = t.matmul(bank, P.ones1[r:r + 1, 0:128],
                                      P.brow[r:r + 1, 1536:2048],
                                      start=False, stop=True,
                                      tile_position=(r, 0))
                    P.PE.inc(mm)

            def vector2(v, ttg=ttg, sub=sub, pe0=pe0):
                for oc in range(2):
                    v.wait_ge(P.PE.h, pe0 + (oc + 1) * 23)
                    for tq in range(4):
                        tt = ttg * 8 + sub * 4 + tq
                        bank = P.psA[:, tq * 512:(tq + 1) * 512]
                        xs = P.xres[:, tt * D + oc * 512:
                                    tt * D + (oc + 1) * 512]
                        m = v.tensor_add(xs, xs, bank)
                        P.DV.inc(m)

            P.phase(f"f0_w2_{ttg}_{sub}", gpsimd=gp2, tensor=tensor2,
                    vector=vector2)
    carver.pop()


def emit_l0_ffn_layer(P, carver, h, li):
    carver.push()
    xfm = carver.take(8 * XFM_STRIDE, BF16)
    qk_sc = carver.take(4 * 64, F32)
    qk_scb = carver.take(4 * 64, BF16)

    carver.push()
    xsb = carver.take(NTT * 512, BF16)
    emit_f0_shift(P, xsb, 512)

    def src_xs(v, tt, slot):
        v.tensor_copy(slot[:, 0:512], P.xres[:, tt * D: tt * D + 512])
        return v.tensor_copy(slot[:, 512:1024], xsb[:, tt * 512:(tt + 1) * 512])

    build_xfm(P, xfm, src_xs)
    carver.pop()

    gp0, pe0 = P.GP.n, P.PE.n

    def gp_qs(g):
        _dma(g, P.GP, P.brow[0:1, 1536:1600], h["f0_qs"][li])
        _dma(g, P.GP, P.brow[0:1, 1600:1664], h["f0_qs_perm"][li])
        _dma(g, P.GP, P.brow[0:1, 1664:1728], h["f0_ks"][li])
        _dma(g, P.GP, P.brow[0:1, 1728:1792], h["f0_ks_perm"][li])
        g.wait_ge(P.GP.h, gp0 + 64)
        for i, k in enumerate(["rot_cs_q", "rot_ss_q", "rot_cs_k",
                               "rot_ss_k"]):
            src = h[k].rearrange("(tt p) d -> p tt d", p=128)
            dst = P.rtab_b[:, i * NTT * 64:(i + 1) * NTT * 64].rearrange(
                "p (tt d) -> p tt d", d=64)
            _dma(g, P.GP, dst, src)

    def t_qs(t):
        t.wait_ge(P.GP.h, gp0 + 64)
        for i in range(4):
            mm = t.matmul(P.psA[:, i * 512: i * 512 + 64],
                          P.ones1[0:1, 0:128],
                          P.brow[0:1, 1536 + i * 64: 1600 + i * 64],
                          start=True, stop=True)
            P.PE.inc(mm)

    def v_qs(v):
        v.wait_ge(P.PE.h, pe0 + 4)
        for i in range(4):
            v.tensor_copy(qk_sc[:, i * 64:(i + 1) * 64],
                          P.psA[:, i * 512: i * 512 + 64])
        P.DV.inc(v.drain())
        v.wait_ge(P.GP.h, gp0 + 8 * 16)
        v.tensor_copy(qk_scb[:], qk_sc[:])
        v.drain()
        for i in range(4):
            tab = P.rtab_b[:, i * NTT * 64:(i + 1) * NTT * 64].rearrange(
                "p (t d) -> p t d", d=64)
            scb = qk_scb[:, i * 64:(i + 1) * 64].unsqueeze(1).broadcast_to(
                [128, NTT, 64])
            m = v.tensor_tensor(tab, tab, scb, ALU.mult)
        P.DV.inc(m)

    P.phase("f0_qscales", gpsimd=gp_qs, tensor=t_qs, vector=v_qs)

    for h0 in (0, 8):
        carver.push()
        qf = carver.take(4 * 2048, BF16)
        kf = carver.take(4 * 2048, BF16)
        vf = carver.take(NTT * 512, BF16)
        carver.push()
        wsl = carver.take(4096, BF16)
        qb16 = carver.take(2 * 512, BF16)
        emit_f0_qkv_half(P, h, li, h0, xfm, qk_sc, qf, kf, vf, wsl, qb16)
        carver.pop()
        emit_f0_attn_half(P, h, li, h0, qf, kf, vf, carver)
        carver.pop()
    emit_f0_ffn(P, h, li, xfm, carver)
    carver.pop()


def emit_l0_ffn(P, carver, h):
    xsave = P.nc.dram_tensor("f0_xsave", [S, D], F32).ap()

    def save(s):
        dst = xsave.rearrange("(tt p) c -> p tt c", p=128)
        srcv = P.xres[:].rearrange("p (tt c) -> p tt c", c=D)
        _dma(s, P.ST, dst, srcv)
        s.wait_ge(P.ST.h, P.ST.n)

    P.phase("f0_xsave", sync=save)
    for li in range(L0F_DEPTH):
        emit_l0_ffn_layer(P, carver, h, li)

    def addback2(g):
        srcv = xsave.rearrange("(tt p) c -> p tt c", p=128)
        dst = P.xres[:].rearrange("p (tt c) -> p tt c", c=D)
        P.GP.inc(g.dma_start(dst, srcv, accum_op=ALU.add), 16)
        g.wait_ge(P.GP.h, P.GP.n)

    P.phase("f0_addback", gpsimd=addback2)
    emit_postln(P)





def emit_l1_attn(P, carver, h):
    carver.push()
    xfm = carver.take(8 * XFM_STRIDE, BF16)
    kfm = carver.take(8 * 2048, BF16)
    gh = carver.take(2048, BF16)

    def src_plain(v, tt, slot):
        return v.tensor_copy(slot, P.xres[:, tt * D:(tt + 1) * D])

    build_xfm(P, xfm, src_plain)

    gp0, pe0, ac0 = P.GP.n, P.PE.n, P.AC.n

    wt0 = [P.WT[0].n, P.WT[1].n]

    def gp_loadw(g):
        for m in range(8):
            src = h["a1_WKT"][:, m * 128:(m + 1) * 128].rearrange(
                "(cc p) o -> p cc o", p=128)
            dst = P.wbuf[:, (m % 2) * 3072:(m % 2) * 3072 + 1024].rearrange(
                "p (cc o) -> p cc o", o=128)
            if m >= 2:
                g.wait_ge(P.PE.h, pe0 + 4 * (m - 1))
            _dma(g, P.WT[m % 2], dst, src)

    def tensor(t):
        for m in range(8):
            t.wait_ge(P.WT[m % 2].h, wt0[m % 2] + 16 * (m // 2 + 1))
            wsl = P.wbuf[:, (m % 2) * 3072:(m % 2) * 3072 + 1024]
            for tg in range(4):
                k = m * 4 + tg
                bank = P.psA[:, (k % 2) * 512:(k % 2) * 512 + 512]
                if k >= 2:
                    t.wait_ge(P.AC.h, ac0 + k - 1)
                for cc in range(8):
                    mm = t.matmul(bank, wsl[:, cc * 128:(cc + 1) * 128],
                                  xfm[:, cc * XFM_STRIDE + 1 + tg * 512:
                                      cc * XFM_STRIDE + 1 + (tg + 1) * 512],
                                  start=(cc == 0), stop=(cc == 7))
                P.PE.inc(mm)

    def scalar(a):
        for m in range(8):
            for tg in range(4):
                k = m * 4 + tg
                a.wait_ge(P.PE.h, pe0 + k + 1)
                bank = P.psA[:, (k % 2) * 512:(k % 2) * 512 + 512]
                P.AC.inc(a.copy(
                    kfm[:, m * 2048 + tg * 512: m * 2048 + (tg + 1) * 512], bank))

    P.phase("l1_kfm", gpsimd=gp_loadw, tensor=tensor, scalar=scalar)

    gp1, pe1, ac1 = P.GP.n, P.PE.n, P.AC.n

    def gp2(g):
        src = h["a1_W1T"].rearrange("(cc p) d -> p cc d", p=128)
        dst = P.wbuf[:, 0:512].rearrange("p (cc d) -> p cc d", d=64)
        _dma(g, P.GP, dst, src)

    def tensor2(t):
        t.wait_ge(P.GP.h, gp1 + 16)
        for tg in range(4):
            bank = P.psA[:, 1024 + (tg % 2) * 512: 1024 + (tg % 2) * 512 + 512]
            if tg >= 2:
                t.wait_ge(P.AC.h, ac1 + tg - 1)
            for cc in range(8):
                mm = t.matmul(bank[0:64, :], P.wbuf[:, cc * 64:(cc + 1) * 64],
                              kfm[:, cc * 2048 + tg * 512:
                                  cc * 2048 + (tg + 1) * 512],
                              start=(cc == 0), stop=(cc == 7))
            P.PE.inc(mm)

    def scalar2(a):
        for tg in range(4):
            a.wait_ge(P.PE.h, pe1 + tg + 1)
            bank = P.psA[:, 1024 + (tg % 2) * 512: 1024 + (tg % 2) * 512 + 512]
            P.AC.inc(a.activation(gh[0:64, tg * 512:(tg + 1) * 512],
                                  bank[0:64, :], AF.Gelu))

    P.phase("l1_hidden", gpsimd=gp2, tensor=tensor2, scalar=scalar2)

    gp2_, pe2, dv2 = P.GP.n, P.PE.n, P.DV.n

    def gp3(g):
        _dma(g, P.GP, P.wbuf[0:64, 0:1024], h["a1_W2T"])

    def tensor3(t):
        t.wait_ge(P.GP.h, gp2_ + 16)
        for tt in range(NTT):
            for oc in range(2):
                k = tt * 2 + oc
                bank = P.psA[:, (k % 4) * 512:(k % 4) * 512 + 512]
                if k >= 4:
                    t.wait_ge(P.DV.h, dv2 + k - 3)
                mm = t.matmul(bank, gh[0:64, tt * 128:(tt + 1) * 128],
                              P.wbuf[0:64, oc * 512:(oc + 1) * 512],
                              start=True, stop=True)
                P.PE.inc(mm)

    def vector3(v):
        for tt in range(NTT):
            for oc in range(2):
                k = tt * 2 + oc
                v.wait_ge(P.PE.h, pe2 + k + 1)
                bank = P.psA[:, (k % 4) * 512:(k % 4) * 512 + 512]
                xs = P.xres[:, tt * D + oc * 512: tt * D + (oc + 1) * 512]
                P.DV.inc(v.tensor_add(xs, xs, bank))

    P.phase("l1_reads", gpsimd=gp3, tensor=tensor3, vector=vector3)

    carver.pop()
    emit_postln(P)




def emit_l1_ffn(P, carver, h):
    carver.push()
    xfm = carver.take(8 * XFM_STRIDE, BF16)
    hw_tm = carver.take(NTT * 128, BF16)
    hwfm = carver.take(2048, BF16)
    gsc = carver.take(32, F32)

    def src_plain(v, tt, slot):
        return v.tensor_copy(slot, P.xres[:, tt * D:(tt + 1) * D])

    build_xfm(P, xfm, src_plain)

    gp0, pe0, dv0, ac0 = P.GP.n, P.PE.n, P.DV.n, P.AC.n

    def gp(g):
        _dma(g, P.GP, P.wbuf[:, 0:64].rearrange("p (cc e) -> p cc e", e=8),
             h["f1_Wg"].rearrange("(cc p) e -> p cc e", p=128))
        _dma(g, P.GP, P.wbuf[:, 1024:2048].rearrange("p (cc o) -> p cc o", o=128),
             h["f1_A"].rearrange("(cc p) o -> p cc o", p=128))

    def tensor(t):
        t.wait_ge(P.GP.h, gp0 + 32)
        for tt in range(NTT):
            lbank = P.psA[:, (tt % 2) * 512:(tt % 2) * 512 + 8]
            hbank = P.psA[:, 1024 + (tt % 2) * 512: 1024 + (tt % 2) * 512 + 128]
            if tt >= 2:
                t.wait_ge(P.DV.h, dv0 + 2 * (tt - 2) + 2)
            for cc in range(8):
                xsl = xfm[:, cc * XFM_STRIDE + 1 + tt * 128:
                          cc * XFM_STRIDE + 1 + (tt + 1) * 128]
                t.matmul(lbank, xsl, P.wbuf[:, cc * 8:(cc + 1) * 8],
                         start=(cc == 0), stop=(cc == 7))
            for cc in range(8):
                xsl = xfm[:, cc * XFM_STRIDE + 1 + tt * 128:
                          cc * XFM_STRIDE + 1 + (tt + 1) * 128]
                mm = t.matmul(hbank, xsl,
                              P.wbuf[:, 1024 + cc * 128: 1024 + (cc + 1) * 128],
                              start=(cc == 0), stop=(cc == 7))
            P.PE.inc(mm)
        for tt in range(NTT):
            t.wait_ge(P.DV.h, dv0 + 2 * tt + 2)
            t.wait_ge(P.AC.h, ac0 + NTT + tt)
            m = t.transpose(P.psT[:, tt * 128:(tt + 1) * 128],
                            hw_tm[:, tt * 128:(tt + 1) * 128], P.identb[:])
            P.PE.inc(m)

    def vector(v):
        L = gsc[:, 0:8]
        m1 = gsc[:, 8:9]
        m2 = gsc[:, 9:10]
        wk = gsc[:, 10:18]
        E = gsc[:, 18:26]
        z = gsc[:, 26:27]
        nm1 = gsc[:, 27:28]
        cwt = gsc[:, 28:29]
        for tt in range(NTT):
            v.wait_ge(P.PE.h, pe0 + tt + 1)
            lbank = P.psA[:, (tt % 2) * 512:(tt % 2) * 512 + 8]
            hbank = P.psA[:, 1024 + (tt % 2) * 512: 1024 + (tt % 2) * 512 + 128]
            v.tensor_copy(L, lbank)
            v.drain()
            v.reduce_max(m1, L, AX.X)
            v.drain()
            v.tensor_single_scalar(wk, L, m1, ALU.is_ge)
            v.drain()
            v.scalar_tensor_tensor(wk, wk, NEG, L, ALU.mult, ALU.add)
            v.drain()
            v.reduce_max(m2, wk, AX.X)
            v.drain()
            v.tensor_scalar_mul(nm1, m1, -1.0)
            P.DV.inc(v.drain())
            v.wait_ge(P.AC.h, ac0 + tt + 1)
            v.tensor_single_scalar(wk, L, m2, ALU.is_ge)
            v.drain()
            v.tensor_mul(E, E, wk)
            v.drain()
            v.reduce_sum(z, E, AX.X)
            v.drain()
            v.reciprocal(z, z)
            v.drain()
            v.tensor_scalar_mul(E, E, z)
            v.drain()
            cwb = E.unsqueeze(2).broadcast_to([128, 8, 16])
            m = v.tensor_mul(
                hw_tm[:, tt * 128:(tt + 1) * 128].rearrange("p (e r) -> p e r", e=8),
                hbank.rearrange("p (e r) -> p e r", e=8), cwb)
            P.DV.inc(m)

    def scalar(a):
        L = gsc[:, 0:8]
        E = gsc[:, 18:26]
        nm1 = gsc[:, 27:28]
        for tt in range(NTT):
            a.wait_ge(P.DV.h, dv0 + 2 * tt + 1)
            a.activation(E, L, AF.Exp, bias=nm1, scale=1.0)
            P.AC.inc(a.drain())
        for tt in range(NTT):
            a.wait_ge(P.PE.h, pe0 + NTT + tt + 1)
            P.AC.inc(a.copy(hwfm[:, tt * 128:(tt + 1) * 128],
                            P.psT[:, tt * 128:(tt + 1) * 128]))

    P.phase("l1_gate", gpsimd=gp, tensor=tensor, vector=vector, scalar=scalar)

    gp1, pe1, dv1 = P.GP.n, P.PE.n, P.DV.n

    def gp2(g):
        _dma(g, P.GP, P.wbuf[:, 0:1024], h["f1_B"])

    def tensor2(t):
        t.wait_ge(P.GP.h, gp1 + 16)
        for tt in range(NTT):
            for oc in range(2):
                k = tt * 2 + oc
                bank = P.psA[:, (k % 4) * 512:(k % 4) * 512 + 512]
                if k >= 4:
                    t.wait_ge(P.DV.h, dv1 + k - 3)
                mm = t.matmul(bank, hwfm[:, tt * 128:(tt + 1) * 128],
                              P.wbuf[:, oc * 512:(oc + 1) * 512],
                              start=True, stop=True)
                P.PE.inc(mm)

    def vector2(v):
        for tt in range(NTT):
            for oc in range(2):
                k = tt * 2 + oc
                v.wait_ge(P.PE.h, pe1 + k + 1)
                bank = P.psA[:, (k % 4) * 512:(k % 4) * 512 + 512]
                xs = P.xres[:, tt * D + oc * 512: tt * D + (oc + 1) * 512]
                P.DV.inc(v.tensor_add(xs, xs, bank))

    P.phase("l1_out", gpsimd=gp2, tensor=tensor2, vector=vector2)

    carver.pop()
    emit_postln(P)


BLOCK_EMITTERS = {}


def _declare_hbm(nc, h, keys):
    hbm = {}
    for k in keys:
        a = h[k]
        hbm[k] = nc.dram_tensor(k, list(a.shape), F32, kind="ExternalInput").ap()
    return hbm


L1_KEYS = ["a1_WKT", "a1_W1T", "a1_W2T", "f1_Wg", "f1_A", "f1_B"]
L0F_KEYS = ["f0_Wqkv", "f0_Wo", "f0_W1", "f0_W2", "f0_qs", "f0_qs_perm",
            "f0_ks", "f0_ks_perm", "f0_ln_g", "f0_ln_b", "f0_b1", "f0_b2"]
A0_KEYS = ["a0_swT", "a0_scalerT", "a0_bwT", "a0_grid", "a0_RT2", "a0_owT",
           "a0_out_b"]
CONST_KEYS = ["ident", "ident64_2", "mask_even", "mask_odd", "mask_tri",
              "rot_cs_q", "rot_ss_q", "rot_cs_k", "rot_ss_k"]


def build_program(h, blocks):
    nc = bass.Bass("TRN2", target_bir_lowering=False, debug=False)
    keys = set(CONST_KEYS)
    if "l1_attn" in blocks:
        keys.update(["a1_WKT", "a1_W1T", "a1_W2T"])
    if "l1_ffn" in blocks:
        keys.update(["f1_Wg", "f1_A", "f1_B"])
    if any(b.startswith("l0_ffn") for b in blocks):
        keys.update(L0F_KEYS)
    if "l0_attn" in blocks:
        keys.update(A0_KEYS)
    hbm = _declare_hbm(nc, h, sorted(keys))
    hbm["x"] = nc.dram_tensor("x", [S, D], F32, kind="ExternalInput").ap()
    hbm["out"] = nc.dram_tensor("out", [S, D], F32, kind="ExternalOutput").ap()
    with contextlib.ExitStack() as stack:
        P = Prog(nc, stack)
        if any(b.startswith("l0_ffn") for b in blocks):
            P.shift_scr = nc.dram_tensor("shift_scr", [S + 1, 512], BF16).ap()
        carver = Carver(P.pool_t[:], P.pool_bytes)
        emit_load_consts(P, hbm)
        emit_load_x(P, hbm["x"])
        need_rtab = any(b.startswith("l0_ffn") for b in blocks)
        if need_rtab:
            P.rtab_b = carver.take(4 * NTT * 64, BF16)
            emit_load_rtab(P, hbm)
        for b in blocks:
            if b == "l0_attn":
                emit_l0_attn(P, carver, hbm)
            elif b == "l1_attn":
                emit_l1_attn(P, carver, hbm)
            elif b == "l1_ffn":
                emit_l1_ffn(P, carver, hbm)
            elif b == "l0_ffn":
                emit_l0_ffn(P, carver, hbm)
            elif b.startswith("l0_ffn_layer"):
                emit_l0_ffn_layer(P, carver, hbm, int(b.rsplit("_", 1)[1]))
            else:
                raise ValueError(b)
        emit_store_out(P, hbm["out"])
    return nc, sorted(keys)


def run_blocks(h, x_full, blocks):
    nc, keys = build_program(h, blocks)
    in_maps = []
    for c in range(8):
        m = {k: h[k] for k in keys}
        m["x"] = np.ascontiguousarray(x_full[c])
        in_maps.append(m)
    res = run_bass_kernel_spmd(nc, in_maps, core_ids=list(range(8)))
    return np.stack([r["out"] for r in res.results], axis=0)


def kernel(**inputs):
    h = _host_prep(inputs)
    out = run_blocks(h, h["x"], ["l0_attn", "l0_ffn", "l1_attn", "l1_ffn"])
    return np.ascontiguousarray(out.astype(np.float32))


# version 16 (final)
# speedup vs baseline: 1.1711x; 1.0011x over previous
import contextlib

import numpy as np
import ml_dtypes

import concourse.bass as bass
import concourse.mybir as mybir
from concourse.bass_utils import run_bass_kernel_spmd

F32 = mybir.dt.float32
BF16 = mybir.dt.bfloat16
AF = mybir.ActivationFunctionType
ALU = mybir.AluOpType
AX = mybir.AxisListType

S = 2048
D = 1024
NTT = 16
NCC = 8
NEG = -1.0e30
XFM_STRIDE = 2064

L0F_DEPTH = 6
L0F_HD = 64
L0F_WINDOW = 256
L0F_DI = 2730
L0F_QKSCALE = 8.0


def _host_constants():
    c = {}
    c["ident"] = np.eye(128, dtype=np.float32)
    c["ident64_2"] = np.concatenate([np.eye(64), np.eye(64)], 0).astype(
        np.float32)
    i = np.arange(128)[:, None]
    j = np.arange(512)[None, :]
    c["mask_even"] = np.where((j >= i) & (j <= i + 256), 0.0, NEG).astype(np.float32)
    c["mask_odd"] = np.where((j >= i + 128) & (j <= i + 384), 0.0, NEG).astype(
        np.float32)
    j1 = np.arange(128)[None, :]
    c["mask_tri"] = np.where(j1 <= i, 0.0, NEG).astype(np.float32)
    half = np.arange(0, L0F_HD, 2, dtype=np.float64)
    inv_freq = 1.0 / 10000.0 ** (half / L0F_HD)
    t = np.arange(S, dtype=np.float64)
    freqs = t[:, None] * inv_freq[None, :]
    freqs = np.concatenate([freqs, freqs], -1)
    base_scale = (half + 0.4 * L0F_HD) / (1.4 * L0F_HD)
    power = (t - S // 2) / (L0F_WINDOW // 2)
    scale = base_scale[None, :] ** power[:, None]
    scale = np.concatenate([scale, scale], -1)
    cos, sin = np.cos(freqs), np.sin(freqs)
    eps = np.where(np.arange(L0F_HD) < 32, -1.0, 1.0)[None, :]
    c["rot_cs_q"] = (L0F_QKSCALE * cos * scale).astype(np.float32)
    c["rot_ss_q"] = (L0F_QKSCALE * eps * sin * scale).astype(np.float32)
    c["rot_cs_k"] = (cos / scale).astype(np.float32)
    c["rot_ss_k"] = (eps * sin / scale).astype(np.float32)
    return c


def _host_prep(inputs):
    p = {}
    p["x"] = np.ascontiguousarray(inputs["L0_attn_x"])
    p["a0_swT"] = np.ascontiguousarray(
        np.transpose(inputs["L0_attn_spline_weight"], (2, 1, 0)))
    p["a0_scalerT"] = np.ascontiguousarray(inputs["L0_attn_spline_scaler"].T)
    p["a0_bwT"] = np.ascontiguousarray(inputs["L0_attn_base_weight"].T)
    p["a0_grid"] = np.ascontiguousarray(inputs["L0_attn_grid"])
    rt = inputs["L0_attn_rotation_matrix"].T
    p["a0_RT2"] = np.ascontiguousarray(np.concatenate([rt, rt], 0))
    p["a0_owT"] = np.ascontiguousarray(inputs["L0_attn_out_w"].T)
    p["a0_out_b"] = np.ascontiguousarray(inputs["L0_attn_out_b"][None, :])
    p["f0_Wqkv"] = np.ascontiguousarray(inputs["L0_ffn_Wqkv"])
    p["f0_Wo"] = np.ascontiguousarray(inputs["L0_ffn_Wo"])
    p["f0_W1"] = np.ascontiguousarray(inputs["L0_ffn_W1"])
    p["f0_W2"] = np.ascontiguousarray(inputs["L0_ffn_W2"])
    qs = inputs["L0_ffn_q_scale"]
    ks = inputs["L0_ffn_k_scale"]
    perm = (np.arange(L0F_HD) + 32) % L0F_HD
    p["f0_qs"] = np.ascontiguousarray(qs[:, None, :])
    p["f0_qs_perm"] = np.ascontiguousarray(qs[:, perm][:, None, :])
    p["f0_ks"] = np.ascontiguousarray(ks[:, None, :])
    p["f0_ks_perm"] = np.ascontiguousarray(ks[:, perm][:, None, :])
    p["f0_ln_g"] = np.ascontiguousarray(inputs["L0_ffn_ln_g"][:, :, None])
    p["f0_ln_b"] = np.ascontiguousarray(inputs["L0_ffn_ln_b"][:, :, None])
    p["f0_b1"] = np.ascontiguousarray(inputs["L0_ffn_b1"][:, None, :])
    p["f0_b2"] = np.ascontiguousarray(inputs["L0_ffn_b2"][:, None, :])
    p["a1_WKT"] = np.ascontiguousarray(inputs["L1_attn_W_K"].T)
    p["a1_W1T"] = np.ascontiguousarray(inputs["L1_attn_W1"].T)
    p["a1_W2T"] = np.ascontiguousarray(inputs["L1_attn_W2"].T)
    p["f1_Wg"] = np.ascontiguousarray(inputs["L1_ffn_Wg"])
    p["f1_A"] = np.ascontiguousarray(
        np.transpose(inputs["L1_ffn_A"], (1, 0, 2)).reshape(D, 128))
    p["f1_B"] = np.ascontiguousarray(inputs["L1_ffn_B"].reshape(128, D))
    p.update(_host_constants())
    return p


class Sem:
    def __init__(self, stack, nc, name):
        self.h = stack.enter_context(nc.semaphore(name))
        self.n = 0

    def inc(self, instr, amount=1):
        instr.then_inc(self.h, amount)
        self.n += amount
        return self.n


class Carver:

    def __init__(self, pool_ap, nbytes):
        self.pool = pool_ap
        self.nbytes = nbytes
        self.off = 0
        self.marks = []

    def take(self, nelem, dtype):
        esz = 4 if dtype == F32 else 2
        self.off = (self.off + 63) & ~63
        start = self.off
        self.off += nelem * esz
        assert self.off <= self.nbytes, f"pool overflow: {self.off}>{self.nbytes}"
        if dtype == BF16:
            return self.pool[:, start // 2: start // 2 + nelem]
        return self.pool.bitcast(F32)[:, start // 4: start // 4 + nelem]

    def push(self):
        self.marks.append(self.off)

    def pop(self):
        self.off = self.marks.pop()


class Prog:

    def __init__(self, nc, stack):
        self.nc = nc
        self.stack = stack
        es = stack.enter_context
        self.LD = Sem(stack, nc, "s_ld")
        self.GP = Sem(stack, nc, "s_gp")
        self.ST = Sem(stack, nc, "s_st")
        self.PE = Sem(stack, nc, "s_pe")
        self.DV = Sem(stack, nc, "s_dv")
        self.AC = Sem(stack, nc, "s_ac")
        self.PL = Sem(stack, nc, "s_pl")
        self.WT = [Sem(stack, nc, "s_wt0"), Sem(stack, nc, "s_wt1")]
        self.ST2 = [Sem(stack, nc, "s_sp0"), Sem(stack, nc, "s_sp1")]
        self.xres = es(nc.sbuf_tensor("xres", [128, NTT * D], F32))
        self.pool_bytes = 106 * 1024
        self.pool_t = es(nc.sbuf_tensor("pool", [128, self.pool_bytes // 2], BF16))
        self.wbuf = es(nc.sbuf_tensor("wbuf", [128, 4096], BF16))
        self.scr = es(nc.sbuf_tensor("scr", [128, 2560], F32))
        self.stat = es(nc.sbuf_tensor("stat", [128, 256], F32))
        self.identb = es(nc.sbuf_tensor("identb", [128, 128], BF16))
        self.identf = es(nc.sbuf_tensor("identf", [128, 128], F32))
        self.rt2b = es(nc.sbuf_tensor("rt2b", [128, 64], BF16))
        self.id642 = es(nc.sbuf_tensor("id642", [128, 64], BF16))
        self.brow = es(nc.sbuf_tensor("brow", [128, 2048], BF16))
        self.ones1 = es(nc.sbuf_tensor("ones1", [128, 512], BF16))
        self.masks = es(nc.sbuf_tensor("masks", [128, 512 * 2 + 128], BF16))
        self.rtab = None
        self.shift_scr = None
        self.psA = es(nc.psum_tensor("psA", [128, 3072], F32))
        self.psT = es(nc.psum_tensor("psT", [128, 2048], BF16))
        self.mu = self.stat[:, 0:NTT]
        self.rs = self.stat[:, NTT: 2 * NTT]
        self.tmp16 = self.stat[:, 32:48]
        self.eps = self.stat[:, 248:249]
        self.eps24 = self.stat[:, 249:250]

    def phase(self, name=None, **bodies):
        with self.nc.Block() as blk:
            for eng, fn in bodies.items():
                getattr(blk, eng)(fn)


def _dma(q, sem, out, in_, **kw):
    sem.inc(q.dma_start(out, in_, **kw), 16)




def emit_load_x(P, x_h):
    def sync(s):
        src = x_h.rearrange("(tt p) c -> p tt c", p=128)
        dst = P.xres[:].rearrange("p (tt c) -> p tt c", c=D)
        _dma(s, P.LD, dst, src)
        s.wait_ge(P.LD.h, P.LD.n)

    P.phase("load_x", sync=sync)


def emit_store_out(P, out_h):
    def sync(s):
        dst = out_h.rearrange("(tt p) c -> p tt c", p=128)
        src = P.xres[:].rearrange("p (tt c) -> p tt c", c=D)
        _dma(s, P.ST, dst, src)
        s.wait_ge(P.ST.h, P.ST.n)

    P.phase("store_out", sync=sync)


def emit_load_consts(P, h):
    def sync(s):
        _dma(s, P.LD, P.identf[:], h["ident"])
        s.wait_ge(P.LD.h, P.LD.n)

    def gpsimd(g):
        _dma(g, P.GP, P.identb[:], h["ident"])
        _dma(g, P.GP, P.id642[:], h["ident64_2"])
        _dma(g, P.GP, P.masks[:, 0:512], h["mask_even"])
        _dma(g, P.GP, P.masks[:, 512:1024], h["mask_odd"])
        _dma(g, P.GP, P.masks[:, 1024:1152], h["mask_tri"])
        g.wait_ge(P.GP.h, P.GP.n)
        g.memset(P.ones1[:], 1.0)
        g.memset(P.brow[96:97, 0:512], 0.0)
        g.drain()
        if P.shift_scr is not None:
            _dma(g, P.GP, P.shift_scr[0:1, :], P.brow[96:97, 0:512])
            g.wait_ge(P.GP.h, P.GP.n)

    def vector(v):
        v.memset(P.eps, 1e-5)
        v.memset(P.eps24, 1e-24)
        v.memset(P.stat[:, 96:100], 1.0)
        P.DV.inc(v.drain())

    def scalar(a):
        a.wait_ge(P.DV.h, P.DV.n)
        j = P.stat[:, 96:100]
        for fi, f in enumerate((AF.Exp, AF.Sqrt, AF.Gelu, AF.Silu, AF.Square,
                                AF.Sign)):
            a.activation(P.stat[:, 100 + fi:101 + fi], j[:, 0:1], f)
        P.AC.inc(a.drain())

    P.phase("load_consts", sync=sync, gpsimd=gpsimd, vector=vector, scalar=scalar)


def emit_load_rtab(P, h):

    def gpsimd(g):
        for i, k in enumerate(["rot_cs_q", "rot_ss_q", "rot_cs_k", "rot_ss_k"]):
            src = h[k].rearrange("(tt p) d -> p tt d", p=128)
            dst = P.rtab_b[:, i * NTT * 64:(i + 1) * NTT * 64].rearrange(
                "p (tt d) -> p tt d", d=64)
            _dma(g, P.GP, dst, src)
        g.wait_ge(P.GP.h, P.GP.n)

    P.phase("load_rtab", gpsimd=gpsimd)


def build_xfm(P, xfm, src_fn, nchunk=8, evac_scale=None, gpsimd=None):
    ncast = 2
    W = nchunk * 128
    cast = P.scr[:, 0: (2 * W) // 2].bitcast(BF16)

    dv0, pe0, ac0 = P.DV.n, P.PE.n, P.AC.n

    def vector(v):
        for cc in range(nchunk):
            v.memset(xfm[:, cc * XFM_STRIDE: cc * XFM_STRIDE + 1], 0.0)
            v.memset(xfm[:, cc * XFM_STRIDE + 1 + S:(cc + 1) * XFM_STRIDE], 0.0)
        if gpsimd is not None:
            v.wait_ge(P.GP.h, P.GP.n)
        for tt in range(NTT):
            slot = cast[:, (tt % ncast) * W:(tt % ncast + 1) * W]
            if tt >= ncast:
                v.wait_ge(P.PE.h, pe0 + (tt - ncast + 1))
            m = src_fn(v, tt, slot)
            P.DV.inc(m)

    def tensor(t):
        for tt in range(NTT):
            slot = cast[:, (tt % ncast) * W:(tt % ncast + 1) * W]
            t.wait_ge(P.DV.h, dv0 + tt + 1)
            if tt >= 1:
                t.wait_ge(P.AC.h, ac0 + tt)
            for cc in range(nchunk):
                m = t.transpose(P.psT[:, cc * 128:(cc + 1) * 128],
                                slot[:, cc * 128:(cc + 1) * 128], P.identb[:])
            P.PE.inc(m)

    def scalar(a):
        for tt in range(NTT):
            a.wait_ge(P.PE.h, pe0 + tt + 1)
            for cc in range(nchunk):
                dst = xfm[:, cc * XFM_STRIDE + 1 + tt * 128:
                          cc * XFM_STRIDE + 1 + (tt + 1) * 128]
                srcp = P.psT[:, cc * 128:(cc + 1) * 128]
                if evac_scale is None:
                    m = a.copy(dst, srcp)
                else:
                    m = a.activation(dst, srcp, AF.Copy,
                                     scale=evac_scale(cc))
            P.AC.inc(m)

    if gpsimd is not None:
        P.phase("xfm", gpsimd=gpsimd, vector=vector, tensor=tensor,
                scalar=scalar)
    else:
        P.phase("xfm", vector=vector, tensor=tensor, scalar=scalar)


def emit_postln(P):
    mu, rs, t16, eps = P.mu, P.rs, P.tmp16, P.eps
    sq = P.scr[:, 0:2048]
    dv0, ac0 = P.DV.n, P.AC.n

    def scalar(a):
        for tt in range(NTT):
            if tt >= 2:
                a.wait_ge(P.DV.h, dv0 + tt - 1)
            m = a.activation(sq[:, (tt % 2) * 1024:(tt % 2 + 1) * 1024],
                             P.xres[:, tt * D:(tt + 1) * D], AF.Square)
            P.AC.inc(m)
        a.wait_ge(P.DV.h, dv0 + NTT + 1)
        a.activation(rs[:], rs[:], AF.Sqrt, bias=eps, scale=1.0)
        P.AC.inc(a.drain())

    def vector(v):
        for tt in range(NTT):
            xs = P.xres[:, tt * D:(tt + 1) * D]
            v.reduce_sum(mu[:, tt: tt + 1], xs, AX.X)
            v.wait_ge(P.AC.h, ac0 + tt + 1)
            m = v.reduce_sum(rs[:, tt: tt + 1],
                             sq[:, (tt % 2) * 1024:(tt % 2 + 1) * 1024], AX.X)
            P.DV.inc(m)
        v.drain()
        v.tensor_scalar_mul(mu[:], mu[:], 1.0 / D)
        v.tensor_scalar_mul(rs[:], rs[:], 1.0 / D)
        v.drain()
        v.tensor_mul(t16[:], mu[:], mu[:])
        v.drain()
        v.tensor_sub(rs[:], rs[:], t16[:])
        P.DV.inc(v.drain())
        v.wait_ge(P.AC.h, ac0 + NTT + 1)
        v.reciprocal(rs[:], rs[:])
        v.drain()
        for tt in range(NTT):
            xs = P.xres[:, tt * D:(tt + 1) * D]
            v.tensor_scalar(xs, xs, mu[:, tt: tt + 1], rs[:, tt: tt + 1],
                            ALU.subtract, ALU.mult)

    P.phase("postln", scalar=scalar, vector=vector)



A0_TG = 256
A0_NG = S // A0_TG


def emit_a0_prepass(P, h, carver, swsc):
    carver.push()
    stg = carver.take(2 * 3072, F32)
    sct = carver.take(3072, F32)
    ob = carver.take(2 * 3072, BF16)
    gp0 = P.GP.n
    wt0 = [P.WT[0].n, P.WT[1].n]
    st0 = [P.ST2[0].n, P.ST2[1].n]
    dv0, pl0 = P.DV.n, P.PL.n
    NK = 72

    def w_mult(eng, sem, k):
        if k >= 2:
            kk = k - 2
            eng.wait_ge(P.ST2[kk % 2].h, st0[kk % 2] + 16 * (kk // 2 + 1))
        eng.wait_ge(P.WT[k % 2].h, wt0[k % 2] + 16 * (k // 2 + 1))
        sl = slice((k % 2) * 3072, (k % 2) * 3072 + 3072)
        if k < 8:
            m = eng.tensor_copy(ob[:, sl], stg[:, sl])
        else:
            cc = (k - 8) // 8
            eng.wait_ge(P.GP.h, gp0 + 16 * (cc + 1))
            m = eng.tensor_tensor(ob[:, sl], stg[:, sl], sct[:], ALU.mult)
        sem.inc(m)

    def gp_all(g):
        for k in range(NK):
            if k >= 8 and (k - 8) % 8 == 0:
                cc = (k - 8) // 8
                if cc >= 1:
                    g.wait_ge(P.DV.h, dv0 + 4 * cc + 4)
                    g.wait_ge(P.PL.h, pl0 + 4 * cc + 4)
                _dma(g, P.GP, sct[:],
                     h["a0_scalerT"][cc * 128:(cc + 1) * 128, :])
            if k >= 2:
                kk = k - 2
                if kk % 2 == 0:
                    g.wait_ge(P.DV.h, dv0 + (kk // 2) + 1)
                else:
                    g.wait_ge(P.PL.h, pl0 + (kk // 2) + 1)
            if k < 8:
                src = h["a0_bwT"][k * 128:(k + 1) * 128, :]
            else:
                i = k - 8
                cc, s = i // 8, i % 8
                src = h["a0_swT"][s, cc * 128:(cc + 1) * 128, :]
            _dma(g, P.WT[k % 2],
                 stg[:, (k % 2) * 3072:(k % 2) * 3072 + 3072], src)
            if k % 2 == 1:
                w_mult(g, P.PL, k)

    def vector(v):
        for k in range(NK):
            if k % 2 == 0:
                w_mult(v, P.DV, k)

    def sync(sy):
        for k in range(NK):
            if k % 2 == 0:
                sy.wait_ge(P.DV.h, dv0 + (k // 2) + 1)
            else:
                sy.wait_ge(P.PL.h, pl0 + (k // 2) + 1)
            dst = swsc[k * 512:(k + 1) * 512, :].rearrange(
                "(f p) o -> p f o", p=128)
            src = ob[:, (k % 2) * 3072:(k % 2) * 3072 + 3072].rearrange(
                "p (f o) -> p f o", o=768)
            _dma(sy, P.ST2[k % 2], dst, src)
        sy.wait_ge(P.ST2[0].h, P.ST2[0].n)
        sy.wait_ge(P.ST2[1].h, P.ST2[1].n)

    P.phase("a0_prepass", gpsimd=gp_all, vector=vector, sync=sync)
    carver.pop()


KSTRIDE = 56


def emit_a0_knots(P, h, knots):
    gp0 = P.GP.n

    def gpsimd(g):
        for cc in range(8):
            _dma(g, P.GP, knots[:, cc * KSTRIDE: cc * KSTRIDE + 12],
                 h["a0_grid"][cc * 128:(cc + 1) * 128, :])
        _dma(g, P.GP, P.rt2b[:], h["a0_RT2"])
        _dma(g, P.GP, P.brow[0:1, 0:1024], h["a0_out_b"])
        g.wait_ge(P.GP.h, P.GP.n)

    def vector(v):
        v.wait_ge(P.GP.h, gp0 + 10 * 16)
        for cc in range(8):
            t = knots[:, cc * KSTRIDE: cc * KSTRIDE + 12]
            nt = knots[:, cc * KSTRIDE + 12: cc * KSTRIDE + 24]
            v.tensor_scalar_mul(nt, t, -1.0)
            v.drain()
            for p, off, n in ((1, 24, 11), (2, 35, 10), (3, 45, 9)):
                d = knots[:, cc * KSTRIDE + off: cc * KSTRIDE + off + n]
                v.tensor_sub(d, t[:, p:p + n], t[:, 0:n])
                v.drain()
                v.reciprocal(d, d)
                v.drain()
                if p == 1:
                    v.tensor_scalar_mul(d, d, 0.5)
                    v.drain()
        P.DV.inc(v.drain())

    P.phase("a0_knots", gpsimd=gpsimd, vector=vector)


def emit_a0_bs_group(P, h, g, knots, xg32, uw32, sgn32, bwork, bs, sx):
    pe0, dv0, ac0 = P.PE.n, P.DV.n, P.AC.n

    def tensor(t):
        for cc in range(8):
            if cc >= 2:
                t.wait_ge(P.DV.h, dv0 + (cc - 2) * 3 + 1)
            for kc in range(2):
                tt = 2 * g + kc
                m = t.transpose(
                    P.psA[:, 2048 + (cc % 2) * 512 + kc * 128:
                          2048 + (cc % 2) * 512 + kc * 128 + 128],
                    P.xres[:, tt * D + cc * 128: tt * D + (cc + 1) * 128],
                    P.identf[:])
            P.PE.inc(m)

    def vector(v):
        for cc in range(8):
            v.wait_ge(P.PE.h, pe0 + cc + 1)
            xs = xg32[:, (cc % 2) * 256:(cc % 2) * 256 + 256]
            if cc >= 2:
                v.wait_ge(P.AC.h, ac0 + 2 * (cc - 2) + 2)
            v.tensor_copy(xs[:, 0:128],
                          P.psA[:, 2048 + (cc % 2) * 512:
                                2048 + (cc % 2) * 512 + 128])
            m = v.tensor_copy(xs[:, 128:256],
                              P.psA[:, 2048 + (cc % 2) * 512 + 128:
                                    2048 + (cc % 2) * 512 + 256])
            P.DV.inc(m)
            v.wait_ge(P.AC.h, ac0 + 2 * cc + 1)
            kb = cc * KSTRIDE
            tcol = knots[:, kb: kb + 12]
            for j in range(11):
                m = v.tensor_sub(bwork[:, j * 256:(j + 1) * 256],
                                 sgn32[:, j * 256:(j + 1) * 256],
                                 sgn32[:, (j + 1) * 256:(j + 2) * 256])
            P.DV.inc(m)
            U = uw32[:, 0:256]
            W = uw32[:, 256:512]
            for p, off in ((1, 24), (2, 35), (3, 45)):
                for j in range(11 - p):
                    bj = bwork[:, j * 256:(j + 1) * 256]
                    bj1 = bwork[:, (j + 1) * 256:(j + 2) * 256]
                    r1 = knots[:, kb + off + j: kb + off + j + 1]
                    r2 = knots[:, kb + off + j + 1: kb + off + j + 2]
                    v.tensor_scalar(U, xs, tcol[:, j: j + 1], r1,
                                    ALU.subtract, ALU.mult)
                    v.tensor_scalar(W, xs, tcol[:, j + p + 1: j + p + 2], r2,
                                    ALU.subtract, ALU.mult)
                    v.tensor_mul(U, U, bj)
                    v.tensor_mul(W, W, bj1)
                    if p == 3:
                        m = v.tensor_sub(bs[:, (j * 8 + cc) * 256:
                                            (j * 8 + cc) * 256 + 256], U, W)
                    else:
                        m = v.tensor_sub(bj, U, W)
            P.DV.inc(m)

    def scalar(a):
        for cc in range(8):
            a.wait_ge(P.DV.h, dv0 + 3 * cc + 1)
            if cc >= 1:
                a.wait_ge(P.DV.h, dv0 + 3 * (cc - 1) + 2)
            xs = xg32[:, (cc % 2) * 256:(cc % 2) * 256 + 256]
            kb = cc * KSTRIDE
            for j in range(12):
                m = a.activation(sgn32[:, j * 256:(j + 1) * 256], xs, AF.Sign,
                                 bias=knots[:, kb + 12 + j: kb + 13 + j],
                                 scale=1.0)
            P.AC.inc(m)
            m = a.activation(sx[:, cc * 256:(cc + 1) * 256], xs, AF.Silu)
            P.AC.inc(m)
    P.phase(f"a0_bs_{g}", tensor=tensor, vector=vector, scalar=scalar)


def emit_a0_mm_group(P, h, g, bs, sx, wtl, stg6, swsc, qkv_scr):
    NL = 72
    NB = NL // 8
    wt0 = [P.WT[0].n, P.WT[1].n]
    st0 = [P.ST2[0].n, P.ST2[1].n]
    pe0, ac0 = P.PE.n, P.AC.n
    sw4 = swsc.rearrange("(i f p) o -> p i f o", f=4, p=128)

    def gpsimd(gq):
        for mg in range(4):
            for bb in range(NB):
                b = mg * NB + bb
                if b >= 2:
                    gq.wait_ge(P.PE.h, pe0 + (b - 2) + 1)
                dst = wtl[:, (b % 2) * 6144:(b % 2) * 6144 + 6144].rearrange(
                    "p (c o) -> p c o", o=768)
                _dma(gq, P.WT[b % 2], dst, sw4[:, 8 * bb:8 * bb + 8, mg])

    def tensor(t):
        for mg in range(4):
            for bb in range(NB):
                b = mg * NB + bb
                t.wait_ge(P.WT[b % 2].h, wt0[b % 2] + 16 * (b // 2 + 1))
                if bb == 0 and mg >= 1:
                    t.wait_ge(P.AC.h, ac0 + mg * 6)
                for sl in range(8):
                    l = bb * 8 + sl
                    if l < 8:
                        rhs = sx[:, l * 256:(l + 1) * 256]
                    else:
                        i = l - 8
                        slot = (i % 8) * 8 + i // 8
                        rhs = bs[:, slot * 256:(slot + 1) * 256]
                    for m in range(6):
                        mm = t.matmul(
                            P.psA[:, m * 512: m * 512 + 256],
                            wtl[:, (b % 2) * 6144 + sl * 768 + m * 128:
                                (b % 2) * 6144 + sl * 768 + (m + 1) * 128],
                            rhs, start=(l == 0), stop=(l == NL - 1))
                P.PE.inc(mm)

    def scalar(a):
        for mg in range(4):
            a.wait_ge(P.PE.h, pe0 + (mg + 1) * NB)
            if mg >= 1:
                a.wait_ge(P.ST2[0].h, st0[0] + 16 * 3 * mg)
                a.wait_ge(P.ST2[1].h, st0[1] + 16 * 3 * mg)
            for m in range(6):
                mm = a.copy(stg6[:, m * 256:(m + 1) * 256],
                            P.psA[:, m * 512: m * 512 + 256])
                P.AC.inc(mm)

    def sync(sy):
        for mg in range(4):
            sy.wait_ge(P.AC.h, ac0 + (mg + 1) * 6)
            for m in range(6):
                k = mg * 6 + m
                _dma(sy, P.ST2[k % 2],
                     qkv_scr[mg * 6 + m][:, g * 256:(g + 1) * 256],
                     stg6[:, m * 256:(m + 1) * 256])
        sy.wait_ge(P.ST2[0].h, P.ST2[0].n)
        sy.wait_ge(P.ST2[1].h, P.ST2[1].n)

    P.phase(f"a0_mm_{g}", gpsimd=gpsimd, tensor=tensor, scalar=scalar,
            sync=sync)


def emit_a0_reload_half(P, h, h0, qkv_scr, rbuf, qf, kf, vf):
    gp0, pe0, ac0 = P.GP.n, P.PE.n, P.AC.n
    chunks = list(range(h0 * 3 // 2, h0 * 3 // 2 + 12))
    nrot = []
    pe_plan = []
    for ci, m in enumerate(chunks):
        for half in range(2):
            b = 2 * m + half
            pe_plan.append((ci, m, b))

    def gpsimd(g):
        for ci, m in enumerate(chunks):
            if ci >= 2:
                t_inc = 2 * (ci - 1)
                g.wait_ge(P.PE.h, pe0 + t_inc)
            _dma(g, P.WT[ci % 2], rbuf[:, (ci % 2) * 2048:(ci % 2) * 2048 + 2048],
                 qkv_scr[m])

    wt0 = [P.WT[0].n, P.WT[1].n]

    def tensor(t):
        k_pe = 0
        for ci, m in enumerate(chunks):
            t.wait_ge(P.WT[ci % 2].h, wt0[ci % 2] + 16 * (ci // 2 + 1))
            for half in range(2):
                b = 2 * m + half
                hd, role = b // 3, b % 3
                rh = 64 * (b % 2)
                rho = 64 * (hd % 2)
                hl = hd - h0
                src = rbuf[rh:rh + 64, (ci % 2) * 2048:(ci % 2) * 2048 + 2048]
                k_pe += 1
                if role == 2:
                    if k_pe >= 2:
                        t.wait_ge(P.AC.h, ac0 + k_pe - 1)
                    for kc in range(16):
                        mm = t.transpose(
                            P.psT[:, kc * 64: kc * 64 + 64],
                            src[:, kc * 128:(kc + 1) * 128],
                            P.id642[rh:rh + 64, :], tile_position=(rh, 0))
                    P.PE.inc(mm)
                else:
                    if k_pe >= 2:
                        t.wait_ge(P.AC.h, ac0 + k_pe - 1)
                    for kb in range(4):
                        mm = t.matmul(
                            P.psA[rho:rho + 64, kb * 512:(kb + 1) * 512],
                            P.rt2b[rh:rh + 64, :],
                            src[:, kb * 512:(kb + 1) * 512],
                            start=True, stop=True, tile_position=(rh, rho))
                    P.PE.inc(mm)

    def scalar(a):
        k_pe = 0
        for ci, m in enumerate(chunks):
            for half in range(2):
                b = 2 * m + half
                hd, role = b // 3, b % 3
                rh = 64 * (b % 2)
                rho = 64 * (hd % 2)
                hl = hd - h0
                pr = hl // 2
                k_pe += 1
                a.wait_ge(P.PE.h, pe0 + k_pe)
                if role == 2:
                    for kc in range(16):
                        mm = a.copy(vf[:, kc * 512 + hl * 64:
                                       kc * 512 + hl * 64 + 64],
                                    P.psT[:, kc * 64: kc * 64 + 64])
                else:
                    arena = qf if role == 0 else kf
                    for kb in range(4):
                        dst = arena[rho:rho + 64, pr * 2048 + kb * 512:
                                    pr * 2048 + (kb + 1) * 512]
                        srcp = P.psA[rho:rho + 64, kb * 512:(kb + 1) * 512]
                        if role == 0:
                            mm = a.activation(dst, srcp, AF.Copy, scale=0.125)
                        else:
                            mm = a.copy(dst, srcp)
                P.AC.inc(mm)

    P.phase(f"a0_reload_{h0}", gpsimd=gpsimd, tensor=tensor, scalar=scalar)


def emit_a0_attn_half(P, h, h0, qf, kf, vf, carver):
    carver.push()
    probs = carver.take(2 * 2048, BF16)
    attnT = carver.take(2048, BF16)
    ctxq = carver.take(2 * 512, BF16)
    nm1 = [P.stat[:, 48:49], P.stat[:, 49:50]]
    zsum = [P.stat[:, 50:51], P.stat[:, 51:52]]
    rcW = P.stat[:, 64:128]
    NSH = 8 * NTT
    CB = 2048
    WB = 2560
    gp0 = P.GP.n
    pe0, dv0, ac0 = P.PE.n, P.DV.n, P.AC.n

    def gp(g):
        src = h["a0_owT"][h0 * 64: h0 * 64 + 512, :].rearrange(
            "(cc p) o -> p cc o", p=128)
        dst = P.wbuf[:, 0:4096].rearrange("p (cc o) -> p cc o", o=1024)
        _dma(g, P.GP, dst, src)

    order = {"pe": [], "dv": [], "ac": []}
    for i in range(NSH + 7):
        if 0 <= i - 3 < NSH:
            order["pe"].append(("B", i - 3))
        if 0 <= i - 4 < NSH:
            order["pe"].append(("C", i - 4))
        if 0 <= i - 6 < NSH and (i - 6) % 8 == 7:
            order["pe"].append(("D1", (i - 6) // 8))
        if i < NSH:
            order["pe"].append(("A", i))
        if 0 <= i - 5 < NSH and (i - 5) % 8 == 7:
            order["pe"].append(("D0", (i - 5) // 8))
        if 0 <= i - 1 < NSH:
            order["dv"].append(("minn", i - 1))
        if 0 <= i - 2 < NSH:
            order["dv"].append(("recip", i - 2))
            order["dv"].append(("mult", i - 2))
        if 0 <= i - 5 < NSH and (i - 5) % 8 == 7:
            order["dv"].append(("eo0", (i - 5) // 8))
        if 0 <= i - 6 < NSH and (i - 6) % 8 == 7:
            order["dv"].append(("eo1", (i - 6) // 8))
        if 0 <= i - 1 < NSH:
            order["ac"].append(("exp", i - 1))
        if 0 <= i - 3 < NSH:
            order["ac"].append(("aev", i - 3))
        if 0 <= i - 4 < NSH:
            order["ac"].append(("cev", i - 4))
    pe_i = {k: n + 1 for n, k in enumerate(order["pe"])}
    dv_i = {k: n + 1 for n, k in enumerate(order["dv"])}
    ac_i = {k: n + 1 for n, k in enumerate(order["ac"])}

    def tensor(t):
        for tag, s in order["pe"]:
            if tag == "A":
                qt, hl = s // 8, s % 8
                rh, pr = 64 * (hl % 2), hl // 2
                if s >= 1:
                    t.wait_ge(P.AC.h, ac0 + ac_i[("exp", s - 1)])
                for kb in range(4):
                    mm = t.matmul(
                        P.psA[:, kb * 512:(kb + 1) * 512],
                        qf[rh:rh + 64, pr * 2048 + qt * 128:
                           pr * 2048 + (qt + 1) * 128],
                        kf[rh:rh + 64, pr * 2048 + kb * 512:
                           pr * 2048 + (kb + 1) * 512],
                        start=True, stop=True, tile_position=(rh, 0))
                P.PE.inc(mm)
            elif tag == "B":
                t.wait_ge(P.DV.h, dv0 + dv_i[("mult", s)])
                if s >= 1:
                    t.wait_ge(P.AC.h, ac0 + ac_i[("aev", s - 1)])
                for kc in range(16):
                    mm = t.transpose(
                        P.psT[:, kc * 128:(kc + 1) * 128],
                        probs[:, (s % 2) * 2048 + kc * 128:
                              (s % 2) * 2048 + (kc + 1) * 128],
                        P.identb[:])
                P.PE.inc(mm)
            elif tag == "C":
                hl = s % 8
                rh, pr = 64 * (hl % 2), hl // 2
                t.wait_ge(P.AC.h, ac0 + ac_i[("aev", s)])
                if s >= 1:
                    t.wait_ge(P.AC.h, ac0 + ac_i[("cev", s - 1)])
                for kc in range(16):
                    mm = t.matmul(
                        P.psA[rh:rh + 64, CB + (pr % 2) * 128:
                              CB + (pr % 2) * 128 + 128],
                        vf[:, kc * 512 + hl * 64: kc * 512 + hl * 64 + 64],
                        attnT[:, kc * 128:(kc + 1) * 128],
                        start=(kc == 0), stop=(kc == 15),
                        tile_position=(0, rh))
                P.PE.inc(mm)
            elif tag == "D0":
                qt = s
                t.wait_ge(P.AC.h, ac0 + ac_i[("cev", 8 * qt + 7)])
                if qt == 0:
                    t.wait_ge(P.GP.h, gp0 + 16)
                else:
                    t.wait_ge(P.DV.h, dv0 + dv_i[("eo1", qt - 1)])
                ob = P.psA[:, WB:WB + 512]
                for cc in range(4):
                    mm = t.matmul(ob, ctxq[:, (qt % 2) * 512 + cc * 128:
                                           (qt % 2) * 512 + (cc + 1) * 128],
                                  P.wbuf[:, cc * 1024: cc * 1024 + 512],
                                  start=(cc == 0), stop=(cc == 3 and
                                                         not (h0 == 0)))
                if h0 == 0:
                    mm = t.matmul(ob, P.ones1[0:1, 0:128],
                                  P.brow[0:1, 0:512],
                                  start=False, stop=True)
                P.PE.inc(mm)
            else:
                qt = s
                t.wait_ge(P.DV.h, dv0 + dv_i[("eo0", qt)])
                ob = P.psA[:, WB:WB + 512]
                for cc in range(4):
                    mm = t.matmul(ob, ctxq[:, (qt % 2) * 512 + cc * 128:
                                           (qt % 2) * 512 + (cc + 1) * 128],
                                  P.wbuf[:, cc * 1024 + 512:
                                         cc * 1024 + 1024],
                                  start=(cc == 0), stop=(cc == 3 and
                                                         not (h0 == 0)))
                if h0 == 0:
                    mm = t.matmul(ob, P.ones1[0:1, 0:128],
                                  P.brow[0:1, 512:1024],
                                  start=False, stop=True)
                P.PE.inc(mm)

    def vector(v):
        for tag, s in order["dv"]:
            if tag == "minn":
                v.wait_ge(P.PE.h, pe0 + pe_i[("A", s)])
                if s >= 2:
                    v.wait_ge(P.AC.h, ac0 + ac_i[("exp", s - 2)])
                m = v.reduce_max(nm1[s % 2], P.psA[:, 0:2048].rearrange(
                    "p (g n) -> p g n", g=1), AX.X, negate=True)
                P.DV.inc(m)
            elif tag == "recip":
                v.wait_ge(P.AC.h, ac0 + ac_i[("exp", s)])
                m = v.reciprocal(rcW, zsum[s % 2].broadcast_to([128, 64]))
                P.DV.inc(m)
            elif tag == "mult":
                m = v.tensor_scalar_mul(
                    probs[:, (s % 2) * 2048:(s % 2) * 2048 + 2048],
                    probs[:, (s % 2) * 2048:(s % 2) * 2048 + 2048],
                    rcW[:, 0:1])
                P.DV.inc(m)
            else:
                qt, oc = s, 0 if tag == "eo0" else 1
                v.wait_ge(P.PE.h, pe0 + pe_i[("D" + str(oc), qt)])
                xs = P.xres[:, qt * D + oc * 512: qt * D + (oc + 1) * 512]
                m = v.tensor_add(xs, xs, P.psA[:, WB:WB + 512])
                P.DV.inc(m)

    def scalar(a):
        for tag, s in order["ac"]:
            if tag == "exp":
                a.wait_ge(P.DV.h, dv0 + dv_i[("minn", s)])
                if s >= 2:
                    a.wait_ge(P.PE.h, pe0 + pe_i[("B", s - 2)])
                m = a.activation(
                    probs[:, (s % 2) * 2048:(s % 2) * 2048 + 2048],
                    P.psA[:, 0:2048], AF.Exp,
                    bias=nm1[s % 2], scale=1.0, accum_out=zsum[s % 2])
                P.AC.inc(m)
            elif tag == "aev":
                a.wait_ge(P.PE.h, pe0 + pe_i[("B", s)])
                if s >= 1:
                    a.wait_ge(P.PE.h, pe0 + pe_i[("C", s - 1)])
                m = a.copy(attnT[:], P.psT[:])
                P.AC.inc(m)
            else:
                qt, hl = s // 8, s % 8
                rh, pr = 64 * (hl % 2), hl // 2
                a.wait_ge(P.PE.h, pe0 + pe_i[("C", s)])
                if hl == 0 and qt >= 2:
                    a.wait_ge(P.PE.h, pe0 + pe_i[("D1", qt - 2)])
                m = a.copy(ctxq[rh:rh + 64, (qt % 2) * 512 + pr * 128:
                                (qt % 2) * 512 + pr * 128 + 128],
                           P.psA[rh:rh + 64, CB + (pr % 2) * 128:
                                 CB + (pr % 2) * 128 + 128])
                P.AC.inc(m)

    P.phase(f"a0_attn_{h0}", gpsimd=gp, tensor=tensor, vector=vector,
            scalar=scalar)
    carver.pop()


def emit_l0_attn(P, carver, h):
    swsc = P.nc.dram_tensor("swsc", [72 * 4 * 128, 768], BF16).ap()
    qkv_scr = [P.nc.dram_tensor(f"qkvscr{m}", [128, 2048], BF16).ap()
               for m in range(24)]
    emit_a0_prepass(P, h, carver, swsc)
    carver.push()
    knots = carver.take(8 * KSTRIDE, F32)
    emit_a0_knots(P, h, knots)
    carver.push()
    bs = carver.take(64 * 256, BF16)
    sx = carver.take(8 * 256, BF16)
    xg32 = carver.take(2 * 256, F32)
    uw32 = carver.take(2 * 256, F32)
    sgn32 = carver.take(12 * 256, F32)
    bwork = carver.take(11 * 256, F32)
    wtl = carver.take(2 * 8 * 768, BF16)
    stg6 = carver.take(6 * 256, BF16)
    for g in range(A0_NG):
        emit_a0_bs_group(P, h, g, knots, xg32, uw32, sgn32, bwork, bs, sx)
        emit_a0_mm_group(P, h, g, bs, sx, wtl, stg6, swsc, qkv_scr)
    carver.pop()
    for h0 in (0, 8):
        carver.push()
        qf = carver.take(4 * 2048, BF16)
        kf = carver.take(4 * 2048, BF16)
        vf = carver.take(NTT * 512, BF16)
        rbuf = carver.take(2 * 2048, BF16)
        emit_a0_reload_half(P, h, h0, qkv_scr, rbuf, qf, kf, vf)
        emit_a0_attn_half(P, h, h0, qf, kf, vf, carver)
        carver.pop()
    carver.pop()
    emit_postln(P)




def emit_f0_shift(P, xsb, src_cols, as_phase=True):

    def gpsimd(g):
        gp0 = P.GP.n
        g.wait_ge(P.GP.h, gp0)
        srcs = P.xres[:, :].rearrange("p (tt c) -> p tt c", c=D)[
            :, :, src_cols:src_cols + 512]
        dsts = P.shift_scr[1:S + 1, :].rearrange("(tt p) c -> p tt c", p=128)
        _dma(g, P.GP, dsts, srcs)
        g.wait_ge(P.GP.h, gp0 + 16)
        srcl = P.shift_scr[0:S, :].rearrange("(tt p) c -> p tt c", p=128)
        dstl = xsb[:, :].rearrange("p (tt c) -> p tt c", c=512)
        _dma(g, P.GP, dstl, srcl)
        g.wait_ge(P.GP.h, gp0 + 32)

    if as_phase:
        P.phase("f0_shift", gpsimd=gpsimd)
    return gpsimd


def emit_f0_qkv_half(P, h, li, h0, xfm, qk_sc, qf, kf, vf, wsl, qb16):
    wq = h["f0_Wqkv"]
    qk32 = P.scr[:, 0:1024]
    sq32 = P.scr[:, 1024:1536]
    qn32 = P.scr[:, 1536:2048]
    rot32 = P.scr[:, 2048:2560]
    ss8 = P.stat[:, 48:56]
    rr8 = P.stat[:, 56:64]
    rrW = P.stat[:, 64:128]

    for c0, kind in ((h0 * 64, "q"), (1024 + h0 * 64, "k"),
                     (2048 + h0 * 64, "v")):
        gp0, pe0, dv0, ac0 = P.GP.n, P.PE.n, P.DV.n, P.AC.n

        def gp(g, c0=c0):
            src = wq[li, :, c0:c0 + 512].rearrange("(cc p) o -> p cc o", p=128)
            dst = wsl[:, 0:4096].rearrange("p (cc o) -> p cc o", o=512)
            _dma(g, P.GP, dst, src)

        inc_mm = lambda tt: tt + 1 if tt <= 2 else 2 * tt - 1
        inc_T = lambda j: 4 + 2 * j if j <= 12 else 17 + j
        inc_sqrt = lambda tt: tt + 1 if tt < 4 else 2 * tt - 2
        inc_evac = lambda j: 5 + 2 * j if j < 12 else 17 + j

        def emit_mm(t, tt, dv0, ac0, kind):
            bank = P.psA[:, (tt % 2) * 512:(tt % 2) * 512 + 512]
            if tt >= 2:
                if kind == "v":
                    t.wait_ge(P.AC.h, ac0 + (tt - 2) + 1)
                else:
                    t.wait_ge(P.DV.h, dv0 + (tt - 2) * 8 + 1)
            for cc in range(8):
                mm = t.matmul(bank,
                              xfm[:, cc * XFM_STRIDE + 1 + tt * 128:
                                  cc * XFM_STRIDE + 1 + (tt + 1) * 128],
                              wsl[:, cc * 512:(cc + 1) * 512],
                              start=(cc == 0), stop=(cc == 7))
            P.PE.inc(mm)

        def emit_T(t, j, dv0, ac0):
            t.wait_ge(P.DV.h, dv0 + j * 8 + 8)
            if j >= 1:
                t.wait_ge(P.AC.h, ac0 + inc_evac(j - 1))
            sb = qb16[:, (j % 2) * 512:(j % 2) * 512 + 512]
            for jj in range(8):
                mm = t.transpose(
                    P.psT[64 * (jj % 2):64 * (jj % 2) + 64,
                          (jj // 2) * 128:(jj // 2) * 128 + 128],
                    sb[:, jj * 64:(jj + 1) * 64], P.identb[:],
                    tile_position=(0, 64 * (jj % 2)))
            P.PE.inc(mm)

        def tensor(t, kind=kind, gp0=gp0, pe0=pe0, dv0=dv0, ac0=ac0):
            t.wait_ge(P.GP.h, gp0 + 16)
            if kind == "v":
                for tt in range(NTT):
                    emit_mm(t, tt, dv0, ac0, kind)
            else:
                for tt in range(3):
                    emit_mm(t, tt, dv0, ac0, kind)
                for j in range(13):
                    emit_T(t, j, dv0, ac0)
                    if j + 3 < NTT:
                        emit_mm(t, j + 3, dv0, ac0, kind)
                for j in range(13, NTT):
                    emit_T(t, j, dv0, ac0)

        def vector(v, kind=kind, pe0=pe0, ac0=ac0):
            if kind == "v":
                return
            soff = 0 if kind == "q" else 128
            toff = 0 if kind == "q" else 2 * NTT * 64
            for tt in range(NTT):
                v.wait_ge(P.PE.h, pe0 + (tt + 1 if tt <= 2 else 2 * tt - 1))
                bank = P.psA[:, (tt % 2) * 512:(tt % 2) * 512 + 512]
                x32 = qk32[:, (tt % 2) * 512:(tt % 2) * 512 + 512]
                m = v.tensor_copy(x32, bank)
                P.DV.inc(m)
                m = v.scalar_tensor_tensor(sq32[:], x32, 1.0, x32, ALU.mult,
                                           ALU.mult)
                P.DV.inc(m)
                m = v.reduce_sum(ss8, sq32.rearrange("p (g d) -> p g d", d=64),
                                 AX.X)
                P.DV.inc(m)
                v.wait_ge(P.AC.h, ac0 + (tt + 1 if tt < 4 else 2 * tt - 2))
                m = v.reciprocal(rrW.rearrange("p (g r) -> p g r", r=8),
                                 rr8.unsqueeze(2).broadcast_to([128, 8, 8]))
                P.DV.inc(m)
                rrb = rrW.rearrange("p (g r) -> p g r", r=8)[
                    :, :, 0:1].broadcast_to([128, 8, 64])
                m = v.tensor_tensor(qn32.rearrange("p (g d) -> p g d", d=64),
                                    x32.rearrange("p (g d) -> p g d", d=64),
                                    rrb, ALU.mult)
                P.DV.inc(m)
                cs_t = P.rtab_b[:, toff + tt * 64: toff + tt * 64 + 64]
                ss_t = P.rtab_b[:, NTT * 64 + toff + tt * 64:
                                NTT * 64 + toff + tt * 64 + 64]
                csb = cs_t.unsqueeze(1).broadcast_to([128, 8, 64])
                m = v.tensor_tensor(rot32.rearrange("p (g d) -> p g d", d=64),
                                    qn32.rearrange("p (g d) -> p g d", d=64),
                                    csb, ALU.mult)
                P.DV.inc(m)
                ssa = ss_t[:, 0:32].unsqueeze(1).broadcast_to([128, 8, 32])
                ssb = ss_t[:, 32:64].unsqueeze(1).broadcast_to([128, 8, 32])
                va = qn32.rearrange("p (g d) -> p g d", d=64)
                sw = sq32.rearrange("p (g d) -> p g d", d=64)
                v.tensor_tensor(sw[:, :, 0:32], va[:, :, 32:64], ssa, ALU.mult)
                m = v.tensor_tensor(sw[:, :, 32:64], va[:, :, 0:32], ssb,
                                    ALU.mult)
                P.DV.inc(m)
                if tt >= 2:
                    v.wait_ge(P.PE.h, pe0 + (4 + 2 * (tt - 2)
                                             if tt - 2 <= 12 else 15 + tt))
                dst = qb16[:, (tt % 2) * 512:(tt % 2) * 512 + 512]
                m = v.tensor_add(dst, rot32, sq32)
                P.DV.inc(m)

        def scalar(a, kind=kind, pe0=pe0, dv0=dv0):
            if kind == "v":
                for tt in range(NTT):
                    a.wait_ge(P.PE.h, pe0 + tt + 1)
                    bank = P.psA[:, (tt % 2) * 512:(tt % 2) * 512 + 512]
                    m = a.copy(vf[:, tt * 512:(tt + 1) * 512], bank)
                    P.AC.inc(m)
            else:
                arena = qf if kind == "q" else kf

                def emit_sqrt(a, tt):
                    a.wait_ge(P.DV.h, dv0 + tt * 8 + 3)
                    m = a.activation(rr8, ss8, AF.Sqrt, bias=P.eps24,
                                     scale=1.0)
                    P.AC.inc(m)

                def emit_ev(a, j):
                    a.wait_ge(P.PE.h, pe0 + (4 + 2 * j if j <= 12 else 17 + j))
                    for pr in range(4):
                        m = a.copy(arena[:, pr * 2048 + j * 128:
                                         pr * 2048 + (j + 1) * 128],
                                   P.psT[:, pr * 128:(pr + 1) * 128])
                    P.AC.inc(m)

                for tt in range(4):
                    emit_sqrt(a, tt)
                for j in range(12):
                    emit_ev(a, j)
                    emit_sqrt(a, j + 4)
                for j in range(12, NTT):
                    emit_ev(a, j)
        P.phase(f"f0_qkv_{kind}", gpsimd=gp, tensor=tensor, vector=vector,
                scalar=scalar)


def emit_f0_attn_half(P, h, li, h0, qf, kf, vf, carver):
    carver.push()
    probs = carver.take(2 * 2048, BF16)
    attnT = carver.take(2048, BF16)
    ctxq = carver.take(2 * 512, BF16)
    asum = P.stat[:, 48:52]
    nm4 = [P.stat[:, 52:56], P.stat[:, 56:60]]
    rc4 = [P.stat[:, 64:128], P.stat[:, 128:192]]
    NS = 2 * NTT
    CB = 2048
    WB = 2560

    def nkof(s):
        return 512 if s // 2 >= 2 else 256

    def ksof(s):
        return 256 * (s // 2 // 2 - 1) if s // 2 >= 2 else 0

    order = {"pe": [], "dv": [], "ac": []}
    for s in range(NS + 6):
        if s < NS:
            order["pe"].append(("A", s))
        if 0 <= s - 5 < NS and (s - 5) % 2 == 1:
            order["pe"].append(("D1", (s - 5) // 2))
        if 0 <= s - 2 < NS:
            order["pe"].append(("B", s - 2))
        if 0 <= s - 3 < NS:
            order["pe"].append(("C", s - 3))
        if 0 <= s - 4 < NS and (s - 4) % 2 == 1:
            order["pe"].append(("D0", (s - 4) // 2))
        if 0 <= s - 1 < NS:
            order["dv"].append(("recip", s - 1))
        if 0 <= s - 2 < NS:
            order["dv"].append(("mult", s - 2))
        if s < NS:
            order["dv"].append(("minnA", s))
            order["dv"].append(("minn", s))
        if 0 <= s - 4 < NS and (s - 4) % 2 == 1:
            order["dv"].append(("eo0", (s - 4) // 2))
        if 0 <= s - 5 < NS and (s - 5) % 2 == 1:
            order["dv"].append(("eo1", (s - 5) // 2))
        if 0 <= s - 1 < NS:
            order["ac"].append(("exp", s - 1))
        if 0 <= s - 2 < NS:
            order["ac"].append(("aev", s - 2))
        if 0 <= s - 3 < NS:
            order["ac"].append(("cev", s - 3))
    pe_i = {k: i + 1 for i, k in enumerate(order["pe"])}
    dv_i = {k: i + 1 for i, k in enumerate(order["dv"])}
    ac_i = {k: i + 1 for i, k in enumerate(order["ac"])}

    gp0 = P.GP.n
    pe0, dv0, ac0 = P.PE.n, P.DV.n, P.AC.n

    def gp(g):
        src = h["f0_Wo"][li, h0 * 64: h0 * 64 + 512, :].rearrange(
            "(cc p) o -> p cc o", p=128)
        dst = P.wbuf[:, 0:4096].rearrange("p (cc o) -> p cc o", o=1024)
        _dma(g, P.GP, dst, src)

    def tensor(t):
        for tag, s in order["pe"]:
            if tag == "A":
                nk, ks, qt = nkof(s), ksof(s), s // 2
                hg = s % 2
                if s >= 1:
                    t.wait_ge(P.DV.h, dv0 + dv_i[("minn", s - 1)])
                for jj in range(4):
                    j = hg * 4 + jj
                    rh, pr = 64 * (j % 2), j // 2
                    mm = t.matmul(
                        P.psA[:, jj * 512: jj * 512 + nk],
                        qf[rh:rh + 64, pr * 2048 + qt * 128:
                           pr * 2048 + (qt + 1) * 128],
                        kf[rh:rh + 64, pr * 2048 + ks: pr * 2048 + ks + nk],
                        start=True, stop=True, tile_position=(rh, 0))
                P.PE.inc(mm)
            elif tag == "B":
                nk = nkof(s)
                t.wait_ge(P.DV.h, dv0 + dv_i[("mult", s)])
                if s >= 1:
                    t.wait_ge(P.AC.h, ac0 + ac_i[("aev", s - 1)])
                for jj in range(4):
                    for kc in range(nk // 128):
                        mm = t.transpose(
                            P.psT[:, (jj * 4 + kc) * 128:
                                  (jj * 4 + kc) * 128 + 128],
                            probs[:, (s % 2) * 2048 + jj * 512 + kc * 128:
                                  (s % 2) * 2048 + jj * 512 + (kc + 1) * 128],
                            P.identb[:])
                P.PE.inc(mm)
            elif tag == "C":
                nk, ks, hg = nkof(s), ksof(s), s % 2
                t.wait_ge(P.AC.h, ac0 + ac_i[("aev", s)])
                if s >= 1:
                    t.wait_ge(P.AC.h, ac0 + ac_i[("cev", s - 1)])
                for jj in range(4):
                    j = hg * 4 + jj
                    rh, pr = 64 * (j % 2), j // 2
                    prl = pr - 2 * hg
                    for kc in range(nk // 128):
                        ktt = (ks // 128) + kc
                        mm = t.matmul(
                            P.psA[rh:rh + 64, CB + prl * 128:
                                  CB + prl * 128 + 128],
                            vf[:, ktt * 512 + j * 64: ktt * 512 + j * 64 + 64],
                            attnT[:, (jj * 4 + kc) * 128:
                                  (jj * 4 + kc) * 128 + 128],
                            start=(kc == 0), stop=(kc == nk // 128 - 1),
                            tile_position=(0, rh))
                P.PE.inc(mm)
            elif tag == "D0":
                qt = s
                t.wait_ge(P.AC.h, ac0 + ac_i[("cev", 2 * qt + 1)])
                if qt == 0:
                    t.wait_ge(P.GP.h, gp0 + 16)
                else:
                    t.wait_ge(P.DV.h, dv0 + dv_i[("eo1", qt - 1)])
                for cc in range(4):
                    mm = t.matmul(P.psA[:, WB:WB + 512],
                                  ctxq[:, (qt % 2) * 512 + cc * 128:
                                       (qt % 2) * 512 + (cc + 1) * 128],
                                  P.wbuf[:, cc * 1024: cc * 1024 + 512],
                                  start=(cc == 0), stop=(cc == 3))
                P.PE.inc(mm)
            else:
                qt = s
                t.wait_ge(P.DV.h, dv0 + dv_i[("eo0", qt)])
                for cc in range(4):
                    mm = t.matmul(P.psA[:, WB:WB + 512],
                                  ctxq[:, (qt % 2) * 512 + cc * 128:
                                       (qt % 2) * 512 + (cc + 1) * 128],
                                  P.wbuf[:, cc * 1024 + 512:
                                         cc * 1024 + 1024],
                                  start=(cc == 0), stop=(cc == 3))
                P.PE.inc(mm)

    def vector(v):
        for tag, s in order["dv"]:
            if tag == "recip":
                v.wait_ge(P.AC.h, ac0 + ac_i[("exp", s)])
                m = v.reciprocal(rc4[s % 2].rearrange("p (g r) -> p g r", r=16),
                                 asum.unsqueeze(2).broadcast_to([128, 4, 16]))
                P.DV.inc(m)
            elif tag == "mult":
                nk = nkof(s)
                for jj in range(4):
                    m = v.tensor_scalar_mul(
                        probs[:, (s % 2) * 2048 + jj * 512:
                              (s % 2) * 2048 + jj * 512 + nk],
                        probs[:, (s % 2) * 2048 + jj * 512:
                              (s % 2) * 2048 + jj * 512 + nk],
                        rc4[s % 2][:, jj * 16: jj * 16 + 1])
                P.DV.inc(m)
            elif tag == "minnA":
                nk, qt = nkof(s), s // 2
                v.wait_ge(P.PE.h, pe0 + pe_i[("A", s)])
                if s >= 2:
                    v.wait_ge(P.AC.h, ac0 + ac_i[("exp", s - 2)])
                mk = P.masks[:, 512 * (qt % 2): 512 * (qt % 2) + 512]
                mkb = mk[:, 512 - nk:512].unsqueeze(1).broadcast_to(
                    [128, 4, nk])
                sc = P.scr[:, 0:4 * nk]
                m = v.tensor_tensor(sc.rearrange("p (g n) -> p g n", g=4),
                                    P.psA[:, 0:2048].rearrange(
                                        "p (g n) -> p g n", g=4)[:, :, 0:nk],
                                    mkb, ALU.add)
                P.DV.inc(m)
            elif tag == "minn":
                nk = nkof(s)
                sc = P.scr[:, 0:4 * nk]
                m = v.reduce_max(nm4[s % 2],
                                 sc.rearrange("p (g n) -> p g n", g=4),
                                 AX.X, negate=True)
                P.DV.inc(m)
            else:
                qt, oc = s, 0 if tag == "eo0" else 1
                v.wait_ge(P.PE.h, pe0 + pe_i[("D" + str(oc), qt)])
                xs = P.xres[:, qt * D + oc * 512: qt * D + (oc + 1) * 512]
                m = v.tensor_add(xs, xs, P.psA[:, WB:WB + 512])
                P.DV.inc(m)

    def scalar(a):
        for tag, s in order["ac"]:
            if tag == "exp":
                nk = nkof(s)
                a.wait_ge(P.DV.h, dv0 + dv_i[("minn", s)])
                if s >= 2:
                    a.wait_ge(P.PE.h, pe0 + pe_i[("B", s - 2)])
                sc = P.scr[:, 0:4 * nk]
                for jj in range(4):
                    m = a.activation(
                        probs[:, (s % 2) * 2048 + jj * 512:
                              (s % 2) * 2048 + jj * 512 + nk],
                        sc[:, jj * nk:(jj + 1) * nk], AF.Exp,
                        bias=nm4[s % 2][:, jj: jj + 1], scale=1.0,
                        accum_out=asum[:, jj: jj + 1])
                P.AC.inc(m)
            elif tag == "aev":
                nk = nkof(s)
                a.wait_ge(P.PE.h, pe0 + pe_i[("B", s)])
                if s >= 1:
                    a.wait_ge(P.PE.h, pe0 + pe_i[("C", s - 1)])
                m = a.copy(
                    attnT[:, 0:2048].rearrange("p (j n) -> p j n", j=4)[
                        :, :, 0:nk],
                    P.psT[:, 0:2048].rearrange("p (j n) -> p j n", j=4)[
                        :, :, 0:nk])
                P.AC.inc(m)
            else:
                qt, hg = s // 2, s % 2
                a.wait_ge(P.PE.h, pe0 + pe_i[("C", s)])
                if qt >= 2 and hg == 0:
                    a.wait_ge(P.PE.h, pe0 + pe_i[("D1", qt - 2)])
                m = a.copy(ctxq[:, (qt % 2) * 512 + 2 * hg * 128:
                                (qt % 2) * 512 + (2 * hg + 2) * 128],
                           P.psA[:, CB: CB + 256])
                P.AC.inc(m)

    P.phase("f0_attn", gpsimd=gp, tensor=tensor, vector=vector, scalar=scalar)
    carver.pop()


def emit_f0_ffn(P, h, li, xfm, carver):
    nc = P.nc
    lgc = carver.take(8, F32)
    lnbb = carver.take(8, BF16)
    wsl = carver.take(2 * 4096, BF16)
    carver.push()
    xsb = carver.take(NTT * 512, BF16)
    gshift2 = emit_f0_shift(P, xsb, 512, as_phase=False)

    mu, rs, t16 = P.mu, P.rs, P.tmp16
    sB = P.stat[:, 48:64]
    ssB = P.stat[:, 64:80]
    junk = P.scr[:, 0:512]
    dv0, ac0 = P.DV.n, P.AC.n

    def v_stats(v):
        v.wait_ge(P.GP.h, P.GP.n)
        jk = lambda i: P.scr[:, i * 512:(i + 1) * 512]
        for tt in range(NTT):
            xa = P.xres[:, tt * D: tt * D + 512]
            xb = xsb[:, tt * 512:(tt + 1) * 512]
            v.tensor_scalar(jk(0), xa, 1.0, 0.0, ALU.mult, ALU.add,
                            accum_out=mu[:, tt: tt + 1])
            v.scalar_tensor_tensor(jk(1), xa, 1.0, xa, ALU.mult, ALU.mult,
                                   accum_out=rs[:, tt: tt + 1])
            v.tensor_scalar(jk(2), xb, 1.0, 0.0, ALU.mult, ALU.add,
                            accum_out=sB[:, tt: tt + 1])
            v.scalar_tensor_tensor(jk(3), xb, 1.0, xb, ALU.mult, ALU.mult,
                                   accum_out=ssB[:, tt: tt + 1])
            P.DV.inc(v.drain())
            v.wait_ge(P.DV.h, P.DV.n)
        v.tensor_add(mu[:], mu[:], sB[:])
        v.tensor_add(rs[:], rs[:], ssB[:])
        v.drain()
        v.tensor_scalar_mul(mu[:], mu[:], 1.0 / D)
        v.tensor_scalar_mul(rs[:], rs[:], 1.0 / D)
        v.drain()
        v.tensor_mul(t16[:], mu[:], mu[:])
        v.drain()
        v.tensor_sub(rs[:], rs[:], t16[:])
        P.DV.inc(v.drain())
        v.wait_ge(P.AC.h, ac0 + 1)
        v.reciprocal(rs[:], rs[:])
        v.drain()

    def a_stats(a):
        a.wait_ge(P.DV.h, dv0 + NTT + 1)
        m = a.activation(rs[:], rs[:], AF.Sqrt, bias=P.eps, scale=1.0)
        P.AC.inc(m)

    P.phase("f0_stats", gpsimd=gshift2, vector=v_stats, scalar=a_stats)

    def src_xn(v, tt, slot):
        v.tensor_scalar(slot[:, 0:512], P.xres[:, tt * D: tt * D + 512],
                        mu[:, tt: tt + 1], rs[:, tt: tt + 1],
                        ALU.subtract, ALU.mult)
        return v.tensor_scalar(slot[:, 512:1024], xsb[:, tt * 512:(tt + 1) * 512],
                               mu[:, tt: tt + 1], rs[:, tt: tt + 1],
                               ALU.subtract, ALU.mult)

    gpb = P.GP.n

    def gp_bias(g):
        _dma(g, P.GP, lgc[:, 0:8],
             h["f0_ln_g"][li].rearrange("(cc p) u -> p (cc u)", p=128),
             allow_slow_non_contiguous=True)
        _dma(g, P.GP, lnbb[:, 0:8],
             h["f0_ln_b"][li].rearrange("(cc p) u -> p (cc u)", p=128),
             allow_slow_non_contiguous=True)
        for sj in range(12):
            j, part = sj // 2, sj % 2
            w = 512 if j < 5 else L0F_DI - 512 * 5
            c0 = part * L0F_DI + 512 * j
            r, cs = 32 * (sj // 3), 512 * (sj % 3)
            _dma(g, P.GP, P.brow[r:r + 1, cs:cs + w],
                 h["f0_b1"][li][:, c0:c0 + w])
        _dma(g, P.GP, P.brow[32:33, 1536:2048], h["f0_b2"][li][:, 0:512])
        _dma(g, P.GP, P.brow[64:65, 1536:2048], h["f0_b2"][li][:, 512:1024])
        g.wait_ge(P.GP.h, P.GP.n)

    WJ = [512] * 5 + [L0F_DI - 2560]
    w1_ready = {}

    def w1_load(g, ttg, j):
        sem = P.ST2[j % 2]
        w = WJ[j]
        for part in range(2):
            c0 = part * L0F_DI + 512 * j
            src = h["f0_W1"][li][:, c0:c0 + w].rearrange(
                "(cc p) o -> p cc o", p=128)
            dst = wsl[:, part * 4096: part * 4096 + 8 * w].rearrange(
                "p (cc o) -> p cc o", o=w)
            _dma(g, sem, dst, src)
        w1_ready[(ttg, j)] = (sem, sem.n)

    def gp_bias2(g):
        gp_bias(g)
        w1_load(g, 0, 0)

    P.phase("f0_bias", gpsimd=gp_bias2)
    build_xfm(P, xfm, src_xn, evac_scale=lambda cc: lgc[:, cc: cc + 1])
    carver.pop()

    carver.push()
    agf = carver.take(22 * 1024, BF16)
    gel = carver.take(512, BF16)
    ag16 = carver.take(512, BF16)
    for ttg in range(2):
        for j in range(6):
            w = WJ[j]
            nblk = (w + 127) // 128
            gp0, pe0, dv0, ac0, pl0 = P.GP.n, P.PE.n, P.DV.n, P.AC.n, P.PL.n
            sj_a, sj_g = j * 2, j * 2 + 1

            def gp(g, j=j, ttg=ttg, pe0=pe0):
                if j < 5:
                    npe_beff = 2 if ttg == 0 else 0
                    g.wait_ge(P.PE.h, pe0 + npe_beff + 13)
                    w1_load(g, ttg, j + 1)

            def tensor(t, j=j, w=w, ttg=ttg, gp0=gp0, pe0=pe0, dv0=dv0,
                       ac0=ac0, pl0=pl0, sj_a=sj_a, sj_g=sj_g):
                sem_r, val_r = w1_ready[(ttg, j)]
                t.wait_ge(sem_r.h, val_r)
                npe = 0
                if ttg == 0:
                    for part, sj in ((0, sj_a), (1, sj_g)):
                        r = 32 * (sj // 3)
                        for cc in range(8):
                            mm = t.matmul(
                                P.psA[r:r + 1, 2048 + part * 512:
                                      2048 + part * 512 + w],
                                lnbb[:, cc: cc + 1],
                                wsl[:, part * 4096 + cc * w:
                                    part * 4096 + (cc + 1) * w],
                                start=(cc == 0), stop=(cc == 7),
                                tile_position=(0, r))
                        P.PE.inc(mm)
                        npe += 1
                ndv_beff = 2 if ttg == 0 else 0
                if ttg == 0:
                    t.wait_ge(P.DV.h, dv0 + 2)

                def emit_mm(tl):
                    tt = ttg * 8 + tl
                    ba = P.psA[:, (tl % 2) * 1024:(tl % 2) * 1024 + w]
                    bg = P.psA[:, (tl % 2) * 1024 + 512:
                               (tl % 2) * 1024 + 512 + w]
                    if tl >= 2:
                        t.wait_ge(P.DV.h, dv0 + ndv_beff + (tl - 2) + 1)
                        t.wait_ge(P.AC.h, ac0 + 2 * (tl - 2) + 1)
                    for part, bank, sj in ((0, ba, sj_a), (1, bg, sj_g)):
                        r, cs = 32 * (sj // 3), 512 * (sj % 3)
                        for cc in range(8):
                            t.matmul(bank,
                                     xfm[:, cc * XFM_STRIDE + 1 + tt * 128:
                                         cc * XFM_STRIDE + 1 + (tt + 1) * 128],
                                     wsl[:, part * 4096 + cc * w:
                                         part * 4096 + (cc + 1) * w],
                                     start=(cc == 0), stop=False)
                        mm = t.matmul(bank, P.ones1[r:r + 1, 0:128],
                                      P.brow[r:r + 1, cs:cs + w],
                                      start=False, stop=True,
                                      tile_position=(r, 0))
                    P.PE.inc(mm)

                def emit_T(j):
                    t.wait_ge(P.DV.h, dv0 + ndv_beff + j + 1)
                    if j >= 1:
                        t.wait_ge(P.AC.h, ac0 + 2 * j)
                    for b in range(nblk):
                        bw = min(128, w - b * 128)
                        mm = t.transpose(P.psT[0:bw, b * 128: b * 128 + 128],
                                         ag16[:, b * 128: b * 128 + bw],
                                         P.identb[:])
                    P.PE.inc(mm)

                emit_mm(0)
                emit_mm(1)
                for j in range(6):
                    emit_mm(j + 2)
                    emit_T(j)
                emit_T(6)
                emit_T(7)

            def vector(v, j=j, w=w, ttg=ttg, pe0=pe0, ac0=ac0, sj_a=sj_a,
                       sj_g=sj_g):
                npe_beff = 2 if ttg == 0 else 0
                if ttg == 0:
                    v.wait_ge(P.PE.h, pe0 + 2)
                    for part, sj in ((0, sj_a), (1, sj_g)):
                        r, cs = 32 * (sj // 3), 512 * (sj % 3)
                        m = v.tensor_add(P.brow[r:r + 1, cs:cs + w],
                                         P.brow[r:r + 1, cs:cs + w],
                                         P.psA[r:r + 1, 2048 + part * 512:
                                               2048 + part * 512 + w])
                        P.DV.inc(m)
                imm = lambda tl: tl + 1 if tl < 2 else 2 * tl - 1
                iT = lambda j: 2 * j + 4 if j <= 5 else j + 9
                for tl in range(8):
                    v.wait_ge(P.PE.h, pe0 + npe_beff + imm(tl))
                    v.wait_ge(P.AC.h, ac0 + 2 * tl + 1)
                    if tl >= 1:
                        v.wait_ge(P.PE.h, pe0 + npe_beff + iT(tl - 1))
                    ba = P.psA[:, (tl % 2) * 1024:(tl % 2) * 1024 + w]
                    m = v.tensor_mul(ag16[:, 0:w], ba, gel[:, 0:w])
                    P.DV.inc(m)

            def scalar(a, j=j, w=w, ttg=ttg, pe0=pe0, dv0=dv0, nblk=nblk):
                npe_beff = 2 if ttg == 0 else 0
                ndv_beff = 2 if ttg == 0 else 0
                imm = lambda tl: tl + 1 if tl < 2 else 2 * tl - 1
                iT = lambda jj: 2 * jj + 4 if jj <= 5 else jj + 9
                for tl in range(8):
                    a.wait_ge(P.PE.h, pe0 + npe_beff + imm(tl))
                    if tl >= 1:
                        a.wait_ge(P.DV.h, dv0 + ndv_beff + tl)
                    bg = P.psA[:, (tl % 2) * 1024 + 512:
                               (tl % 2) * 1024 + 512 + w]
                    m = a.activation(gel[:, 0:w], bg, AF.Gelu)
                    P.AC.inc(m)
                    a.wait_ge(P.PE.h, pe0 + npe_beff + iT(tl))
                    for b in range(nblk):
                        bw = min(128, w - b * 128)
                        m = a.copy(agf[0:bw, (4 * j + b) * 1024 + tl * 128:
                                       (4 * j + b) * 1024 + (tl + 1) * 128],
                                   P.psT[0:bw, b * 128: b * 128 + 128])
                    P.AC.inc(m)

            P.phase(f"f0_w1_{ttg}_{j}", gpsimd=gp,
                    tensor=tensor, vector=vector, scalar=scalar)

        CCW = [128] * 21 + [L0F_DI - 21 * 128]
        bcc = [[0, 1, 2, 3], [4, 5, 6, 7], [8, 9, 10, 11], [12, 13, 14, 15],
               [16, 17, 18, 19], [20], [21]]
        W2B = [(oc, ccs) for oc in range(2) for ccs in bcc]
        batchof = lambda cc: cc // 4 if cc < 20 else (5 if cc == 20 else 6)
        w2slot = lambda cc: ((batchof(cc) % 2) * 4 +
                             (cc - 4 * batchof(cc) if cc < 20 else 0))
        for sub in range(2):
            gp0, pe0, dv0 = P.GP.n, P.PE.n, P.DV.n

            wt0 = [P.WT[0].n, P.WT[1].n]

            def gp2(g, ttg=ttg, sub=sub, gp0=gp0, pe0=pe0):
                lastg = {}
                for bi, (oc, ccs) in enumerate(W2B):
                    grp = w2slot(ccs[0]) // 4
                    if grp in lastg:
                        poc, pccs = W2B[lastg[grp]]
                        g.wait_ge(P.PE.h, pe0 + poc * 23 + pccs[-1] + 1)
                    lastg[grp] = bi
                    c0 = ccs[0]
                    nb = len(ccs)
                    sl0 = w2slot(c0)
                    if CCW[ccs[-1]] == 128:
                        src = h["f0_W2"][li][c0 * 128:(c0 + nb) * 128,
                                             oc * 512:(oc + 1) * 512
                                             ].rearrange("(c p) o -> p c o",
                                                         p=128)
                        dst = wsl[:, sl0 * 512:(sl0 + nb) * 512].rearrange(
                            "p (c o) -> p c o", o=512)
                    else:
                        wcc = CCW[ccs[-1]]
                        src = h["f0_W2"][li][c0 * 128: c0 * 128 + wcc,
                                             oc * 512:(oc + 1) * 512]
                        dst = wsl[0:wcc, sl0 * 512: sl0 * 512 + 512]
                    _dma(g, P.WT[bi % 2], dst, src)
                if ttg == 0 and sub == 1:
                    g.wait_ge(P.PE.h, pe0 + 46)
                    w1_load(g, 1, 0)

            def tensor2(t, ttg=ttg, sub=sub, gp0=gp0, pe0=pe0, dv0=dv0):
                wtc = [0, 0]
                for bi, (oc, ccs) in enumerate(W2B):
                    wtc[bi % 2] += 16
                    for cc in ccs:
                        wcc = CCW[cc]
                        t.wait_ge(P.WT[bi % 2].h, wt0[bi % 2] + wtc[bi % 2])
                        for tq in range(4):
                            tl = sub * 4 + tq
                            bank = P.psA[:, tq * 512:(tq + 1) * 512]
                            if cc == 0 and oc == 1:
                                t.wait_ge(P.DV.h, dv0 + tq + 1)
                            mm = t.matmul(
                                bank,
                                agf[0:wcc, cc * 1024 + tl * 128:
                                    cc * 1024 + (tl + 1) * 128],
                                wsl[0:wcc, w2slot(cc) * 512:
                                    w2slot(cc) * 512 + 512],
                                start=(cc == 0), stop=False)
                        P.PE.inc(mm)
                    if ccs[-1] != 21:
                        continue
                    r = 32 * (oc + 1)
                    for tq in range(4):
                        bank = P.psA[:, tq * 512:(tq + 1) * 512]
                        mm = t.matmul(bank, P.ones1[r:r + 1, 0:128],
                                      P.brow[r:r + 1, 1536:2048],
                                      start=False, stop=True,
                                      tile_position=(r, 0))
                    P.PE.inc(mm)

            def vector2(v, ttg=ttg, sub=sub, pe0=pe0):
                for oc in range(2):
                    v.wait_ge(P.PE.h, pe0 + (oc + 1) * 23)
                    for tq in range(4):
                        tt = ttg * 8 + sub * 4 + tq
                        bank = P.psA[:, tq * 512:(tq + 1) * 512]
                        xs = P.xres[:, tt * D + oc * 512:
                                    tt * D + (oc + 1) * 512]
                        m = v.tensor_add(xs, xs, bank)
                        P.DV.inc(m)

            P.phase(f"f0_w2_{ttg}_{sub}", gpsimd=gp2, tensor=tensor2,
                    vector=vector2)
    carver.pop()


def emit_l0_ffn_layer(P, carver, h, li):
    carver.push()
    xfm = carver.take(8 * XFM_STRIDE, BF16)
    qk_sc = carver.take(4 * 64, F32)
    qk_scb = carver.take(4 * 64, BF16)

    carver.push()
    xsb = carver.take(NTT * 512, BF16)
    gshift = emit_f0_shift(P, xsb, 512, as_phase=False)

    def src_xs(v, tt, slot):
        v.tensor_copy(slot[:, 0:512], P.xres[:, tt * D: tt * D + 512])
        return v.tensor_copy(slot[:, 512:1024], xsb[:, tt * 512:(tt + 1) * 512])

    build_xfm(P, xfm, src_xs, gpsimd=gshift)
    carver.pop()

    gp0, pe0 = P.GP.n, P.PE.n

    def gp_qs(g):
        _dma(g, P.GP, P.brow[0:1, 1536:1600], h["f0_qs"][li])
        _dma(g, P.GP, P.brow[0:1, 1600:1664], h["f0_qs_perm"][li])
        _dma(g, P.GP, P.brow[0:1, 1664:1728], h["f0_ks"][li])
        _dma(g, P.GP, P.brow[0:1, 1728:1792], h["f0_ks_perm"][li])
        g.wait_ge(P.GP.h, gp0 + 64)
        for i, k in enumerate(["rot_cs_q", "rot_ss_q", "rot_cs_k",
                               "rot_ss_k"]):
            src = h[k].rearrange("(tt p) d -> p tt d", p=128)
            dst = P.rtab_b[:, i * NTT * 64:(i + 1) * NTT * 64].rearrange(
                "p (tt d) -> p tt d", d=64)
            _dma(g, P.GP, dst, src)

    def t_qs(t):
        t.wait_ge(P.GP.h, gp0 + 64)
        for i in range(4):
            mm = t.matmul(P.psA[:, i * 512: i * 512 + 64],
                          P.ones1[0:1, 0:128],
                          P.brow[0:1, 1536 + i * 64: 1600 + i * 64],
                          start=True, stop=True)
            P.PE.inc(mm)

    def v_qs(v):
        v.wait_ge(P.PE.h, pe0 + 4)
        for i in range(4):
            v.tensor_copy(qk_sc[:, i * 64:(i + 1) * 64],
                          P.psA[:, i * 512: i * 512 + 64])
        P.DV.inc(v.drain())
        v.wait_ge(P.GP.h, gp0 + 8 * 16)
        v.tensor_copy(qk_scb[:], qk_sc[:])
        v.drain()
        for i in range(4):
            tab = P.rtab_b[:, i * NTT * 64:(i + 1) * NTT * 64].rearrange(
                "p (t d) -> p t d", d=64)
            scb = qk_scb[:, i * 64:(i + 1) * 64].unsqueeze(1).broadcast_to(
                [128, NTT, 64])
            m = v.tensor_tensor(tab, tab, scb, ALU.mult)
        P.DV.inc(m)

    P.phase("f0_qscales", gpsimd=gp_qs, tensor=t_qs, vector=v_qs)

    for h0 in (0, 8):
        carver.push()
        qf = carver.take(4 * 2048, BF16)
        kf = carver.take(4 * 2048, BF16)
        vf = carver.take(NTT * 512, BF16)
        carver.push()
        wsl = carver.take(4096, BF16)
        qb16 = carver.take(2 * 512, BF16)
        emit_f0_qkv_half(P, h, li, h0, xfm, qk_sc, qf, kf, vf, wsl, qb16)
        carver.pop()
        emit_f0_attn_half(P, h, li, h0, qf, kf, vf, carver)
        carver.pop()
    emit_f0_ffn(P, h, li, xfm, carver)
    carver.pop()


def emit_l0_ffn(P, carver, h):
    xsave = P.nc.dram_tensor("f0_xsave", [S, D], F32).ap()

    def save(s):
        dst = xsave.rearrange("(tt p) c -> p tt c", p=128)
        srcv = P.xres[:].rearrange("p (tt c) -> p tt c", c=D)
        _dma(s, P.ST, dst, srcv)
        s.wait_ge(P.ST.h, P.ST.n)

    P.phase("f0_xsave", sync=save)
    for li in range(L0F_DEPTH):
        emit_l0_ffn_layer(P, carver, h, li)

    def addback2(g):
        srcv = xsave.rearrange("(tt p) c -> p tt c", p=128)
        dst = P.xres[:].rearrange("p (tt c) -> p tt c", c=D)
        P.GP.inc(g.dma_start(dst, srcv, accum_op=ALU.add), 16)
        g.wait_ge(P.GP.h, P.GP.n)

    P.phase("f0_addback", gpsimd=addback2)
    emit_postln(P)





def emit_l1_attn(P, carver, h):
    carver.push()
    xfm = carver.take(8 * XFM_STRIDE, BF16)
    kfm = carver.take(8 * 2048, BF16)
    gh = carver.take(2048, BF16)

    def src_plain(v, tt, slot):
        return v.tensor_copy(slot, P.xres[:, tt * D:(tt + 1) * D])

    build_xfm(P, xfm, src_plain)

    gp0, pe0, ac0 = P.GP.n, P.PE.n, P.AC.n

    wt0 = [P.WT[0].n, P.WT[1].n]

    def gp_loadw(g):
        for m in range(8):
            src = h["a1_WKT"][:, m * 128:(m + 1) * 128].rearrange(
                "(cc p) o -> p cc o", p=128)
            dst = P.wbuf[:, (m % 2) * 3072:(m % 2) * 3072 + 1024].rearrange(
                "p (cc o) -> p cc o", o=128)
            if m >= 2:
                g.wait_ge(P.PE.h, pe0 + 4 * (m - 1))
            _dma(g, P.WT[m % 2], dst, src)

    def tensor(t):
        for m in range(8):
            t.wait_ge(P.WT[m % 2].h, wt0[m % 2] + 16 * (m // 2 + 1))
            wsl = P.wbuf[:, (m % 2) * 3072:(m % 2) * 3072 + 1024]
            for tg in range(4):
                k = m * 4 + tg
                bank = P.psA[:, (k % 2) * 512:(k % 2) * 512 + 512]
                if k >= 2:
                    t.wait_ge(P.AC.h, ac0 + k - 1)
                for cc in range(8):
                    mm = t.matmul(bank, wsl[:, cc * 128:(cc + 1) * 128],
                                  xfm[:, cc * XFM_STRIDE + 1 + tg * 512:
                                      cc * XFM_STRIDE + 1 + (tg + 1) * 512],
                                  start=(cc == 0), stop=(cc == 7))
                P.PE.inc(mm)

    def scalar(a):
        for m in range(8):
            for tg in range(4):
                k = m * 4 + tg
                a.wait_ge(P.PE.h, pe0 + k + 1)
                bank = P.psA[:, (k % 2) * 512:(k % 2) * 512 + 512]
                P.AC.inc(a.copy(
                    kfm[:, m * 2048 + tg * 512: m * 2048 + (tg + 1) * 512], bank))

    P.phase("l1_kfm", gpsimd=gp_loadw, tensor=tensor, scalar=scalar)

    gp1, pe1, ac1 = P.GP.n, P.PE.n, P.AC.n

    def gp2(g):
        src = h["a1_W1T"].rearrange("(cc p) d -> p cc d", p=128)
        dst = P.wbuf[:, 0:512].rearrange("p (cc d) -> p cc d", d=64)
        _dma(g, P.GP, dst, src)

    def tensor2(t):
        t.wait_ge(P.GP.h, gp1 + 16)
        for tg in range(4):
            bank = P.psA[:, 1024 + (tg % 2) * 512: 1024 + (tg % 2) * 512 + 512]
            if tg >= 2:
                t.wait_ge(P.AC.h, ac1 + tg - 1)
            for cc in range(8):
                mm = t.matmul(bank[0:64, :], P.wbuf[:, cc * 64:(cc + 1) * 64],
                              kfm[:, cc * 2048 + tg * 512:
                                  cc * 2048 + (tg + 1) * 512],
                              start=(cc == 0), stop=(cc == 7))
            P.PE.inc(mm)

    def scalar2(a):
        for tg in range(4):
            a.wait_ge(P.PE.h, pe1 + tg + 1)
            bank = P.psA[:, 1024 + (tg % 2) * 512: 1024 + (tg % 2) * 512 + 512]
            P.AC.inc(a.activation(gh[0:64, tg * 512:(tg + 1) * 512],
                                  bank[0:64, :], AF.Gelu))

    P.phase("l1_hidden", gpsimd=gp2, tensor=tensor2, scalar=scalar2)

    gp2_, pe2, dv2 = P.GP.n, P.PE.n, P.DV.n

    def gp3(g):
        _dma(g, P.GP, P.wbuf[0:64, 0:1024], h["a1_W2T"])

    def tensor3(t):
        t.wait_ge(P.GP.h, gp2_ + 16)
        for tt in range(NTT):
            for oc in range(2):
                k = tt * 2 + oc
                bank = P.psA[:, (k % 4) * 512:(k % 4) * 512 + 512]
                if k >= 4:
                    t.wait_ge(P.DV.h, dv2 + k - 3)
                mm = t.matmul(bank, gh[0:64, tt * 128:(tt + 1) * 128],
                              P.wbuf[0:64, oc * 512:(oc + 1) * 512],
                              start=True, stop=True)
                P.PE.inc(mm)

    def vector3(v):
        for tt in range(NTT):
            for oc in range(2):
                k = tt * 2 + oc
                v.wait_ge(P.PE.h, pe2 + k + 1)
                bank = P.psA[:, (k % 4) * 512:(k % 4) * 512 + 512]
                xs = P.xres[:, tt * D + oc * 512: tt * D + (oc + 1) * 512]
                P.DV.inc(v.tensor_add(xs, xs, bank))

    P.phase("l1_reads", gpsimd=gp3, tensor=tensor3, vector=vector3)

    carver.pop()
    emit_postln(P)




def emit_l1_ffn(P, carver, h):
    carver.push()
    xfm = carver.take(8 * XFM_STRIDE, BF16)
    hw_tm = carver.take(NTT * 128, BF16)
    hwfm = carver.take(2048, BF16)
    gsc = carver.take(32, F32)

    def src_plain(v, tt, slot):
        return v.tensor_copy(slot, P.xres[:, tt * D:(tt + 1) * D])

    build_xfm(P, xfm, src_plain)

    gp0, pe0, dv0, ac0 = P.GP.n, P.PE.n, P.DV.n, P.AC.n

    def gp(g):
        _dma(g, P.GP, P.wbuf[:, 0:64].rearrange("p (cc e) -> p cc e", e=8),
             h["f1_Wg"].rearrange("(cc p) e -> p cc e", p=128))
        _dma(g, P.GP, P.wbuf[:, 1024:2048].rearrange("p (cc o) -> p cc o", o=128),
             h["f1_A"].rearrange("(cc p) o -> p cc o", p=128))

    def tensor(t):
        t.wait_ge(P.GP.h, gp0 + 32)
        for tt in range(NTT):
            lbank = P.psA[:, (tt % 2) * 512:(tt % 2) * 512 + 8]
            hbank = P.psA[:, 1024 + (tt % 2) * 512: 1024 + (tt % 2) * 512 + 128]
            if tt >= 2:
                t.wait_ge(P.DV.h, dv0 + 2 * (tt - 2) + 2)
            for cc in range(8):
                xsl = xfm[:, cc * XFM_STRIDE + 1 + tt * 128:
                          cc * XFM_STRIDE + 1 + (tt + 1) * 128]
                t.matmul(lbank, xsl, P.wbuf[:, cc * 8:(cc + 1) * 8],
                         start=(cc == 0), stop=(cc == 7))
            for cc in range(8):
                xsl = xfm[:, cc * XFM_STRIDE + 1 + tt * 128:
                          cc * XFM_STRIDE + 1 + (tt + 1) * 128]
                mm = t.matmul(hbank, xsl,
                              P.wbuf[:, 1024 + cc * 128: 1024 + (cc + 1) * 128],
                              start=(cc == 0), stop=(cc == 7))
            P.PE.inc(mm)
        for tt in range(NTT):
            t.wait_ge(P.DV.h, dv0 + 2 * tt + 2)
            t.wait_ge(P.AC.h, ac0 + NTT + tt)
            m = t.transpose(P.psT[:, tt * 128:(tt + 1) * 128],
                            hw_tm[:, tt * 128:(tt + 1) * 128], P.identb[:])
            P.PE.inc(m)

    def vector(v):
        L = gsc[:, 0:8]
        m1 = gsc[:, 8:9]
        m2 = gsc[:, 9:10]
        wk = gsc[:, 10:18]
        E = gsc[:, 18:26]
        z = gsc[:, 26:27]
        nm1 = gsc[:, 27:28]
        cwt = gsc[:, 28:29]
        for tt in range(NTT):
            v.wait_ge(P.PE.h, pe0 + tt + 1)
            lbank = P.psA[:, (tt % 2) * 512:(tt % 2) * 512 + 8]
            hbank = P.psA[:, 1024 + (tt % 2) * 512: 1024 + (tt % 2) * 512 + 128]
            v.tensor_copy(L, lbank)
            v.drain()
            v.reduce_max(m1, L, AX.X)
            v.drain()
            v.tensor_single_scalar(wk, L, m1, ALU.is_ge)
            v.drain()
            v.scalar_tensor_tensor(wk, wk, NEG, L, ALU.mult, ALU.add)
            v.drain()
            v.reduce_max(m2, wk, AX.X)
            v.drain()
            v.tensor_scalar_mul(nm1, m1, -1.0)
            P.DV.inc(v.drain())
            v.wait_ge(P.AC.h, ac0 + tt + 1)
            v.tensor_single_scalar(wk, L, m2, ALU.is_ge)
            v.drain()
            v.tensor_mul(E, E, wk)
            v.drain()
            v.reduce_sum(z, E, AX.X)
            v.drain()
            v.reciprocal(z, z)
            v.drain()
            v.tensor_scalar_mul(E, E, z)
            v.drain()
            cwb = E.unsqueeze(2).broadcast_to([128, 8, 16])
            m = v.tensor_mul(
                hw_tm[:, tt * 128:(tt + 1) * 128].rearrange("p (e r) -> p e r", e=8),
                hbank.rearrange("p (e r) -> p e r", e=8), cwb)
            P.DV.inc(m)

    def scalar(a):
        L = gsc[:, 0:8]
        E = gsc[:, 18:26]
        nm1 = gsc[:, 27:28]
        for tt in range(NTT):
            a.wait_ge(P.DV.h, dv0 + 2 * tt + 1)
            a.activation(E, L, AF.Exp, bias=nm1, scale=1.0)
            P.AC.inc(a.drain())
        for tt in range(NTT):
            a.wait_ge(P.PE.h, pe0 + NTT + tt + 1)
            P.AC.inc(a.copy(hwfm[:, tt * 128:(tt + 1) * 128],
                            P.psT[:, tt * 128:(tt + 1) * 128]))

    P.phase("l1_gate", gpsimd=gp, tensor=tensor, vector=vector, scalar=scalar)

    gp1, pe1, dv1 = P.GP.n, P.PE.n, P.DV.n

    def gp2(g):
        _dma(g, P.GP, P.wbuf[:, 0:1024], h["f1_B"])

    def tensor2(t):
        t.wait_ge(P.GP.h, gp1 + 16)
        for tt in range(NTT):
            for oc in range(2):
                k = tt * 2 + oc
                bank = P.psA[:, (k % 4) * 512:(k % 4) * 512 + 512]
                if k >= 4:
                    t.wait_ge(P.DV.h, dv1 + k - 3)
                mm = t.matmul(bank, hwfm[:, tt * 128:(tt + 1) * 128],
                              P.wbuf[:, oc * 512:(oc + 1) * 512],
                              start=True, stop=True)
                P.PE.inc(mm)

    def vector2(v):
        for tt in range(NTT):
            for oc in range(2):
                k = tt * 2 + oc
                v.wait_ge(P.PE.h, pe1 + k + 1)
                bank = P.psA[:, (k % 4) * 512:(k % 4) * 512 + 512]
                xs = P.xres[:, tt * D + oc * 512: tt * D + (oc + 1) * 512]
                P.DV.inc(v.tensor_add(xs, xs, bank))

    P.phase("l1_out", gpsimd=gp2, tensor=tensor2, vector=vector2)

    carver.pop()
    emit_postln(P)


BLOCK_EMITTERS = {}


def _declare_hbm(nc, h, keys):
    hbm = {}
    for k in keys:
        a = h[k]
        hbm[k] = nc.dram_tensor(k, list(a.shape), F32, kind="ExternalInput").ap()
    return hbm


L1_KEYS = ["a1_WKT", "a1_W1T", "a1_W2T", "f1_Wg", "f1_A", "f1_B"]
L0F_KEYS = ["f0_Wqkv", "f0_Wo", "f0_W1", "f0_W2", "f0_qs", "f0_qs_perm",
            "f0_ks", "f0_ks_perm", "f0_ln_g", "f0_ln_b", "f0_b1", "f0_b2"]
A0_KEYS = ["a0_swT", "a0_scalerT", "a0_bwT", "a0_grid", "a0_RT2", "a0_owT",
           "a0_out_b"]
CONST_KEYS = ["ident", "ident64_2", "mask_even", "mask_odd", "mask_tri",
              "rot_cs_q", "rot_ss_q", "rot_cs_k", "rot_ss_k"]


def build_program(h, blocks):
    nc = bass.Bass("TRN2", target_bir_lowering=False, debug=False)
    keys = set(CONST_KEYS)
    if "l1_attn" in blocks:
        keys.update(["a1_WKT", "a1_W1T", "a1_W2T"])
    if "l1_ffn" in blocks:
        keys.update(["f1_Wg", "f1_A", "f1_B"])
    if any(b.startswith("l0_ffn") for b in blocks):
        keys.update(L0F_KEYS)
    if "l0_attn" in blocks:
        keys.update(A0_KEYS)
    hbm = _declare_hbm(nc, h, sorted(keys))
    hbm["x"] = nc.dram_tensor("x", [S, D], F32, kind="ExternalInput").ap()
    hbm["out"] = nc.dram_tensor("out", [S, D], F32, kind="ExternalOutput").ap()
    with contextlib.ExitStack() as stack:
        P = Prog(nc, stack)
        if any(b.startswith("l0_ffn") for b in blocks):
            P.shift_scr = nc.dram_tensor("shift_scr", [S + 1, 512], BF16).ap()
        carver = Carver(P.pool_t[:], P.pool_bytes)
        emit_load_consts(P, hbm)
        emit_load_x(P, hbm["x"])
        need_rtab = any(b.startswith("l0_ffn") for b in blocks)
        if need_rtab:
            P.rtab_b = carver.take(4 * NTT * 64, BF16)
            emit_load_rtab(P, hbm)
        for b in blocks:
            if b == "l0_attn":
                emit_l0_attn(P, carver, hbm)
            elif b == "l1_attn":
                emit_l1_attn(P, carver, hbm)
            elif b == "l1_ffn":
                emit_l1_ffn(P, carver, hbm)
            elif b == "l0_ffn":
                emit_l0_ffn(P, carver, hbm)
            elif b.startswith("l0_ffn_layer"):
                emit_l0_ffn_layer(P, carver, hbm, int(b.rsplit("_", 1)[1]))
            else:
                raise ValueError(b)
        emit_store_out(P, hbm["out"])
    return nc, sorted(keys)


def run_blocks(h, x_full, blocks):
    nc, keys = build_program(h, blocks)
    in_maps = []
    for c in range(8):
        m = {k: h[k] for k in keys}
        m["x"] = np.ascontiguousarray(x_full[c])
        in_maps.append(m)
    res = run_bass_kernel_spmd(nc, in_maps, core_ids=list(range(8)))
    return np.stack([r["out"] for r in res.results], axis=0)


def kernel(**inputs):
    h = _host_prep(inputs)
    out = run_blocks(h, h["x"], ["l0_attn", "l0_ffn", "l1_attn", "l1_ffn"])
    return np.ascontiguousarray(out.astype(np.float32))
